# version 35
# baseline (speedup 1.0000x reference)
import contextlib
import numpy as np
import concourse.bass as bass
import concourse.mybir as mybir

F32 = mybir.dt.float32
BF16 = mybir.dt.bfloat16
I32 = mybir.dt.int32
AF = mybir.ActivationFunctionType
ALU = mybir.AluOpType
AX = mybir.AxisListType

D = 1024
H = 4096
NH = 16
DC = D // 128
HC = H // 128
EPS1 = 1e-5
EPS2 = 1e-6
ROFF = 256.0


class Builder:
    def __init__(self, nc, es):
        self.nc = nc
        self.ops = {"sync": [], "tensor": [], "vector": [], "scalar": [], "gpsimd": []}
        self.sems = {}
        self.cnt = {}
        for name in ("dS", "dG", "dGa", "dGb", "dRa", "dRb", "dRc", "dRd", "dXa", "dXb", "dA", "dAb", "sT", "sV", "sS", "sG"):
            self.sems[name] = es.enter_context(nc.semaphore(name))
            self.cnt[name] = 0

    def emit(self, eng, make, inc=None, inc_by=1):
        if inc is not None:
            self.cnt[inc] += inc_by
        sem = self.sems[inc] if inc is not None else None

        def run(e, make=make, sem=sem, inc_by=inc_by):
            ins = make(e)
            if sem is not None:
                ins.then_inc(sem, inc_by)
        self.ops[eng].append(run)
        return self.cnt[inc] if inc is not None else None

    def mark(self, eng, inc):
        return self.emit(eng, lambda e: e.drain(), inc=inc)

    def wait(self, eng, sem, val):
        assert val is not None and val >= 0
        s = self.sems[sem]
        self.ops[eng].append(lambda e, s=s, val=val: e.wait_ge(s, val))

    def dma(self, eng, out, in_, slow=False, sem=None):
        if sem is None:
            sem = {"sync": "dS", "gpsimd": "dG", "scalar": "dA"}[eng]
        return self.emit(eng, lambda e, o=out, i=in_, s=slow: e.dma_start(
            out=o, in_=i, allow_slow_non_contiguous=s), inc=sem, inc_by=16)

    def V(self, make, inc=None):
        return self.emit("vector", make, inc=inc)

    def Vd(self, make, inc=None):
        r = self.emit("vector", make, inc=inc)
        self.emit("vector", lambda e: e.drain())
        return r


VW0 = 130

def build_core(S=4096, phases=4, dbg=False):
    T = S // 128
    KC = S // 128
    QBS = min(512, S)
    QB = S // QBS
    TAS = min(256, S)
    TA = S // TAS
    NSA = TAS // 128
    TBS = min(512, S)
    TB = S // TBS
    NSUB = TBS // 128
    QBS2 = min(256, S)
    TB2 = S // QBS2
    NS2 = QBS2 // 128

    es = contextlib.ExitStack()
    nc = bass.Bass("TRN2", target_bir_lowering=False)
    b = Builder(nc, es)

    def din(name, shape, dt=F32):
        return nc.dram_tensor(name, shape, dt, kind="ExternalInput").ap()
    x_h = din("x", [S, D])
    Wq_h = din("Wq", [D, D]); Wk_h = din("Wk", [D, D]); Wv_h = din("Wv", [D, D]); Wo_h = din("Wo", [D, D])
    bq_h = din("bq", [1, D]); bk_h = din("bk", [1, D]); bv_h = din("bv", [1, D]); bo_h = din("bo", [1, D])
    gam_h = din("gam", [1, D]); bet_h = din("bet", [1, D])
    mu_h = din("mu", [1, 1])
    w1_h = din("w1", [H, D]); w2_h = din("w2", [D, H])
    cw_h = din("cw", [H, 3]); cb_h = din("cb", [1, H])
    id_h = din("ident_in", [128, 128])
    out_h = nc.dram_tensor("out", [S, D], F32, kind="ExternalOutput").ap()

    KND = "ExternalOutput" if dbg else "Internal"
    q_scr = nc.dram_tensor("q_scr", [DC, 128, S], F32, kind=KND).ap()
    k_scr = nc.dram_tensor("k_scr", [DC, 128, S], F32, kind=KND).ap()
    v_scr = nc.dram_tensor("v_scr", [DC, T, 128, VW0], F32, kind=KND).ap()
    ov_scr = nc.dram_tensor("ov_scr", [T, 128, D], F32, kind=KND).ap()
    xq_scr = nc.dram_tensor("xq_scr", [DC, 128, S], BF16, kind=KND).ap()
    r_scr = nc.dram_tensor("r_scr", [HC, 128, S], F32, kind=KND).ap()
    flat_scr = nc.dram_tensor("flat_scr", [128, 128], F32, kind="Internal").ap()

    A = nc.alloc_sbuf_tensor
    ident = A("ident", [128, 128], F32).ap()
    ident16 = A("ident16", [128, 128], BF16).ap()
    ones_row = A("ones_row", [1, 128], F32).ap()
    mu_sb = A("mu_sb", [1, 1], F32).ap()
    mubias = A("mubias", [128, 1], F32).ap()
    gam_row = A("gam_row", [1, D], F32).ap()
    bet_col = A("bet_col", [128, DC], F32).ap()
    bv_rep = A("bv_rep", [128, D], F32).ap()
    bo_rep = A("bo_rep", [128, D], F32).ap()
    bqk_eff = A("bqk_eff", [128, 2 * DC], F32).ap()
    bv_col = A("bv_col", [128, DC], F32).ap()
    row_D = A("row_D", [1, D], F32).ap()
    row_D2 = A("row_D2", [1, D], F32).ap()
    d1_all = A("d1_all", [128, T], F32).ap()
    s2sc = A("s2sc", [128, T], F32).ap()
    d2_all = A("d2_all", [128, T], F32).ap()
    convw = A("convw", [128, 3 * HC], F32).ap()
    convb = A("convb", [128, HC], F32).ap()
    stat = A("stat", [128, 16], F32).ap()
    nstat = A("nstat", [1, 16], F32).ap()
    trsp = A("trsp", [128, 128], F32).ap()

    POOL_B = 179 * 1024
    pool = A("pool", [128, POOL_B], mybir.dt.uint8)
    pool_addr = nc.lookup_mloc(pool).addr

    def carve(off, name, shape, dt):
        handle = nc.alloc_sbuf_tensor_at(name, shape, dt, offset=pool_addr + int(off))
        return handle.ap()

    WqTh = carve(0, "WqTh", [128, DC * D], BF16)
    WqTl = carve(16384, "WqTl", [128, DC * D], BF16)
    WkTh = carve(32768, "WkTh", [128, DC * D], BF16)
    WkTl = carve(49152, "WkTl", [128, DC * D], BF16)
    WvTh = carve(65536, "WvTh", [128, DC * D], BF16)
    WvTl = carve(81920, "WvTl", [128, DC * D], BF16)
    wraw = carve(98304, "wraw", [128, D], F32)
    fold = carve(102400, "fold", [128, D], F32)
    x_t = carve(106496, "x_t", [128, D], F32)
    xn_t = carve(110592, "xn_t", [128, D], F32)
    xh_t = carve(114688, "xh_t", [128, D], F32)
    y_t = carve(118784, "y_t", [128, D], F32)
    i32_t = carve(122880, "i32_t", [128, D], I32)
    f_t = carve(126976, "f_t", [128, D], F32)
    xq_t = carve(131072, "xq_t", [128, D], BF16)
    xh_fmh = carve(133120, "xh_fmh", [128, D], BF16)
    xh_fml = carve(135168, "xh_fml", [128, D], BF16)
    xq_fm = carve(137216, "xq_fm", [128, D], BF16)
    st_q = carve(139264, "st_q", [128, D], F32)
    st_k = carve(143360, "st_k", [128, D], F32)
    st_v = carve(147456, "st_v", [128, DC * VW0], F32)
    rawT = carve(151680, "rawT", [128, D], F32)
    gam_rep = carve(155776, "gam_rep", [128, D], F32)
    x_tB = carve(159872, "x_tB", [128, D], F32)
    xh_tB = carve(163968, "xh_tB", [128, D], F32)
    xq_tB = carve(168064, "xq_tB", [128, D], BF16)
    xh_fmhB = carve(170112, "xh_fmhB", [128, D], BF16)
    xh_fmlB = carve(172160, "xh_fmlB", [128, D], BF16)
    q_tm = carve(174208, "q_tm", [128, D], F32)
    k_tm = carve(178304, "k_tm", [128, D], F32)
    assert 178304 + 4096 <= POOL_B

    k_pair = [carve(0, "k_pair0", [128, S], F32),
              carve(4 * S, "k_pair1", [128, S], F32)]
    q_pair = [carve(8 * S, "q_pair0", [128, S], F32),
              carve(12 * S, "q_pair1", [128, S], F32)]
    VW = 130
    v_pairb = [carve(16 * S, "v_pair0", [128, KC * VW], F32),
               carve(16 * S + 4 * KC * VW, "v_pair1", [128, KC * VW], F32)]
    p2o = 16 * S + 8 * KC * VW
    PLAG = 6
    t_buf = [carve(p2o + 4096 * i, "t_buf%d" % i, [128, 1024], F32) for i in range(PLAG)]
    e_buf = [carve(p2o + 4096 * PLAG + 4096 * i, "e_buf%d" % i, [128, 1024], F32) for i in range(PLAG)]
    p_buf = [carve(p2o + 8192 * PLAG + 4096 * i, "p_buf%d" % i, [128, 1024], F32) for i in range(PLAG)]
    ov_st = carve(p2o + 12288 * PLAG, "ov_st", [128, 512], F32)
    rc_st = carve(p2o + 12288 * PLAG + 2048, "rc_st", [128, 8], F32)
    ovt_sb = carve(p2o + 12288 * PLAG + 2176, "ovt_sb", [128, 1024], F32)
    assert p2o + 12288 * PLAG + 6272 <= POOL_B

    w1qT = carve(0, "w1qT", [128, DC * H], BF16)
    HB = TAS + 2
    h_blk = carve(65536, "h_blk", [128, HC * HB], F32)
    p3a0 = 65536 + 4 * HC * HB
    d1_rep = carve(p3a0, "d1_rep", [128, S], F32)
    p3a1 = p3a0 + 4 * S
    w_st = carve(p3a1, "w_st", [128, D], F32)
    wy = carve(p3a1 + 4096, "wy", [128, D], F32)
    wi32 = carve(p3a1 + 8192, "wi32", [128, D], I32)
    wf = carve(p3a1 + 12288, "wf", [128, D], F32)
    wq_st = carve(p3a1 + 16384, "wq_st", [128, D], BF16)
    wsum = carve(p3a1 + 18432, "wsum", [128, HC], F32)
    xq_blk = carve(p3a1 + 18560, "xq_blk", [128, DC * HB], BF16)
    p3a2 = p3a1 + 18560 + 2 * DC * HB
    cbufs = [carve(p3a2, "cbuf0", [128, TAS], F32),
             carve(p3a2 + 4 * TAS, "cbuf1", [128, TAS], F32)]
    r_sb = [carve(p3a2 + 8 * TAS, "r_sb0", [128, TAS], F32),
            carve(p3a2 + 12 * TAS, "r_sb1", [128, TAS], F32)]
    redb = carve(p3a2 + 16 * TAS, "redb", [128, NSA * HC], F32)
    p3a3 = p3a2 + 16 * TAS + 4 * NSA * HC
    row_Sa = carve(p3a3, "row_Sa", [1, S], F32)
    w_stB = carve(p3a3 + 4 * S, "w_stB", [128, D], F32)
    p3a4 = p3a3 + 4 * S + 4096
    xq_blkB = carve(p3a4, "xq_blkB", [128, DC * HB], BF16)
    cbuf2 = carve(p3a4 + 2 * DC * HB, "cbuf2", [128, TAS], F32)
    wq_stB = carve(p3a4, "wq_stB", [128, D], BF16)
    assert p3a4 + 2 * DC * HB + 4 * TAS <= POOL_B

    w2qT = carve(0, "w2qT", [128, HC * D], BF16)
    w2st = carve(65536, "w2st", [128, H], F32)
    WoT = carve(65536, "WoT2", [128, DC * D], F32)
    p3b0 = 65536 + 4 * DC * D
    xq2P_ = [carve(p3b0, "xq2_blk0", [128, HC * QBS2], BF16),
             carve(p3b0 + 2 * HC * QBS2, "xq2_blk1", [128, HC * QBS2], BF16)]
    p3b1 = p3b0 + 4 * HC * QBS2
    r_ld = [carve(p3b1, "r_ld0", [128, QBS2], F32),
            carve(p3b1 + 1024, "r_ld1", [128, QBS2], F32),
            carve(p3b1 + 34816 + 12288, "r_ld2", [128, QBS2], F32),
            carve(p3b1 + 34816 + 13312, "r_ld3", [128, QBS2], F32)]
    roffb = carve(p3b1 + 34816 + 14336, "roffb", [128, QBS2], F32)
    halfb = carve(p3b1 + 34816 + 15360, "halfb", [128, QBS2], F32)
    s2_repP = [carve(p3b1 + 2048, "s2_rep0", [128, 2 * QBS2], F32),
               carve(p3b1 + 4096, "s2_rep1", [128, 2 * QBS2], F32)]
    ov_t = carve(p3b1 + 6144, "ov_t", [128, D], F32)
    ovfm = carve(p3b1 + 10240, "ovfm", [128, D], F32)
    x2_t = carve(p3b1 + 14336, "x2_t", [128, D], F32)
    ov_tB = carve(p3b1 + 34816, "ov_tB", [128, D], F32)
    x2_tB = carve(p3b1 + 34816 + 4096, "x2_tB", [128, D], F32)
    ffn_t = carve(p3b1 + 18432, "ffn_t", [128, D], F32)
    acc_t = carve(p3b1 + 22528, "acc_t", [128, D], F32)
    w2stB = carve(p3b1 + 34816, "w2stB", [128, H], F32)
    assert p3b1 + 34816 + 4 * H <= POOL_B
    hy = carve(p3b1 + 26624, "hy", [128, QBS2], F32)
    hi32 = carve(p3b1 + 27648, "hi32", [128, QBS2], I32)
    hf = carve(p3b1 + 28672, "hf", [128, QBS2], F32)
    hyG = carve(p3b1 + 29696, "hyG", [128, QBS2], F32)
    hi32G = carve(p3b1 + 30720, "hi32G", [128, QBS2], I32)
    hfG = carve(p3b1 + 31744, "hfG", [128, QBS2], F32)
    row_Sb = carve(p3b1 + 32768, "row_Sb", [1, 2 * QBS2], F32)
    acc_tB = carve(p3b1 + 34816 + 8192, "acc_tB", [128, D], F32)
    assert p3b1 + 34816 + 4 * H <= POOL_B

    PS_A = nc.alloc_psum_tensor("PS_A", [128, 1024], F32).ap()
    PS_B = nc.alloc_psum_tensor("PS_B", [128, 1024], F32).ap()
    PS_C = nc.alloc_psum_tensor("PS_C", [128, 1024], F32).ap()
    PS_D = nc.alloc_psum_tensor("PS_D", [128, 512], F32).ap()
    PS_E = nc.alloc_psum_tensor("PS_E", [128, 512], F32).ap()
    PS_B16 = PS_B.bitcast(BF16)

    block = es.enter_context(nc.Block())

    def transpose_tile(src, dst, ps, ident_, n=DC, inc_last="sV"):
        for dc in range(n):
            b.emit("tensor", lambda e, dc=dc, src=src, ps=ps, ident_=ident_: e.transpose(
                ps[:, dc * 128:(dc + 1) * 128], src[:, dc * 128:(dc + 1) * 128], ident_))
        t = b.mark("tensor", "sT")
        b.wait("vector", "sT", t)
        vv = None
        for dc in range(n):
            vv = b.emit("vector", lambda e, dc=dc, dst=dst, ps=ps: e.tensor_copy(
                dst[:, dc * 128:(dc + 1) * 128], ps[:, dc * 128:(dc + 1) * 128]),
                inc=(inc_last if dc == n - 1 else None))
        return vv

    b.dma("sync", ident, id_h)
    b.dma("sync", mu_sb, mu_h)
    b.dma("sync", gam_row, gam_h)
    b.dma("sync", bet_col, bet_h.squeeze(0).rearrange("(dc p) -> p dc", p=128), slow=True)
    b.dma("sync", convw.rearrange("p (hc k) -> p hc k", k=3), cw_h.rearrange("(hc p) k -> p hc k", p=128))
    b.dma("sync", convb, cb_h.squeeze(0).rearrange("(hc p) -> p hc", p=128), slow=True)
    b.dma("sync", row_D, bv_h)
    b.dma("sync", row_D2, bo_h)
    b.dma("sync", bqk_eff[:, 0:DC], bq_h.squeeze(0).rearrange("(oc p) -> p oc", p=128), slow=True)
    b.dma("sync", bqk_eff[:, DC:2 * DC], bk_h.squeeze(0).rearrange("(oc p) -> p oc", p=128), slow=True)
    b.dma("sync", bv_col, bv_h.squeeze(0).rearrange("(oc p) -> p oc", p=128), slow=True)
    d0 = b.cnt["dS"]
    b.wait("sync", "dS", d0)
    b.wait("vector", "dS", d0)
    b.V(lambda e: e.tensor_copy(ident16, ident))
    b.V(lambda e: e.memset(ones_row, 1.0))
    b.V(lambda e: e.memset(stat[:, 15:16], 1.0))
    v0 = b.mark("vector", "sV")
    b.wait("tensor", "sV", v0)
    b.wait("tensor", "dS", d0)
    b.emit("tensor", lambda e: e.matmul(PS_D[:, 0:1], ones_row, mu_sb, start=True, stop=True))

    def rep_slab(dst, src_row, ps):
        last = None
        for hh in range(D // 512):
            last = b.emit("tensor", lambda e, hh=hh, ps=ps, src_row=src_row: e.matmul(
                ps[:, hh * 512:(hh + 1) * 512], ones_row, src_row[:, hh * 512:(hh + 1) * 512],
                start=True, stop=True),
                inc=("sT" if hh == D // 512 - 1 else None))
        b.wait("vector", "sT", last)
        return b.Vd(lambda e, dst=dst, ps=ps: e.tensor_copy(dst, ps[:, 0:D]), inc="sV"), last

    v_gam, _ = rep_slab(gam_rep, gam_row, PS_A)
    b.wait("tensor", "sV", v_gam)
    v_bo, _ = rep_slab(bo_rep, row_D2, PS_A)
    t_mu = b.mark("tensor", "sT")
    b.wait("vector", "sT", t_mu)
    v_mu = b.Vd(lambda e: e.tensor_scalar(mubias, PS_D[:, 0:1], -0.5, None, ALU.mult), inc="sV")

    b.emit("scalar", lambda e: e.preload_act_table(AF.Tanh))

    b.wait("vector", "sV", v_bo)
    WTs = (("q", Wq_h, (WqTh, WqTl), True), ("k", Wk_h, (WkTh, WkTl), True),
           ("v", Wv_h, (WvTh, WvTl), True))
    wrawP = [wraw, carve(159872, "wrawB", [128, D], F32)]
    wsemP = ["dXa", "dXb"]
    wld = {}
    wproc = {}
    for wi, (wname, W_h, WT_sb, need_bias) in enumerate(WTs):
        for oc in range(DC):
            ci = wi * DC + oc
            if ci == 0:
                wld[0] = b.dma("sync", wrawP[0], W_h[0:128, :], sem=wsemP[0])
            if ci + 1 < 3 * DC:
                wi2, oc2 = divmod(ci + 1, DC)
                W2 = WTs[wi2][1]
                if ci >= 1:
                    b.wait("sync", "sV", wproc[ci - 1])
                wld[ci + 1] = b.dma("sync", wrawP[(ci + 1) % 2],
                                    W2[oc2 * 128:(oc2 + 1) * 128, :], sem=wsemP[(ci + 1) % 2])
            wraw_c = wrawP[ci % 2]
            b.wait("vector", wsemP[ci % 2], wld[ci])
            v1 = b.Vd(lambda e, wraw_c=wraw_c: e.tensor_mul(fold, wraw_c, gam_rep), inc="sV")
            b.wait("tensor", "sV", v1)
            for dc in range(DC):
                b.emit("tensor", lambda e, dc=dc: e.transpose(
                    PS_C[:, dc * 128:(dc + 1) * 128], fold[:, dc * 128:(dc + 1) * 128], ident))
            if need_bias:
                for dc in range(DC):
                    b.emit("tensor", lambda e, dc=dc, wraw_c=wraw_c: e.transpose(
                        PS_A[:, dc * 128:(dc + 1) * 128], wraw_c[:, dc * 128:(dc + 1) * 128], ident))
            t1 = b.mark("tensor", "sT")
            WTh, WTl = WT_sb
            b.wait("scalar", "sT", t1)
            for dc in range(DC):
                sl = slice(dc * D + oc * 128, dc * D + oc * 128 + 128)
                b.emit("scalar", lambda e, dc=dc, sl=sl, WTh=WTh: e.copy(
                    WTh[:, sl], PS_C[:, dc * 128:(dc + 1) * 128]))
            if need_bias:
                for dc in range(DC):
                    b.emit("scalar", lambda e, dc=dc: e.copy(
                        rawT[:, dc * 128:(dc + 1) * 128], PS_A[:, dc * 128:(dc + 1) * 128]))
            s_h = b.mark("scalar", "sS")
            b.wait("vector", "sS", s_h)
            for dc in range(DC):
                sl = slice(dc * D + oc * 128, dc * D + oc * 128 + 128)
                b.V(lambda e, dc=dc, sl=sl, WTh=WTh, WTl=WTl: e.tensor_sub(
                    WTl[:, sl], PS_C[:, dc * 128:(dc + 1) * 128], WTh[:, sl]))
            vv = b.mark("vector", "sV")
            wproc[ci] = vv
            b.wait("tensor", "sV", vv)
            if need_bias:
                for dc in range(DC):
                    b.emit("tensor", lambda e, dc=dc, oc=oc, wi=wi: e.matmul(
                        PS_E[:, wi * DC + oc: wi * DC + oc + 1],
                        rawT[:, dc * 128:(dc + 1) * 128],
                        bet_col[:, dc:dc + 1],
                        start=(dc == 0 and oc == 0 and wi == 0),
                        stop=(dc == DC - 1 and oc == DC - 1 and wi == 2)))
    t_bet = b.mark("tensor", "sT")
    b.wait("vector", "sT", t_bet)
    b.Vd(lambda e: e.tensor_add(bqk_eff, bqk_eff, PS_E[:, 0:2 * DC]))
    v_be = b.Vd(lambda e: e.tensor_add(bv_col, bv_col, PS_E[:, 2 * DC:3 * DC]), inc="sV")
    b.wait("tensor", "sV", v_be)
    t_bt = b.emit("tensor", lambda e: e.transpose(PS_D[0:DC, 0:128], bv_col, ident), inc="sT")
    b.wait("vector", "sT", t_bt)
    v_bt = b.Vd(lambda e: e.tensor_copy(trsp[0:DC, 0:128], PS_D[0:DC, 0:128]), inc="sV")
    b.wait("gpsimd", "sV", v_bt)
    b.dma("gpsimd", flat_scr[0:DC, :], trsp[0:DC, 0:128])
    b.wait("gpsimd", "dG", b.cnt["dG"])
    b.dma("gpsimd", row_D[:, 0:D], flat_scr.rearrange("a b -> (a b)").unsqueeze(0)[:, 0:D])
    b.wait("tensor", "dG", b.cnt["dG"])
    b.wait("tensor", "sV", v_bt)
    v_bvr, _ = rep_slab(bv_rep, row_D, PS_A)
    b.wait("tensor", "sV", v_bvr)

    def newton_rsqrt(col_w, col_r):
        b.Vd(lambda e: e.tensor_scalar(stat[:, 11:12], stat[:, col_w:col_w + 1],
                                       0.5, 0.5, ALU.mult, ALU.add))
        for _ in range(4):
            b.Vd(lambda e: e.reciprocal(stat[:, 12:13], stat[:, 11:12]))
            b.Vd(lambda e: e.tensor_mul(stat[:, 12:13], stat[:, 12:13], stat[:, col_w:col_w + 1]))
            b.Vd(lambda e: e.tensor_add(stat[:, 11:12], stat[:, 11:12], stat[:, 12:13]))
            b.Vd(lambda e: e.tensor_scalar(stat[:, 11:12], stat[:, 11:12], 0.5, None, ALU.mult))
        b.Vd(lambda e: e.reciprocal(stat[:, col_r:col_r + 1], stat[:, 11:12]))

    def ln_tile(src, dst, eps, cm, cr):
        b.Vd(lambda e: e.reduce_sum(stat[:, 8:9], src, AX.X))
        b.Vd(lambda e: e.tensor_scalar(stat[:, 9:10], stat[:, 8:9], -1.0 / D, None, ALU.mult))
        v0 = b.mark("vector", "sV")
        b.wait("scalar", "sV", v0)
        s0 = b.emit("scalar", lambda e, src=src: e.activation(
            y_t, src, AF.Square, bias=stat[:, 9:10], scale=1.0), inc="sS")
        b.wait("vector", "sS", s0)
        b.Vd(lambda e: e.reduce_sum(stat[:, 10:11], y_t, AX.X))
        b.Vd(lambda e: e.tensor_scalar(stat[:, 10:11], stat[:, 10:11], 1.0 / D, eps,
                                       ALU.mult, ALU.add))
        newton_rsqrt(10, cr)
        b.Vd(lambda e: e.tensor_mul(stat[:, cm:cm + 1], stat[:, 9:10], stat[:, cr:cr + 1]))
        b.Vd(lambda e, src=src, dst=dst: e.tensor_scalar(
            dst, src, stat[:, cr:cr + 1], stat[:, cm:cm + 1], ALU.mult, ALU.add))

    def quant_tile(src, dst_bf16, d1_col):
        b.Vd(lambda e: e.tensor_reduce(stat[:, 13:14], src, AX.X, ALU.max,
                                       apply_absolute_value=True))
        b.Vd(lambda e: e.tensor_scalar_max(stat[:, 13:14], stat[:, 13:14], 1e-5))
        b.Vd(lambda e: e.reciprocal(stat[:, 14:15], stat[:, 13:14]))
        b.Vd(lambda e: e.tensor_scalar(stat[:, 14:15], stat[:, 14:15], 127.0, None, ALU.mult))
        if d1_col is not None:
            b.Vd(lambda e: e.tensor_scalar(d1_all[:, d1_col:d1_col + 1], stat[:, 13:14],
                                           1.0 / 127.0, None, ALU.mult))
        b.Vd(lambda e, src=src: e.tensor_scalar(y_t, src, stat[:, 14:15], ROFF,
                                                ALU.mult, ALU.add))
        b.Vd(lambda e: e.tensor_copy(i32_t, y_t))
        b.Vd(lambda e: e.tensor_sub(f_t, y_t, i32_t))
        b.Vd(lambda e: e.tensor_single_scalar(f_t, f_t, 0.5, ALU.is_ge))
        b.Vd(lambda e, dst=dst_bf16: e.scalar_tensor_tensor(dst, i32_t, -ROFF, f_t,
                                                            ALU.add, ALU.add))

    b.V(lambda e: e.memset(st_v.rearrange("p (pr g w) -> p pr g w", g=2, w=65)[:, :, :, 64:65], 1.0))
    b.emit("vector", lambda e: e.drain())

    xP = [x_t, x_tB]
    xhP = [xh_t, xh_tB]
    xqP = [xq_t, xq_tB]
    fmhP = [xh_fmh, xh_fmhB]
    fmlP = [xh_fml, xh_fmlB]
    rdy = {}
    backg = {}
    xsem = lambda i: "dXa" if i % 2 == 0 else "dXb"
    b.wait("sync", "sV", b.cnt["sV"])
    dxs = {0: b.dma("sync", xP[0], x_h[0:128, :], sem=xsem(0))}

    def front(ti):
        par = ti % 2
        if ti + 1 < T:
            if ti >= 1:
                b.wait("sync", "sV", rdy[ti - 1])
            dxs[ti + 1] = b.dma("sync", xP[1 - par], x_h[(ti + 1) * 128:(ti + 2) * 128, :],
                                sem=xsem(ti + 1))
        b.wait("vector", xsem(ti), dxs[ti])
        ln_tile(xP[par], xn_t, EPS1, 2, 1)
        ln_tile(xn_t, xhP[par], EPS2, 5, 4)
        quant_tile(xn_t, xqP[par], ti)
        rdy[ti] = b.mark("vector", "sV")

    fmrec = {}
    fm_dc = {}
    backg_v = {}

    def back_tr(ti):
        par = ti % 2
        xh_, xq_ = xhP[par], xqP[par]
        fmh_, fml_ = fmhP[par], fmlP[par]
        b.wait("tensor", "sV", rdy[ti])
        if ti >= 1:
            b.wait("tensor", "sV", backg_v[ti - 1])
        for dc in range(DC):
            b.emit("tensor", lambda e, dc=dc, xh_=xh_: e.transpose(
                PS_C[:, dc * 128:(dc + 1) * 128], xh_[:, dc * 128:(dc + 1) * 128], ident))
        t_xh = b.mark("tensor", "sT")
        if ti >= 1:
            b.wait("tensor", "sV", backg[ti - 1])
        for dc in range(DC):
            b.emit("tensor", lambda e, dc=dc, xq_=xq_: e.transpose(
                PS_D.bitcast(BF16)[:, dc * 128:(dc + 1) * 128],
                xq_[:, dc * 128:(dc + 1) * 128], ident16))
        t_xq = b.mark("tensor", "sT")
        b.wait("vector", "sT", t_xh)
        if ti >= 1:
            b.wait("vector", "dG", backg_dG[ti - 1])
        fm_marks = []
        for dc in range(DC):
            b.V(lambda e, dc=dc, fmh_=fmh_: e.tensor_copy(
                fmh_[:, dc * 128:(dc + 1) * 128], PS_C[:, dc * 128:(dc + 1) * 128]))
            b.emit("vector", lambda e: e.drain())
            b.V(lambda e, dc=dc, fmh_=fmh_, fml_=fml_: e.tensor_sub(
                fml_[:, dc * 128:(dc + 1) * 128], PS_C[:, dc * 128:(dc + 1) * 128],
                fmh_[:, dc * 128:(dc + 1) * 128]))
            fm_marks.append(b.mark("vector", "sV"))
        fm_dc[ti] = fm_marks
        b.wait("vector", "sT", t_xq)
        for dc in range(DC):
            b.V(lambda e, dc=dc: e.tensor_copy(
                xq_fm[:, dc * 128:(dc + 1) * 128],
                PS_D.bitcast(BF16)[:, dc * 128:(dc + 1) * 128]))
        v_fm2 = b.mark("vector", "sV")
        b.wait("gpsimd", "sV", v_fm2)
        b.dma("gpsimd", xq_scr[:, :, ti * 128:(ti + 1) * 128].rearrange("dc p t -> p dc t"),
              xq_fm.rearrange("p (dc t) -> p dc t", dc=DC))
        fmrec[ti] = v_fm2

    def back_mm(ti):
        par = ti % 2
        fmh_, fml_ = fmhP[par], fmlP[par]
        qkv_ps = {"q": PS_A, "k": PS_B, "v": PS_C}
        marks = {}
        for wi, (wname, WTp) in enumerate((("q", (WqTh, WqTl)), ("k", (WkTh, WkTl)),
                                           ("v", (WvTh, WvTl)))):
            ps = qkv_ps[wname]
            WTh, WTl = WTp
            for og in range(2):
                combos = [(fmh_, WTh), (fmh_, WTl), (fml_, WTh)]
                n_tot = len(combos) * DC
                i = 0
                for dc in range(DC):
                    if wi == 0 and og == 0:
                        b.wait("tensor", "sV", fm_dc[ti][dc])
                    for (xa, wb) in combos:
                        b.emit("tensor", lambda e, og=og, dc=dc, ps=ps, xa=xa, wb=wb,
                               first=(i == 0), last=(i == n_tot - 1): e.matmul(
                            ps[:, og * 512:(og + 1) * 512],
                            xa[:, dc * 128:(dc + 1) * 128],
                            wb[:, dc * D + og * 512: dc * D + og * 512 + 512],
                            start=first, stop=last))
                        i += 1
            marks[wname] = b.mark("tensor", "sT")
        b.wait("vector", "sT", marks["q"])
        b.V(lambda e: e.tensor_copy(q_tm, PS_A[:, 0:1024]))
        b.wait("vector", "sT", marks["k"])
        b.V(lambda e: e.tensor_copy(k_tm, PS_B[:, 0:1024]))
        v_tm = b.mark("vector", "sV")
        b.wait("vector", "sT", marks["v"])
        b.V(lambda e: e.tensor_add(
            st_v.rearrange("p (pr g w) -> p pr g w", g=2, w=65)[:, :, :, 0:64],
            PS_C[:, 0:1024].rearrange("p (pr g i) -> p pr g i", g=2, i=64),
            bv_rep.rearrange("p (pr g i) -> p pr g i", g=2, i=64)))
        v_ev_v = b.mark("vector", "sV")
        backg_v[ti] = v_ev_v
        b.wait("gpsimd", "sV", v_ev_v)
        b.dma("gpsimd", v_scr[:, ti, :, :].rearrange("pr p w -> p pr w"),
              st_v.rearrange("p (pr w) -> p pr w", w=VW0))
        b.wait("tensor", "sV", v_tm)
        for dc in range(DC):
            pst = PS_A if dc < 4 else PS_B
            b.emit("tensor", lambda e, dc=dc, pst=pst: e.transpose(
                pst[:, (dc % 4) * 128:(dc % 4 + 1) * 128], q_tm[:, dc * 128:(dc + 1) * 128], ident))
        t_trq = b.mark("tensor", "sT")
        for dc in range(DC):
            pst = PS_D if dc < 4 else PS_E
            b.emit("tensor", lambda e, dc=dc, pst=pst: e.transpose(
                pst[:, (dc % 4) * 128:(dc % 4 + 1) * 128], k_tm[:, dc * 128:(dc + 1) * 128], ident))
        t_trk = b.mark("tensor", "sT")
        b.wait("vector", "sT", t_trq)
        for oc in range(DC):
            pst = PS_A if oc < 4 else PS_B
            b.V(lambda e, oc=oc, pst=pst: e.tensor_scalar(
                st_q[:, oc * 128:(oc + 1) * 128], pst[:, (oc % 4) * 128:(oc % 4 + 1) * 128],
                bqk_eff[:, oc:oc + 1], None, ALU.add))
        v_evq = b.mark("vector", "sV")
        b.wait("gpsimd", "sV", v_evq)
        b.dma("gpsimd", q_scr[:, :, ti * 128:(ti + 1) * 128].rearrange("oc p t -> p oc t"),
              st_q.rearrange("p (oc t) -> p oc t", oc=DC))
        b.wait("vector", "sT", t_trk)
        for oc in range(DC):
            pst = PS_D if oc < 4 else PS_E
            b.V(lambda e, oc=oc, pst=pst: e.tensor_scalar(
                st_k[:, oc * 128:(oc + 1) * 128], pst[:, (oc % 4) * 128:(oc % 4 + 1) * 128],
                bqk_eff[:, DC + oc:DC + oc + 1], None, ALU.add))
        vvk = b.mark("vector", "sV")
        b.wait("gpsimd", "sV", vvk)
        b.dma("gpsimd", k_scr[:, :, ti * 128:(ti + 1) * 128].rearrange("oc p t -> p oc t"),
              st_k.rearrange("p (oc t) -> p oc t", oc=DC))
        backg[ti] = vvk
        backg_dG[ti] = b.cnt["dG"]

    backg_dG = {}
    for ti in range(T):
        if ti >= 1:
            back_tr(ti - 1)
        front(ti)
        if ti >= 1:
            back_mm(ti - 1)
    back_tr(T - 1)
    back_mm(T - 1)

    vB = b.cnt["sV"]; tB = b.cnt["sT"]; sB = b.cnt["sS"]; gB = b.cnt["dG"]; dB = b.cnt["dS"]
    for eng in ("sync", "tensor", "vector", "scalar", "gpsimd"):
        b.wait(eng, "sV", vB); b.wait(eng, "sT", tB)
        if sB: b.wait(eng, "sS", sB)
        b.wait(eng, "dG", gB); b.wait(eng, "dS", dB)
        b.wait(eng, "dXa", b.cnt["dXa"]); b.wait(eng, "dXb", b.cnt["dXb"])


    if phases <= 1:
        b.wait("gpsimd", "dG", b.cnt["dG"])
        b.wait("gpsimd", "dS", b.cnt["dS"])
        b.wait("gpsimd", "dGa", b.cnt["dGa"])
        b.wait("gpsimd", "dGb", b.cnt["dGb"])
        return finish(nc, es, block, b, ops_only=True)

    NP = NH // 2
    psS = [PS_A, PS_B]
    pv_pairs = [[PS_D, PS_E], [PS_C[:, 0:512], PS_C[:, 512:1024]]]
    rec = {}

    def load_pair(p, buf):
        b.dma("sync", k_pair[buf], k_scr[p, :, :])
        b.dma("sync", q_pair[buf], q_scr[p, :, :])
        nq = max(1, KC // 8)
        for qt in range(0, KC, nq):
            b.dma("sync", v_pairb[buf].rearrange("p (kc w) -> p kc w", w=VW)[:, qt:qt + nq, :],
                  v_scr[p, qt:qt + nq, :, :].rearrange("t p w -> p t w"))
        return b.cnt["dS"]

    def emit_pv(p, qb, kc, buf):
        b.wait("tensor", "sV", rec[("p", p, qb, kc)])
        gqb = p * QB + qb
        pair = pv_pairs[gqb % 2]
        if kc == 0:
            prev = rec.get(("evtr", gqb - 2))
            if prev is not None:
                b.wait("tensor", "sV", prev)
        pb_ = p_buf[(qb * KC + kc) % PLAG]
        last = None
        for hh in range(2):
            last = b.emit("tensor", lambda e, hh=hh, pb_=pb_, kc=kc, buf=buf, pair=pair: e.matmul(
                pair[hh][0:65, 0:512],
                v_pairb[buf][:, kc * 130 + hh * 65: kc * 130 + hh * 65 + 65],
                pb_[:, hh * 512:(hh + 1) * 512],
                start=(kc == 0), stop=(kc == KC - 1)),
                inc=("sT" if hh == 1 else None))
        rec[("pv", p, qb, kc)] = last

    pair_loaded = {0: load_pair(0, 0)}

    ovst_dg = [0]

    def emit_tail(p, qb, buf):
        gqb = p * QB + qb
        pair = pv_pairs[gqb % 2]
        b.wait("vector", "sT", rec[("pv", p, qb, KC - 1)])
        for hh in range(2):
            b.V(lambda e, hh=hh, pair=pair: e.tensor_copy(
                ovt_sb[0:65, hh * 512:(hh + 1) * 512], pair[hh][0:65, 0:512]))
        v_ovt = b.mark("vector", "sV")
        rec[("evac", p, qb)] = v_ovt
        b.wait("tensor", "sV", v_ovt)
        for hh in range(2):
            for j in range(4):
                b.emit("tensor", lambda e, hh=hh, j=j, pair=pair: e.transpose(
                    pair[hh][:, j * 65: j * 65 + 65],
                    ovt_sb[0:65, hh * 512 + j * 128: hh * 512 + j * 128 + 128],
                    ident[0:65, 0:65]))
        t_tr = b.mark("tensor", "sT")
        b.wait("vector", "sT", t_tr)
        for hh in range(2):
            for j in range(4):
                b.V(lambda e, hh=hh, j=j, pair=pair: e.reciprocal(
                    rc_st[:, hh * 4 + j: hh * 4 + j + 1],
                    pair[hh][:, j * 65 + 64: j * 65 + 65]))
        b.emit("vector", lambda e: e.drain())
        b.wait("vector", "dG", ovst_dg[0])
        for j in range(4):
            for hh in range(2):
                b.V(lambda e, hh=hh, j=j, pair=pair: e.tensor_scalar(
                    ov_st[:, j * 128 + hh * 64: j * 128 + (hh + 1) * 64],
                    pair[hh][:, j * 65: j * 65 + 64],
                    rc_st[:, hh * 4 + j: hh * 4 + j + 1], None, ALU.mult))
        vv = b.mark("vector", "sV")
        b.wait("gpsimd", "sV", vv)
        ti0 = qb * (QBS // 128)
        b.dma("gpsimd", ov_scr[ti0:ti0 + 4, :, p * 128:(p + 1) * 128].rearrange("j p d -> p j d"),
              ov_st.rearrange("p (j d) -> p j d", j=4))
        ovst_dg[0] = b.cnt["dG"]
        rec[("evtr", gqb)] = b.cnt["sV"]

    lin = lambda p, qb, kc: (p * QB + qb) * KC + kc
    NKC = QB * KC
    for p in range(NP):
        buf = p % 2
        if p + 1 < NP:
            b.wait("sync", "dS", pair_loaded[p])
            pair_loaded[p + 1] = load_pair(p + 1, 1 - buf)
        b.wait("tensor", "dS", pair_loaded[p])
        b.wait("scalar", "dS", pair_loaded[p])
        for gk in range(NKC):
            qb, kc = divmod(gk, KC)
            st = gk % 2
            prev = lin(p, qb, kc) - 2
            if prev >= 0:
                pp, rem = divmod(prev, QB * KC)
                qq, kk = divmod(rem, KC)
                b.wait("tensor", "sS", rec[("acts", pp, qq, kk)])
            kp, qp = k_pair[buf], q_pair[buf]
            b.emit("tensor", lambda e, kp=kp, qp=qp, st=st, kc=kc, qb=qb: e.matmul(
                psS[st][:, 0:512], kp[0:64, kc * 128:(kc + 1) * 128],
                qp[0:64, qb * QBS: qb * QBS + 512],
                start=True, stop=True, tile_position=(0, 0)))
            rec[("scores", p, qb, kc)] = b.emit(
                "tensor", lambda e, kp=kp, qp=qp, st=st, kc=kc, qb=qb: e.matmul(
                    psS[st][:, 512:1024], kp[64:128, kc * 128:(kc + 1) * 128],
                    qp[64:128, qb * QBS: qb * QBS + 512],
                    start=True, stop=True, tile_position=(64, 0)), inc="sT")
            if gk >= PLAG:
                qb3, kc3 = divmod(gk - PLAG, KC)
                emit_pv(p, qb3, kc3, buf)
                if kc3 == KC - 1:
                    emit_tail(p, qb3, buf)
            b.wait("scalar", "sT", rec[("scores", p, qb, kc)])
            if gk >= PLAG:
                qb3, kc3 = divmod(gk - PLAG, KC)
                b.wait("scalar", "sV", rec[("p", p, qb3, kc3)])
            tb_, eb_ = t_buf[gk % PLAG], e_buf[gk % PLAG]
            b.emit("scalar", lambda e, st=st, tb_=tb_: e.activation(
                tb_, psS[st][:, 0:1024], AF.Tanh, bias=mubias, scale=1.0 / 16))
            rec[("acts", p, qb, kc)] = b.emit(
                "scalar", lambda e, st=st, eb_=eb_: e.activation(
                    eb_, psS[st][:, 0:1024], AF.Exp, bias=0.0, scale=0.125), inc="sS")
            b.wait("vector", "sS", rec[("acts", p, qb, kc)])
            if gk >= PLAG:
                qb3, kc3 = divmod(gk - PLAG, KC)
                b.wait("vector", "sT", rec[("pv", p, qb3, kc3)])
            pb_ = p_buf[gk % PLAG]
            rec[("p", p, qb, kc)] = b.emit(
                "vector", lambda e, tb_=tb_, eb_=eb_, pb_=pb_: e.scalar_tensor_tensor(
                    pb_, tb_, 1.0, eb_, ALU.add, ALU.mult), inc="sV")
        for gk2 in range(max(0, NKC - PLAG), NKC):
            qb3, kc3 = divmod(gk2, KC)
            emit_pv(p, qb3, kc3, buf)
            if kc3 == KC - 1:
                emit_tail(p, qb3, buf)
        b.wait("sync", "sT", rec[("scores", p, QB - 1, KC - 1)])
        b.wait("sync", "sV", rec[("evtr", p * QB + QB - 1)])

    vB = b.cnt["sV"]; tB = b.cnt["sT"]; sB = b.cnt["sS"]; gB = b.cnt["dG"]; dB = b.cnt["dS"]
    for eng in ("sync", "tensor", "vector", "scalar", "gpsimd"):
        b.wait(eng, "sV", vB); b.wait(eng, "sT", tB); b.wait(eng, "sS", sB)
        b.wait(eng, "dG", gB); b.wait(eng, "dS", dB)


    if phases <= 2:
        b.wait("gpsimd", "dG", b.cnt["dG"])
        b.wait("gpsimd", "dS", b.cnt["dS"])
        b.wait("gpsimd", "dGa", b.cnt["dGa"])
        b.wait("gpsimd", "dGb", b.cnt["dGb"])
        return finish(nc, es, block, b, ops_only=True)

    wstP = [w_st, w_stB]
    wsemQ = ["dXa", "dXb"]
    b.wait("sync", "sV", b.cnt["sV"])
    mld = {0: b.dma("sync", wstP[0], w1_h[0:128, :], sem=wsemQ[0])}
    mproc = {}
    for hc in range(HC):
        if hc + 1 < HC:
            if hc >= 1:
                b.wait("sync", "sV", mproc[hc - 1])
            mld[hc + 1] = b.dma("sync", wstP[(hc + 1) % 2],
                                w1_h[(hc + 1) * 128:(hc + 2) * 128, :], sem=wsemQ[(hc + 1) % 2])
        b.wait("vector", wsemQ[hc % 2], mld[hc])
        mproc[hc] = b.V(lambda e, hc=hc: e.tensor_reduce(
            wsum[:, hc:hc + 1], wstP[hc % 2], AX.X, ALU.add, apply_absolute_value=True), inc="sV")
    b.emit("vector", lambda e: e.drain())
    b.Vd(lambda e: e.reduce_sum(stat[:, 8:9], wsum[:, 0:HC], AX.X))
    v_ws = b.mark("vector", "sV")
    b.wait("tensor", "sV", v_ws)
    t_ws = b.emit("tensor", lambda e: e.matmul(PS_D[0:1, 0:1], stat[:, 8:9], stat[:, 15:16],
                                               start=True, stop=True), inc="sT")
    b.wait("vector", "sT", t_ws)
    b.Vd(lambda e: e.tensor_scalar(nstat[:, 0:1], PS_D[0:1, 0:1], 1.0 / (H * D), 1e-5,
                                   ALU.mult, ALU.max))
    b.Vd(lambda e: e.reciprocal(nstat[:, 1:2], nstat[:, 0:1]))
    v_w1s = b.mark("vector", "sV")
    b.wait("tensor", "sV", v_w1s)
    t_sc = b.emit("tensor", lambda e: e.matmul(PS_D[:, 4:6], ones_row, nstat[:, 0:2],
                                               start=True, stop=True), inc="sT")
    b.wait("vector", "sT", t_sc)
    b.Vd(lambda e: e.tensor_copy(stat[:, 6:8], PS_D[:, 4:6]))
    v_w1c = b.mark("vector", "sV")
    b.wait("tensor", "sV", v_w1c)

    def quant_rows_to(dst_bf16, src, scale_col, clip1):
        assert clip1
        b.Vd(lambda e: e.tensor_scalar(wy, src, scale_col, None, ALU.mult))
        b.Vd(lambda e: e.tensor_single_scalar(wf, wy, 0.5, ALU.is_ge))
        b.Vd(lambda e: e.tensor_single_scalar(wy, wy, -1.0, ALU.mult))
        b.Vd(lambda e: e.tensor_single_scalar(wy, wy, 0.5, ALU.is_ge))
        b.Vd(lambda e: e.tensor_sub(dst_bf16, wf, wy))

    wqP = [wq_st, wq_stB]
    b.wait("sync", "sV", b.cnt["sV"])
    qld = {0: b.dma("sync", wstP[0], w1_h[0:128, :], sem=wsemQ[0])}
    vq_m = {}
    vv_m = {}
    trn_m = {}

    def w1_copies(hc):
        b.wait("vector", "sT", trn_m[hc])
        for dc in range(DC):
            b.V(lambda e, dc=dc, hc=hc: e.tensor_copy(
                w1qT[:, dc * H + hc * 128: dc * H + hc * 128 + 128],
                PS_B16[:, dc * 128:(dc + 1) * 128]))
        vv_m[hc] = b.mark("vector", "sV")

    for hc in range(HC):
        if hc + 1 < HC:
            if hc >= 1:
                b.wait("sync", "sV", vq_m[hc - 1])
            qld[hc + 1] = b.dma("sync", wstP[(hc + 1) % 2],
                                w1_h[(hc + 1) * 128:(hc + 2) * 128, :], sem=wsemQ[(hc + 1) % 2])
        b.wait("vector", wsemQ[hc % 2], qld[hc])
        if hc >= 2:
            b.wait("vector", "sT", trn_m[hc - 2])
        quant_rows_to(wqP[hc % 2], wstP[hc % 2], stat[:, 7:8], True)
        vq_m[hc] = b.mark("vector", "sV")
        if hc >= 1:
            w1_copies(hc - 1)
        b.wait("tensor", "sV", vv_m[hc - 1] if hc >= 1 else vq_m[hc])
        if hc == 0:
            pass
        else:
            b.wait("tensor", "sV", vq_m[hc])
        for dc in range(DC):
            b.emit("tensor", lambda e, dc=dc, hc=hc: e.transpose(
                PS_B16[:, dc * 128:(dc + 1) * 128], wqP[hc % 2][:, dc * 128:(dc + 1) * 128],
                ident16))
        trn_m[hc] = b.mark("tensor", "sT")
    w1_copies(HC - 1)
    b.wait("tensor", "sV", vv_m[HC - 1])
    b.Vd(lambda e: e.tensor_scalar(d1_all[:, 0:T], d1_all[:, 0:T], stat[:, 6:7],
                                   None, ALU.mult))
    v_d1 = b.mark("vector", "sV")
    b.wait("tensor", "sV", v_d1)
    t_d1 = b.emit("tensor", lambda e: e.transpose(PS_D[0:T, 0:128], d1_all[:, 0:T], ident),
                  inc="sT")
    b.wait("vector", "sT", t_d1)
    v_d1b = b.Vd(lambda e: e.tensor_copy(trsp[0:T, 0:128], PS_D[0:T, 0:128]), inc="sV")
    b.wait("gpsimd", "sV", v_d1b)
    b.dma("gpsimd", flat_scr[0:T, :], trsp[0:T, 0:128])
    b.wait("gpsimd", "dG", b.cnt["dG"])
    b.dma("gpsimd", row_Sa[:, 0:S], flat_scr.rearrange("a b -> (a b)").unsqueeze(0)[:, 0:S])
    g_row = b.cnt["dG"]
    b.wait("tensor", "dG", g_row)
    b.wait("tensor", "sV", v_d1b)
    last = None
    for hh in range(S // 512):
        last = b.emit("tensor", lambda e, hh=hh: e.matmul(
            PS_A[:, (hh % 2) * 512:(hh % 2) * 512 + 512], ones_row,
            row_Sa[:, hh * 512: hh * 512 + 512], start=True, stop=True),
            inc="sT")
        b.wait("vector", "sT", last)
        vv = b.V(lambda e, hh=hh: e.tensor_copy(
            d1_rep[:, hh * 512:(hh + 1) * 512], PS_A[:, (hh % 2) * 512:(hh % 2) * 512 + 512]),
            inc="sV")
        b.wait("tensor", "sV", vv)
    v_p3prep = b.mark("vector", "sV")
    b.wait("tensor", "sV", v_p3prep)
    b.wait("scalar", "sV", v_p3prep)
    b.wait("sync", "sV", v_p3prep)

    b.emit("scalar", lambda e: e.preload_act_table(AF.Silu))
    if dbg:
        dbg_w1qT = nc.dram_tensor("dbg_w1qT", [128, DC * H], BF16, kind="ExternalOutput").ap()
        dbg_d1rep = nc.dram_tensor("dbg_d1rep", [128, S], F32, kind="ExternalOutput").ap()
        dbg_hblk = nc.dram_tensor("dbg_hblk", [128, HC * (TAS + 2)], F32, kind="ExternalOutput").ap()
        b.wait("gpsimd", "sV", b.cnt["sV"])
        b.dma("gpsimd", dbg_w1qT, w1qT)
        b.dma("gpsimd", dbg_d1rep, d1_rep)
        b.wait("sync", "dG", b.cnt["dG"])

    spill_rec = {}
    evac_rec = {}
    silu_rec = {}
    red_rec = {}
    trn_rec = {}
    cv1_rec = {}
    d_xq = {}
    blk_tf = {}
    xq_blkP = [xq_blk, xq_blkB]
    cbufs3 = [cbufs[0], cbufs[1], cbuf2]
    PSH3 = [PS_A, PS_B, PS_C]

    def xq_load(tb):
        lo = tb * TAS
        has_l = tb > 0
        has_r = tb < TA - 1
        src_lo = lo - 1 if has_l else lo
        src_hi = lo + TAS + 1 if has_r else lo + TAS
        dst_lo = 0 if has_l else 1
        xqb = xq_blkP[tb % 2]
        if tb >= 2:
            b.wait("sync", "sT", blk_tf[tb - 2])
        xsm = "dXa" if tb % 2 == 0 else "dXb"
        b.dma("sync", xqb.rearrange("p (dc t) -> p dc t", dc=DC)[:, :, dst_lo:dst_lo + (src_hi - src_lo)],
              xq_scr[:, :, src_lo:src_hi].rearrange("dc p t -> p dc t"), sem=xsm)
        d_xq[tb] = (xsm, b.cnt[xsm])

    def emit_transposes(key):
        pst = PS_E if key % 2 == 0 else PS_D
        b.wait("tensor", "sS", silu_rec[key])
        if key >= 2:
            b.wait("tensor", "sV", red_rec[key - 2])
        rbuf = r_sb[key % 2]
        for j in range(NSA):
            b.emit("tensor", lambda e, j=j, rbuf=rbuf, pst=pst: e.transpose(
                pst[:, j * 128:(j + 1) * 128], rbuf[:, j * 128:(j + 1) * 128], ident))
        trn_rec[key] = b.mark("tensor", "sT")

    def emit_reduces(key):
        pst = PS_E if key % 2 == 0 else PS_D
        hc = key % HC
        b.wait("vector", "sT", trn_rec[key])
        for j in range(NSA):
            b.V(lambda e, j=j, hc=hc, pst=pst: e.tensor_reduce(
                redb[:, hc * NSA + j: hc * NSA + j + 1], pst[:, j * 128:(j + 1) * 128],
                AX.X, ALU.max, apply_absolute_value=True))
        red_rec[key] = b.mark("vector", "sV")
        if hc == HC - 1:
            tbp = key // HC
            for j in range(NSA):
                b.V(lambda e, j=j, tbp=tbp: e.tensor_reduce(
                    s2sc[:, tbp * NSA + j: tbp * NSA + j + 1],
                    redb.rearrange("p (hc j) -> p hc j", j=NSA)[:, :, j], AX.X, ALU.max))
            b.emit("vector", lambda e: e.drain())

    xq_load(0)
    for tb in range(TA):
        lo = tb * TAS
        has_l = tb > 0
        has_r = tb < TA - 1
        xqb = xq_blkP[tb % 2]
        if tb + 1 < TA:
            xq_load(tb + 1)
        if not has_l or not has_r:
            b.wait("vector", d_xq[tb][0], d_xq[tb][1])
            if not has_l:
                b.V(lambda e, xqb=xqb: e.memset(
                    xqb.rearrange("p (dc t) -> p dc t", dc=DC)[:, :, 0:1], 0.0))
            if not has_r:
                b.V(lambda e, xqb=xqb: e.memset(
                    xqb.rearrange("p (dc t) -> p dc t", dc=DC)[:, :, HB - 1:HB], 0.0))
            vz = b.mark("vector", "sV")
            b.wait("tensor", "sV", vz)
        b.wait("tensor", d_xq[tb][0], d_xq[tb][1])

        for hc in range(HC):
            key = tb * HC + hc
            psh = PSH3[key % 3]
            if key >= 3:
                b.wait("tensor", "sV", evac_rec[key - 3])
            for dc in range(DC):
                b.emit("tensor", lambda e, hc=hc, dc=dc, psh=psh, xqb=xqb: e.matmul(
                    psh[:, 0:HB],
                    w1qT[:, dc * H + hc * 128: dc * H + hc * 128 + 128],
                    xqb[:, dc * HB: dc * HB + HB],
                    start=(dc == 0), stop=(dc == DC - 1)))
            t_h = b.mark("tensor", "sT")
            if key >= 1:
                emit_transposes(key - 1)
            b.wait("vector", "sT", t_h)
            off = hc * HB
            c0 = 1 if not has_l else 0
            c1 = HB - 1 if not has_r else HB
            d0 = lo - 1 + c0
            b.V(lambda e, off=off, c0=c0, c1=c1, d0=d0, psh=psh: e.tensor_mul(
                h_blk[:, off + c0: off + c1], psh[:, c0:c1],
                d1_rep[:, d0: d0 + (c1 - c0)]))
            if not has_l:
                b.V(lambda e, off=off: e.memset(h_blk[:, off: off + 1], 0.0))
            if not has_r:
                b.V(lambda e, off=off: e.memset(h_blk[:, off + HB - 1: off + HB], 0.0))
            v_h = b.mark("vector", "sV")
            evac_rec[key] = v_h
            cbuf = cbufs3[key % 3]
            b.wait("scalar", "sV", v_h)
            cv1_rec[key] = b.emit("scalar", lambda e, off=off, hc=hc, cbuf=cbuf: e.activation(
                cbuf, h_blk[:, off: off + TAS], AF.Copy,
                bias=0.0, scale=convw[:, hc * 3: hc * 3 + 1]), inc="sS")
            b.wait("vector", "sS", cv1_rec[key])
            b.Vd(lambda e, off=off, hc=hc, cbuf=cbuf: e.scalar_tensor_tensor(
                cbuf, h_blk[:, off + 1: off + 1 + TAS],
                convw[:, hc * 3 + 1: hc * 3 + 2], cbuf, ALU.mult, ALU.add))
            b.Vd(lambda e, off=off, hc=hc, cbuf=cbuf: e.scalar_tensor_tensor(
                cbuf, h_blk[:, off + 2: off + 2 + TAS],
                convw[:, hc * 3 + 2: hc * 3 + 3], cbuf, ALU.mult, ALU.add))
            v_c = b.mark("vector", "sV")
            if key >= 1:
                emit_reduces(key - 1)
            b.wait("scalar", "sV", v_c)
            if key - 2 >= 0:
                b.wait("scalar", "dGa" if key % 2 == 0 else "dGb", spill_rec[key - 2])
            rbuf = r_sb[key % 2]
            silu_rec[key] = b.emit("scalar", lambda e, rbuf=rbuf, cbuf=cbuf, hc=hc: e.activation(
                rbuf, cbuf, AF.Silu, bias=convb[:, hc:hc + 1], scale=1.0), inc="sS")
            b.wait("gpsimd", "sS", silu_rec[key])
            spill_rec[key] = b.dma("gpsimd", r_scr[hc, :, lo:lo + TAS], rbuf,
                                   sem=("dGa" if key % 2 == 0 else "dGb"))
        blk_tf[tb] = b.cnt["sT"]
        if tb == TA - 1:
            kend = tb * HC + HC - 1
            emit_transposes(kend)
            emit_reduces(kend)
    b.wait("gpsimd", "dG", b.cnt["dG"])
    b.wait("gpsimd", "dGa", b.cnt["dGa"])
    b.wait("gpsimd", "dGb", b.cnt["dGb"])

    vB = b.cnt["sV"]; tB = b.cnt["sT"]; sB = b.cnt["sS"]; gB = b.cnt["dG"]; dB = b.cnt["dS"]
    gA = b.cnt["dGa"]; gBb = b.cnt["dGb"]
    xA = b.cnt["dXa"]; xBb = b.cnt["dXb"]
    for eng in ("sync", "tensor", "vector", "scalar", "gpsimd"):
        b.wait(eng, "sV", vB); b.wait(eng, "sT", tB); b.wait(eng, "sS", sB)
        b.wait(eng, "dG", gB); b.wait(eng, "dS", dB)
        b.wait(eng, "dGa", gA); b.wait(eng, "dGb", gBb)
        b.wait(eng, "dXa", xA); b.wait(eng, "dXb", xBb)


    if phases <= 3:
        b.wait("gpsimd", "dG", b.cnt["dG"])
        b.wait("gpsimd", "dS", b.cnt["dS"])
        b.wait("gpsimd", "dGa", b.cnt["dGa"])
        b.wait("gpsimd", "dGb", b.cnt["dGb"])
        return finish(nc, es, block, b, ops_only=True)

    w2P = [w2st, w2stB]
    b.wait("sync", "sV", b.cnt["sV"])
    m2ld = {0: b.dma("sync", w2P[0], w2_h[0:128, :], sem="dXa")}
    m2proc = {}
    for dc in range(DC):
        if dc + 1 < DC:
            if dc >= 1:
                b.wait("sync", "sV", m2proc[dc - 1])
            m2ld[dc + 1] = b.dma("sync", w2P[(dc + 1) % 2],
                                 w2_h[(dc + 1) * 128:(dc + 2) * 128, :],
                                 sem=("dXa" if (dc + 1) % 2 == 0 else "dXb"))
        b.wait("vector", "dXa" if dc % 2 == 0 else "dXb", m2ld[dc])
        m2proc[dc] = b.V(lambda e, dc=dc: e.tensor_reduce(
            wsum[:, dc:dc + 1], w2P[dc % 2], AX.X, ALU.add, apply_absolute_value=True), inc="sV")
    b.emit("vector", lambda e: e.drain())
    b.Vd(lambda e: e.reduce_sum(stat[:, 8:9], wsum[:, 0:DC], AX.X))
    v_ws2 = b.mark("vector", "sV")
    b.wait("tensor", "sV", v_ws2)
    t_ws2 = b.emit("tensor", lambda e: e.matmul(PS_D[0:1, 2:3], stat[:, 8:9], stat[:, 15:16],
                                                start=True, stop=True), inc="sT")
    b.wait("vector", "sT", t_ws2)
    b.Vd(lambda e: e.tensor_scalar(nstat[:, 2:3], PS_D[0:1, 2:3], 1.0 / (H * D), 1e-5,
                                   ALU.mult, ALU.max))
    b.Vd(lambda e: e.reciprocal(nstat[:, 3:4], nstat[:, 2:3]))
    v_w2s = b.mark("vector", "sV")
    b.wait("tensor", "sV", v_w2s)
    t_sc2 = b.emit("tensor", lambda e: e.matmul(PS_D[:, 6:8], ones_row, nstat[:, 2:4],
                                                start=True, stop=True), inc="sT")
    b.wait("vector", "sT", t_sc2)
    b.Vd(lambda e: e.tensor_copy(stat[:, 6:8], PS_D[:, 6:8]))
    b.Vd(lambda e: e.tensor_scalar_max(s2sc[:, 0:T], s2sc[:, 0:T], 1e-5))
    b.Vd(lambda e: e.tensor_scalar(d2_all[:, 0:T], s2sc[:, 0:T], 1.0 / 127.0,
                                   None, ALU.mult))
    b.Vd(lambda e: e.tensor_scalar(d2_all[:, 0:T], d2_all[:, 0:T], stat[:, 6:7],
                                   None, ALU.mult))
    b.Vd(lambda e: e.reciprocal(s2sc[:, 0:T], s2sc[:, 0:T]))
    b.Vd(lambda e: e.tensor_scalar(s2sc[:, 0:T], s2sc[:, 0:T], 127.0, None, ALU.mult))
    v_d2 = b.mark("vector", "sV")
    b.wait("tensor", "sV", v_d2)

    wqP = [wq_st, wq_stB]
    b.wait("sync", "sV", b.cnt["sV"])
    q2ld = {0: b.dma("sync", w2P[0], w2_h[0:128, :], sem="dXa")}
    vq2 = {}
    vv2 = {}
    trn2 = {}
    N2P = DC * 4

    def w2_copies(i):
        dc, part = divmod(i, 4)
        b.wait("vector", "sT", trn2[i])
        for t8 in range(8):
            hcg = part * 8 + t8
            b.V(lambda e, t8=t8, hcg=hcg, dc=dc: e.tensor_copy(
                w2qT[:, hcg * D + dc * 128: hcg * D + dc * 128 + 128],
                PS_B16[:, t8 * 128:(t8 + 1) * 128]))
        vv2[i] = b.mark("vector", "sV")

    for i in range(N2P):
        dc, part = divmod(i, 4)
        if part == 0:
            if dc + 1 < DC:
                if dc >= 1:
                    b.wait("sync", "sV", vq2[dc * 4 - 1])
                q2ld[dc + 1] = b.dma("sync", w2P[(dc + 1) % 2],
                                     w2_h[(dc + 1) * 128:(dc + 2) * 128, :],
                                     sem=("dXa" if (dc + 1) % 2 == 0 else "dXb"))
            b.wait("vector", "dXa" if dc % 2 == 0 else "dXb", q2ld[dc])
        sl = slice(part * D, (part + 1) * D)
        if i >= 2:
            b.wait("vector", "sT", trn2[i - 2])
        quant_rows_to(wqP[i % 2], w2P[dc % 2][:, sl], stat[:, 7:8], True)
        vq2[i] = b.mark("vector", "sV")
        if i >= 1:
            w2_copies(i - 1)
        b.wait("tensor", "sV", vv2[i - 1] if i >= 1 else vq2[i])
        if i >= 1:
            b.wait("tensor", "sV", vq2[i])
        for t8 in range(8):
            b.emit("tensor", lambda e, t8=t8, i=i: e.transpose(
                PS_B16[:, t8 * 128:(t8 + 1) * 128], wqP[i % 2][:, t8 * 128:(t8 + 1) * 128],
                ident16))
        trn2[i] = b.mark("tensor", "sT")
    w2_copies(N2P - 1)
    b.wait("tensor", "sV", vv2[N2P - 1])
    b.wait("sync", "sV", vv2[N2P - 1])
    wo_ld = {0: b.dma("sync", w_st, Wo_h[0:128, :], sem="dXa")}
    wo_vv = {}
    for oc in range(DC):
        wop = [w_st, w_stB][oc % 2]
        if oc + 1 < DC:
            if oc >= 1:
                b.wait("sync", "sV", wo_vv[oc - 1])
            wo_ld[oc + 1] = b.dma("sync", [w_st, w_stB][(oc + 1) % 2],
                                  Wo_h[(oc + 1) * 128:(oc + 2) * 128, :],
                                  sem=("dXa" if (oc + 1) % 2 == 0 else "dXb"))
        b.wait("tensor", "dXa" if oc % 2 == 0 else "dXb", wo_ld[oc])
        if oc >= 1:
            b.wait("tensor", "sV", wo_vv[oc - 1])
        for dc in range(DC):
            b.emit("tensor", lambda e, dc=dc, wop=wop: e.transpose(
                PS_C[:, dc * 128:(dc + 1) * 128], wop[:, dc * 128:(dc + 1) * 128], ident))
        t_tr = b.mark("tensor", "sT")
        b.wait("vector", "dXa" if oc % 2 == 0 else "dXb", wo_ld[oc])
        b.wait("vector", "sT", t_tr)
        for dc in range(DC):
            b.V(lambda e, dc=dc, oc=oc: e.tensor_copy(
                WoT[:, dc * D + oc * 128: dc * D + oc * 128 + 128],
                PS_C[:, dc * 128:(dc + 1) * 128]))
        wo_vv[oc] = b.mark("vector", "sV")
    b.wait("tensor", "sV", wo_vv[DC - 1])
    v_pb = b.cnt["sV"]
    for eng in ("sync", "tensor", "vector", "scalar", "gpsimd"):
        b.wait(eng, "sV", v_pb)
    vov_prev = None
    prev_qrec = None
    ov_tP = [ov_t, ov_tB]
    x2_tP = [x2_t, x2_tB]
    acc_tP = [acc_t, acc_tB]
    dld = {}
    asm_rec = {}
    qdone = {}
    tf_last = {}
    out_dg = {}
    rsem = lambda i: ("dRa", "dRb", "dRc", "dRd")[i % 4]

    def build_slab(g):
        dst = s2_repP[g % 2]
        b.Vd(lambda e, g=g: e.tensor_copy(stat[:, 8:12], s2sc[:, g * 4:g * 4 + 4]))
        v_s2c = b.mark("vector", "sV")
        b.wait("tensor", "sV", v_s2c)
        t_s2 = b.emit("tensor", lambda e: e.transpose(PS_D[0:4, 0:128], stat[:, 8:12], ident),
                      inc="sT")
        b.wait("vector", "sT", t_s2)
        v_s2r = b.Vd(lambda e: e.tensor_copy(trsp[0:4, 0:128], PS_D[0:4, 0:128]), inc="sV")
        b.wait("gpsimd", "sV", v_s2r)
        b.dma("gpsimd", flat_scr[0:4, :], trsp[0:4, 0:128])
        b.wait("gpsimd", "dG", b.cnt["dG"])
        b.dma("gpsimd", row_Sb[:, 0:2 * QBS2],
              flat_scr.rearrange("a b -> (a b)").unsqueeze(0)[:, 0:2 * QBS2])
        gr = b.cnt["dG"]
        b.wait("tensor", "dG", gr)
        b.wait("tensor", "sV", v_s2r)
        t_s2b = b.emit("tensor", lambda e, dst=dst: e.matmul(
            PS_C[:, 0:2 * QBS2], ones_row, row_Sb[:, 0:2 * QBS2],
            start=True, stop=True), inc="sT")
        b.wait("vector", "sT", t_s2b)
        v_s2 = b.Vd(lambda e, dst=dst: e.tensor_copy(dst, PS_C[:, 0:2 * QBS2]), inc="sV")
        slab_rdy[g] = v_s2
        b.wait("tensor", "sV", v_s2)

    def quant_half(qb, lohalf):
        nonlocal prev_qrec
        lo = qb * QBS2
        slab = s2_repP[(qb // 2) % 2][:, (qb % 2) * QBS2:(qb % 2 + 1) * QBS2]
        xq2b = xq2P_[qb % 2]
        h0, h1 = lohalf * (HC // 2), (lohalf + 1) * (HC // 2)
        if lohalf == 0:
            if qb >= 2:
                b.wait("vector", "sT", tf_last[qb - 2])
            for k in (0, 1, 2):
                if prev_qrec is not None:
                    b.wait("sync", "sV", prev_qrec[k])
                qh_rec[k] = b.dma("sync", r_ld[k], r_scr[k, :, lo:lo + QBS2], sem=rsem(k))
        for hcq in range(h0, h1):
            rb = r_ld[hcq % 4]
            nxt = hcq + 3
            if nxt < HC:
                if hcq - 1 >= 0:
                    b.wait("sync", "sV", qq_rec[hcq - 1])
                elif prev_qrec is not None:
                    b.wait("sync", "sV", prev_qrec[3])
                qh_rec[nxt] = b.dma("sync", r_ld[nxt % 4],
                                    r_scr[nxt, :, lo:lo + QBS2], sem=rsem(nxt))
            b.wait("vector", rsem(hcq), qh_rec[hcq])

            def chain(e, rb=rb, slab=slab, hcq=hcq, xq2b=xq2b):
                e.tensor_mul(hy, rb, slab)
                e.drain()
                e.tensor_single_scalar(hi32, hy, ROFF, ALU.add)
                e.drain()
                e.scalar_tensor_tensor(hf, hy, ROFF, hi32, ALU.add, ALU.subtract)
                e.drain()
                e.tensor_single_scalar(hf, hf, 0.5, ALU.is_ge)
                e.drain()
                e.scalar_tensor_tensor(
                    xq2b[:, hcq * QBS2:(hcq + 1) * QBS2], hi32, -ROFF, hf, ALU.add, ALU.add)
            b.emit("vector", chain)
            qq_rec[hcq] = b.mark("vector", "sV")
        if lohalf == 1:
            prev_qrec = tuple(qq_rec[HC - 4 + k] for k in range(4))
            qdone[qb] = qq_rec[HC - 1]

    qh_rec = {}
    qq_rec = {}
    slab_rdy = {}
    g_acc_prev = [None]
    build_slab(0)
    quant_half(0, 0)
    quant_half(0, 1)
    b.dma("scalar", ov_t, ov_scr[0, :, :], sem="dA")
    b.dma("scalar", x2_t, x_h[0:128, :], sem="dA")
    dld[0] = ("dA", b.cnt["dA"])

    for tb in range(TB2):
        if tb + 1 < TB2 and (tb + 1) % 2 == 0:
            build_slab((tb + 1) // 2)
        xq2b = xq2P_[tb % 2]
        for j in range(NS2):
            ti = tb * NS2 + j
            if tb + 1 < TB2:
                quant_half(tb + 1, j)
            nti = ti + 1
            if nti < T:
                prev_v = asm_rec.get(nti - 2)
                if prev_v is not None:
                    b.wait("scalar", "sV", prev_v)
                sA = "dA" if nti % 2 == 0 else "dAb"
                b.dma("scalar", ov_tP[nti % 2], ov_scr[nti, :, :], sem=sA)
                b.dma("scalar", x2_tP[nti % 2], x_h[nti * 128:(nti + 1) * 128, :], sem=sA)
                dld[nti] = (sA, b.cnt[sA])
            ovb, xb = ov_tP[ti % 2], x2_tP[ti % 2]
            acc_ = acc_tP[ti % 2]
            b.wait("tensor", dld[ti][0], dld[ti][1])
            if vov_prev is not None:
                b.wait("tensor", "sS", vov_prev)
            for dc in range(DC):
                b.emit("tensor", lambda e, dc=dc, ovb=ovb: e.transpose(
                    PS_C[:, dc * 128:(dc + 1) * 128], ovb[:, dc * 128:(dc + 1) * 128], ident))
            t_ov = b.mark("tensor", "sT")
            psf = PS_A
            if j == 0:
                b.wait("tensor", "sV", qdone[tb])
            if ti >= 1:
                b.wait("tensor", "sV", asm_rec[ti - 1])
            for half in range(2):
                for hcq in range(HC):
                    b.emit("tensor", lambda e, half=half, hcq=hcq, j=j, psf=psf,
                           xq2b=xq2b: e.matmul(
                        psf[:, half * 512:(half + 1) * 512],
                        xq2b[:, hcq * QBS2 + j * 128: hcq * QBS2 + j * 128 + 128],
                        w2qT[:, hcq * D + half * 512: hcq * D + half * 512 + 512],
                        start=(hcq == 0), stop=(hcq == HC - 1)))
            t_f = b.mark("tensor", "sT")
            if j == NS2 - 1:
                tf_last[tb] = t_f
            b.wait("scalar", "sT", t_ov)
            b.emit("scalar", lambda e: e.copy(ovfm[:, 0:1024], PS_C[:, 0:1024]))
            v_ov = b.mark("scalar", "sS")
            vov_prev = v_ov
            b.wait("tensor", "sS", v_ov)
            for half in range(2):
                for dc in range(DC):
                    b.emit("tensor", lambda e, half=half, dc=dc: e.matmul(
                        PS_B[:, half * 512:(half + 1) * 512],
                        ovfm[:, dc * 128:(dc + 1) * 128],
                        WoT[:, dc * D + half * 512: dc * D + half * 512 + 512],
                        start=(dc == 0), stop=(dc == DC - 1)))
            t_o = b.mark("tensor", "sT")
            b.wait("scalar", "sT", t_f)
            if g_acc_prev[0] is not None:
                b.wait("scalar", "sG", g_acc_prev[0])
            b.emit("scalar", lambda e, ti=ti, psf=psf: e.mul(
                ffn_t[:, 0:1024], psf[:, 0:1024], d2_all[:, ti:ti + 1]))
            v_ffn = b.mark("scalar", "sS")
            b.wait("gpsimd", dld[ti][0], dld[ti][1])
            if ti >= 2:
                b.wait("gpsimd", "dGa" if ti % 2 == 0 else "dGb", out_dg[ti - 2])
            b.emit("gpsimd", lambda e, xb=xb, acc_=acc_: e.tensor_add(acc_, xb, bo_rep))
            b.emit("gpsimd", lambda e: e.drain())
            b.wait("gpsimd", "sS", v_ffn)
            b.emit("gpsimd", lambda e, acc_=acc_: e.tensor_add(acc_, acc_, ffn_t))
            g_acc = b.mark("gpsimd", "sG")
            g_acc_prev[0] = g_acc
            b.wait("vector", "sG", g_acc)
            b.wait("vector", "sT", t_o)
            for half in range(2):
                b.V(lambda e, half=half, acc_=acc_: e.scalar_tensor_tensor(
                    acc_[:, half * 512:(half + 1) * 512], PS_B[:, half * 512:(half + 1) * 512],
                    1.0, acc_[:, half * 512:(half + 1) * 512], ALU.mult, ALU.add))
            v_out = b.mark("vector", "sV")
            asm_rec[ti] = v_out
            b.wait("gpsimd", "sV", v_out)
            b.dma("gpsimd", out_h[ti * 128:(ti + 1) * 128, :], acc_,
                  sem=("dGa" if ti % 2 == 0 else "dGb"))
            out_dg[ti] = b.cnt["dGa" if ti % 2 == 0 else "dGb"]
    b.wait("vector", "dG", b.cnt["dG"])

    b.wait("gpsimd", "dG", b.cnt["dG"])
    b.wait("gpsimd", "dGa", b.cnt["dGa"])
    b.wait("gpsimd", "dGb", b.cnt["dGb"])
    b.wait("gpsimd", "dS", b.cnt["dS"])
    if b.cnt["dA"]:
        b.wait("gpsimd", "dA", b.cnt["dA"])
    if b.cnt["dAb"]:
        b.wait("gpsimd", "dAb", b.cnt["dAb"])
    for _rs in ("dRa", "dRb", "dRc", "dRd"):
        if b.cnt[_rs]:
            b.wait("gpsimd", _rs, b.cnt[_rs])
    return finish(nc, es, block, b)


def finish(nc, es, block, b, ops_only=False):
    ops = b.ops

    @block.sync
    def _(sync):
        for f in ops["sync"]:
            f(sync)

    @block.tensor
    def _(tensor):
        for f in ops["tensor"]:
            f(tensor)

    @block.vector
    def _(vector):
        for f in ops["vector"]:
            f(vector)

    @block.scalar
    def _(scalar):
        for f in ops["scalar"]:
            f(scalar)

    @block.gpsimd
    def _(gpsimd):
        for f in ops["gpsimd"]:
            f(gpsimd)

    es.close()
    return nc


def make_in_map(x_b, ins):
    f = np.float32
    return {
        "x": np.ascontiguousarray(x_b, dtype=f),
        "Wq": np.asarray(ins["L0_attn_Wq"], f), "Wk": np.asarray(ins["L0_attn_Wk"], f),
        "Wv": np.asarray(ins["L0_attn_Wv"], f), "Wo": np.asarray(ins["L0_attn_Wo"], f),
        "bq": np.asarray(ins["L0_attn_bq"], f).reshape(1, D),
        "bk": np.asarray(ins["L0_attn_bk"], f).reshape(1, D),
        "bv": np.asarray(ins["L0_attn_bv"], f).reshape(1, D),
        "bo": np.asarray(ins["L0_attn_bo"], f).reshape(1, D),
        "gam": np.asarray(ins["L0_attn_ln_gamma"], f).reshape(1, D),
        "bet": np.asarray(ins["L0_attn_ln_beta"], f).reshape(1, D),
        "mu": np.asarray(ins["L0_attn_mu"], f).reshape(1, 1),
        "w1": np.asarray(ins["L0_ffn_w1"], f),
        "w2": np.asarray(ins["L0_ffn_w2"], f),
        "cw": np.asarray(ins["L0_ffn_conv_w"], f).reshape(H, 3),
        "cb": np.asarray(ins["L0_ffn_conv_b"], f).reshape(1, H),
        "ident_in": np.eye(128, dtype=f),
    }


from concourse.bass_utils import run_bass_kernel_spmd

S_FULL = 4096
N_CORES = 8


def kernel(**inputs):
    x = np.asarray(inputs["L0_attn_sequence"], np.float32)
    nc = build_core(S_FULL)
    in_maps = [make_in_map(x[b], inputs) for b in range(N_CORES)]
    res = run_bass_kernel_spmd(nc, in_maps, core_ids=list(range(N_CORES)))
    return np.stack([np.asarray(r["out"], np.float32) for r in res.results], axis=0)


# version 36 (final)
# speedup vs baseline: 1.0689x; 1.0689x over previous
import contextlib
import numpy as np
import concourse.bass as bass
import concourse.mybir as mybir

F32 = mybir.dt.float32
BF16 = mybir.dt.bfloat16
I32 = mybir.dt.int32
AF = mybir.ActivationFunctionType
ALU = mybir.AluOpType
AX = mybir.AxisListType

D = 1024
H = 4096
NH = 16
DC = D // 128
HC = H // 128
EPS1 = 1e-5
EPS2 = 1e-6
ROFF = 256.0


class Builder:
    def __init__(self, nc, es):
        self.nc = nc
        self.ops = {"sync": [], "tensor": [], "vector": [], "scalar": [], "gpsimd": []}
        self.sems = {}
        self.cnt = {}
        for name in ("dS", "dG", "dGa", "dGb", "dRa", "dRb", "dRc", "dRd", "dXa", "dXb", "dA", "dAb", "sT", "sV", "sS", "sG"):
            self.sems[name] = es.enter_context(nc.semaphore(name))
            self.cnt[name] = 0

    def emit(self, eng, make, inc=None, inc_by=1):
        if inc is not None:
            self.cnt[inc] += inc_by
        sem = self.sems[inc] if inc is not None else None

        def run(e, make=make, sem=sem, inc_by=inc_by):
            ins = make(e)
            if sem is not None:
                ins.then_inc(sem, inc_by)
        self.ops[eng].append(run)
        return self.cnt[inc] if inc is not None else None

    def mark(self, eng, inc):
        return self.emit(eng, lambda e: e.drain(), inc=inc)

    def wait(self, eng, sem, val):
        assert val is not None and val >= 0
        s = self.sems[sem]
        self.ops[eng].append(lambda e, s=s, val=val: e.wait_ge(s, val))

    def dma(self, eng, out, in_, slow=False, sem=None):
        if sem is None:
            sem = {"sync": "dS", "gpsimd": "dG", "scalar": "dA"}[eng]
        return self.emit(eng, lambda e, o=out, i=in_, s=slow: e.dma_start(
            out=o, in_=i, allow_slow_non_contiguous=s), inc=sem, inc_by=16)

    def V(self, make, inc=None):
        return self.emit("vector", make, inc=inc)

    def Vd(self, make, inc=None):
        r = self.emit("vector", make, inc=inc)
        self.emit("vector", lambda e: e.drain())
        return r


VW0 = 130

def build_core(S=4096, phases=4, dbg=False):
    T = S // 128
    KC = S // 128
    QBS = min(512, S)
    QB = S // QBS
    TAS = min(256, S)
    TA = S // TAS
    NSA = TAS // 128
    TBS = min(512, S)
    TB = S // TBS
    NSUB = TBS // 128
    QBS2 = min(256, S)
    TB2 = S // QBS2
    NS2 = QBS2 // 128

    es = contextlib.ExitStack()
    nc = bass.Bass("TRN2", target_bir_lowering=False)
    b = Builder(nc, es)

    def din(name, shape, dt=F32):
        return nc.dram_tensor(name, shape, dt, kind="ExternalInput").ap()
    x_h = din("x", [S, D])
    Wq_h = din("Wq", [D, D]); Wk_h = din("Wk", [D, D]); Wv_h = din("Wv", [D, D]); Wo_h = din("Wo", [D, D])
    bq_h = din("bq", [1, D]); bk_h = din("bk", [1, D]); bv_h = din("bv", [1, D]); bo_h = din("bo", [1, D])
    gam_h = din("gam", [1, D]); bet_h = din("bet", [1, D])
    mu_h = din("mu", [1, 1])
    w1_h = din("w1", [H, D]); w2_h = din("w2", [D, H])
    cw_h = din("cw", [H, 3]); cb_h = din("cb", [1, H])
    id_h = din("ident_in", [128, 128])
    out_h = nc.dram_tensor("out", [S, D], F32, kind="ExternalOutput").ap()

    KND = "ExternalOutput" if dbg else "Internal"
    q_scr = nc.dram_tensor("q_scr", [DC, 128, S], F32, kind=KND).ap()
    k_scr = nc.dram_tensor("k_scr", [DC, 128, S], F32, kind=KND).ap()
    v_scr = nc.dram_tensor("v_scr", [DC, T, 128, VW0], F32, kind=KND).ap()
    ov_scr = nc.dram_tensor("ov_scr", [T, 128, D], F32, kind=KND).ap()
    xq_scr = nc.dram_tensor("xq_scr", [DC, 128, S], BF16, kind=KND).ap()
    r_scr = nc.dram_tensor("r_scr", [HC, 128, S], F32, kind=KND).ap()
    flat_scr = nc.dram_tensor("flat_scr", [128, 128], F32, kind="Internal").ap()

    A = nc.alloc_sbuf_tensor
    ident = A("ident", [128, 128], F32).ap()
    ident16 = A("ident16", [128, 128], BF16).ap()
    ones_row = A("ones_row", [1, 128], F32).ap()
    mu_sb = A("mu_sb", [1, 1], F32).ap()
    mubias = A("mubias", [128, 1], F32).ap()
    gam_row = A("gam_row", [1, D], F32).ap()
    bet_col = A("bet_col", [128, DC], F32).ap()
    bv_rep = A("bv_rep", [128, D], F32).ap()
    bo_rep = A("bo_rep", [128, D], F32).ap()
    bqk_eff = A("bqk_eff", [128, 2 * DC], F32).ap()
    bv_col = A("bv_col", [128, DC], F32).ap()
    row_D = A("row_D", [1, D], F32).ap()
    row_D2 = A("row_D2", [1, D], F32).ap()
    d1_all = A("d1_all", [128, T], F32).ap()
    s2sc = A("s2sc", [128, T], F32).ap()
    d2_all = A("d2_all", [128, T], F32).ap()
    convw = A("convw", [128, 3 * HC], F32).ap()
    convb = A("convb", [128, HC], F32).ap()
    stat = A("stat", [128, 16], F32).ap()
    nstat = A("nstat", [1, 16], F32).ap()
    trsp = A("trsp", [128, 128], F32).ap()

    POOL_B = 179 * 1024
    pool = A("pool", [128, POOL_B], mybir.dt.uint8)
    pool_addr = nc.lookup_mloc(pool).addr

    def carve(off, name, shape, dt):
        handle = nc.alloc_sbuf_tensor_at(name, shape, dt, offset=pool_addr + int(off))
        return handle.ap()

    WqTh = carve(0, "WqTh", [128, DC * D], BF16)
    WqTl = carve(16384, "WqTl", [128, DC * D], BF16)
    WkTh = carve(32768, "WkTh", [128, DC * D], BF16)
    WkTl = carve(49152, "WkTl", [128, DC * D], BF16)
    WvTh = carve(65536, "WvTh", [128, DC * D], BF16)
    WvTl = carve(81920, "WvTl", [128, DC * D], BF16)
    wraw = carve(98304, "wraw", [128, D], F32)
    fold = carve(102400, "fold", [128, D], F32)
    x_t = carve(106496, "x_t", [128, D], F32)
    xn_t = carve(110592, "xn_t", [128, D], F32)
    xh_t = carve(114688, "xh_t", [128, D], F32)
    y_t = carve(118784, "y_t", [128, D], F32)
    i32_t = carve(122880, "i32_t", [128, D], I32)
    f_t = carve(126976, "f_t", [128, D], F32)
    xq_t = carve(131072, "xq_t", [128, D], BF16)
    xh_fmh = carve(133120, "xh_fmh", [128, D], BF16)
    xh_fml = carve(135168, "xh_fml", [128, D], BF16)
    xq_fm = carve(137216, "xq_fm", [128, D], BF16)
    st_q = carve(139264, "st_q", [128, D], F32)
    st_k = carve(143360, "st_k", [128, D], F32)
    st_v = carve(147456, "st_v", [128, DC * VW0], F32)
    rawT = carve(151680, "rawT", [128, D], F32)
    gam_rep = carve(155776, "gam_rep", [128, D], F32)
    x_tB = carve(159872, "x_tB", [128, D], F32)
    xh_tB = carve(163968, "xh_tB", [128, D], F32)
    xq_tB = carve(168064, "xq_tB", [128, D], BF16)
    xh_fmhB = carve(170112, "xh_fmhB", [128, D], BF16)
    xh_fmlB = carve(172160, "xh_fmlB", [128, D], BF16)
    q_tm = carve(174208, "q_tm", [128, D], F32)
    k_tm = carve(178304, "k_tm", [128, D], F32)
    assert 178304 + 4096 <= POOL_B

    k_pair = [carve(0, "k_pair0", [128, S], F32),
              carve(4 * S, "k_pair1", [128, S], F32)]
    q_pair = [carve(8 * S, "q_pair0", [128, S], F32),
              carve(12 * S, "q_pair1", [128, S], F32)]
    VW = 130
    v_pairb = [carve(16 * S, "v_pair0", [128, KC * VW], F32),
               carve(16 * S + 4 * KC * VW, "v_pair1", [128, KC * VW], F32)]
    p2o = 16 * S + 8 * KC * VW
    PLAG = 6
    t_buf = [carve(p2o + 4096 * i, "t_buf%d" % i, [128, 1024], F32) for i in range(PLAG)]
    e_buf = [carve(p2o + 4096 * PLAG + 4096 * i, "e_buf%d" % i, [128, 1024], F32) for i in range(PLAG)]
    p_buf = [carve(p2o + 8192 * PLAG + 4096 * i, "p_buf%d" % i, [128, 1024], F32) for i in range(PLAG)]
    ov_st = carve(p2o + 12288 * PLAG, "ov_st", [128, 512], F32)
    rc_st = carve(p2o + 12288 * PLAG + 2048, "rc_st", [128, 8], F32)
    ovt_sb = carve(p2o + 12288 * PLAG + 2176, "ovt_sb", [128, 1024], F32)
    assert p2o + 12288 * PLAG + 6272 <= POOL_B

    w1qT = carve(0, "w1qT", [128, DC * H], BF16)
    HB = TAS + 2
    h_blk = carve(65536, "h_blk", [128, HC * HB], F32)
    p3a0 = 65536 + 4 * HC * HB
    d1_rep = carve(p3a0, "d1_rep", [128, S], F32)
    p3a1 = p3a0 + 4 * S
    w_st = carve(p3a1, "w_st", [128, D], F32)
    wy = carve(p3a1 + 4096, "wy", [128, D], F32)
    wi32 = carve(p3a1 + 8192, "wi32", [128, D], I32)
    wf = carve(p3a1 + 12288, "wf", [128, D], F32)
    wq_st = carve(p3a1 + 16384, "wq_st", [128, D], BF16)
    wsum = carve(p3a1 + 18432, "wsum", [128, HC], F32)
    xq_blk = carve(p3a1 + 18560, "xq_blk", [128, DC * HB], BF16)
    p3a2 = p3a1 + 18560 + 2 * DC * HB
    cbufs = [carve(p3a2, "cbuf0", [128, TAS], F32),
             carve(p3a2 + 4 * TAS, "cbuf1", [128, TAS], F32)]
    r_sb = [carve(p3a2 + 8 * TAS, "r_sb0", [128, TAS], F32),
            carve(p3a2 + 12 * TAS, "r_sb1", [128, TAS], F32)]
    redb = carve(p3a2 + 16 * TAS, "redb", [128, NSA * HC], F32)
    p3a3 = p3a2 + 16 * TAS + 4 * NSA * HC
    row_Sa = carve(p3a3, "row_Sa", [1, S], F32)
    w_stB = carve(p3a3 + 4 * S, "w_stB", [128, D], F32)
    p3a4 = p3a3 + 4 * S + 4096
    xq_blkB = carve(p3a4, "xq_blkB", [128, DC * HB], BF16)
    cbuf2 = carve(p3a4 + 2 * DC * HB, "cbuf2", [128, TAS], F32)
    wq_stB = carve(p3a4, "wq_stB", [128, D], BF16)
    assert p3a4 + 2 * DC * HB + 4 * TAS <= POOL_B

    w2qT = carve(0, "w2qT", [128, HC * D], BF16)
    w2st = carve(65536, "w2st", [128, H], F32)
    WoT = carve(65536, "WoT2", [128, DC * D], F32)
    p3b0 = 65536 + 4 * DC * D
    xq2P_ = [carve(p3b0, "xq2_blk0", [128, HC * QBS2], BF16),
             carve(p3b0 + 2 * HC * QBS2, "xq2_blk1", [128, HC * QBS2], BF16)]
    p3b1 = p3b0 + 4 * HC * QBS2
    r_ld = [carve(p3b1, "r_ld0", [128, QBS2], F32),
            carve(p3b1 + 1024, "r_ld1", [128, QBS2], F32),
            carve(p3b1 + 34816 + 12288, "r_ld2", [128, QBS2], F32),
            carve(p3b1 + 34816 + 13312, "r_ld3", [128, QBS2], F32)]
    roffb = carve(p3b1 + 34816 + 14336, "roffb", [128, QBS2], F32)
    halfb = carve(p3b1 + 34816 + 15360, "halfb", [128, QBS2], F32)
    s2_repP = [carve(p3b1 + 2048, "s2_rep0", [128, 2 * QBS2], F32),
               carve(p3b1 + 4096, "s2_rep1", [128, 2 * QBS2], F32)]
    ov_t = carve(p3b1 + 6144, "ov_t", [128, D], F32)
    ovfm = carve(p3b1 + 10240, "ovfm", [128, D], F32)
    x2_t = carve(p3b1 + 14336, "x2_t", [128, D], F32)
    ov_tB = carve(p3b1 + 34816, "ov_tB", [128, D], F32)
    x2_tB = carve(p3b1 + 34816 + 4096, "x2_tB", [128, D], F32)
    ffn_t = carve(p3b1 + 18432, "ffn_t", [128, D], F32)
    acc_t = carve(p3b1 + 22528, "acc_t", [128, D], F32)
    w2stB = carve(p3b1 + 34816, "w2stB", [128, H], F32)
    assert p3b1 + 34816 + 4 * H <= POOL_B
    hy = carve(p3b1 + 26624, "hy", [128, QBS2], F32)
    hi32 = carve(p3b1 + 27648, "hi32", [128, QBS2], I32)
    hf = carve(p3b1 + 28672, "hf", [128, QBS2], F32)
    hyG = carve(p3b1 + 29696, "hyG", [128, QBS2], F32)
    hi32G = carve(p3b1 + 30720, "hi32G", [128, QBS2], I32)
    hfG = carve(p3b1 + 31744, "hfG", [128, QBS2], F32)
    row_Sb = carve(p3b1 + 32768, "row_Sb", [1, 2 * QBS2], F32)
    acc_tB = carve(p3b1 + 34816 + 8192, "acc_tB", [128, D], F32)
    assert p3b1 + 34816 + 4 * H <= POOL_B

    PS_A = nc.alloc_psum_tensor("PS_A", [128, 1024], F32).ap()
    PS_B = nc.alloc_psum_tensor("PS_B", [128, 1024], F32).ap()
    PS_C = nc.alloc_psum_tensor("PS_C", [128, 1024], F32).ap()
    PS_D = nc.alloc_psum_tensor("PS_D", [128, 512], F32).ap()
    PS_E = nc.alloc_psum_tensor("PS_E", [128, 512], F32).ap()
    PS_B16 = PS_B.bitcast(BF16)

    block = es.enter_context(nc.Block())

    def transpose_tile(src, dst, ps, ident_, n=DC, inc_last="sV"):
        for dc in range(n):
            b.emit("tensor", lambda e, dc=dc, src=src, ps=ps, ident_=ident_: e.transpose(
                ps[:, dc * 128:(dc + 1) * 128], src[:, dc * 128:(dc + 1) * 128], ident_))
        t = b.mark("tensor", "sT")
        b.wait("vector", "sT", t)
        vv = None
        for dc in range(n):
            vv = b.emit("vector", lambda e, dc=dc, dst=dst, ps=ps: e.tensor_copy(
                dst[:, dc * 128:(dc + 1) * 128], ps[:, dc * 128:(dc + 1) * 128]),
                inc=(inc_last if dc == n - 1 else None))
        return vv

    b.dma("sync", ident, id_h)
    b.dma("sync", mu_sb, mu_h)
    b.dma("sync", gam_row, gam_h)
    b.dma("sync", bet_col, bet_h.squeeze(0).rearrange("(dc p) -> p dc", p=128), slow=True)
    b.dma("sync", convw.rearrange("p (hc k) -> p hc k", k=3), cw_h.rearrange("(hc p) k -> p hc k", p=128))
    b.dma("sync", convb, cb_h.squeeze(0).rearrange("(hc p) -> p hc", p=128), slow=True)
    b.dma("sync", row_D, bv_h)
    b.dma("sync", row_D2, bo_h)
    b.dma("sync", bqk_eff[:, 0:DC], bq_h.squeeze(0).rearrange("(oc p) -> p oc", p=128), slow=True)
    b.dma("sync", bqk_eff[:, DC:2 * DC], bk_h.squeeze(0).rearrange("(oc p) -> p oc", p=128), slow=True)
    b.dma("sync", bv_col, bv_h.squeeze(0).rearrange("(oc p) -> p oc", p=128), slow=True)
    d0 = b.cnt["dS"]
    b.wait("sync", "dS", d0)
    b.wait("vector", "dS", d0)
    b.V(lambda e: e.tensor_copy(ident16, ident))
    b.V(lambda e: e.memset(ones_row, 1.0))
    b.V(lambda e: e.memset(stat[:, 15:16], 1.0))
    v0 = b.mark("vector", "sV")
    b.wait("tensor", "sV", v0)
    b.wait("tensor", "dS", d0)
    b.emit("tensor", lambda e: e.matmul(PS_D[:, 0:1], ones_row, mu_sb, start=True, stop=True))

    def rep_slab(dst, src_row, ps):
        last = None
        for hh in range(D // 512):
            last = b.emit("tensor", lambda e, hh=hh, ps=ps, src_row=src_row: e.matmul(
                ps[:, hh * 512:(hh + 1) * 512], ones_row, src_row[:, hh * 512:(hh + 1) * 512],
                start=True, stop=True),
                inc=("sT" if hh == D // 512 - 1 else None))
        b.wait("vector", "sT", last)
        return b.Vd(lambda e, dst=dst, ps=ps: e.tensor_copy(dst, ps[:, 0:D]), inc="sV"), last

    v_gam, _ = rep_slab(gam_rep, gam_row, PS_A)
    b.wait("tensor", "sV", v_gam)
    v_bo, _ = rep_slab(bo_rep, row_D2, PS_A)
    t_mu = b.mark("tensor", "sT")
    b.wait("vector", "sT", t_mu)
    v_mu = b.Vd(lambda e: e.tensor_scalar(mubias, PS_D[:, 0:1], -0.5, None, ALU.mult), inc="sV")

    b.emit("scalar", lambda e: e.preload_act_table(AF.Tanh))

    b.wait("vector", "sV", v_bo)
    WTs = (("q", Wq_h, (WqTh, WqTl), True), ("k", Wk_h, (WkTh, WkTl), True),
           ("v", Wv_h, (WvTh, WvTl), True))
    wrawP = [wraw, carve(159872, "wrawB", [128, D], F32)]
    wsemP = ["dXa", "dXb"]
    wld = {}
    wproc = {}
    for wi, (wname, W_h, WT_sb, need_bias) in enumerate(WTs):
        for oc in range(DC):
            ci = wi * DC + oc
            if ci == 0:
                wld[0] = b.dma("sync", wrawP[0], W_h[0:128, :], sem=wsemP[0])
            if ci + 1 < 3 * DC:
                wi2, oc2 = divmod(ci + 1, DC)
                W2 = WTs[wi2][1]
                if ci >= 1:
                    b.wait("sync", "sV", wproc[ci - 1])
                wld[ci + 1] = b.dma("sync", wrawP[(ci + 1) % 2],
                                    W2[oc2 * 128:(oc2 + 1) * 128, :], sem=wsemP[(ci + 1) % 2])
            wraw_c = wrawP[ci % 2]
            b.wait("vector", wsemP[ci % 2], wld[ci])
            v1 = b.Vd(lambda e, wraw_c=wraw_c: e.tensor_mul(fold, wraw_c, gam_rep), inc="sV")
            b.wait("tensor", "sV", v1)
            for dc in range(DC):
                b.emit("tensor", lambda e, dc=dc: e.transpose(
                    PS_C[:, dc * 128:(dc + 1) * 128], fold[:, dc * 128:(dc + 1) * 128], ident))
            if need_bias:
                for dc in range(DC):
                    b.emit("tensor", lambda e, dc=dc, wraw_c=wraw_c: e.transpose(
                        PS_A[:, dc * 128:(dc + 1) * 128], wraw_c[:, dc * 128:(dc + 1) * 128], ident))
            t1 = b.mark("tensor", "sT")
            WTh, WTl = WT_sb
            b.wait("scalar", "sT", t1)
            for dc in range(DC):
                sl = slice(dc * D + oc * 128, dc * D + oc * 128 + 128)
                b.emit("scalar", lambda e, dc=dc, sl=sl, WTh=WTh: e.copy(
                    WTh[:, sl], PS_C[:, dc * 128:(dc + 1) * 128]))
            if need_bias:
                for dc in range(DC):
                    b.emit("scalar", lambda e, dc=dc: e.copy(
                        rawT[:, dc * 128:(dc + 1) * 128], PS_A[:, dc * 128:(dc + 1) * 128]))
            s_h = b.mark("scalar", "sS")
            b.wait("vector", "sS", s_h)
            for dc in range(DC):
                sl = slice(dc * D + oc * 128, dc * D + oc * 128 + 128)
                b.V(lambda e, dc=dc, sl=sl, WTh=WTh, WTl=WTl: e.tensor_sub(
                    WTl[:, sl], PS_C[:, dc * 128:(dc + 1) * 128], WTh[:, sl]))
            vv = b.mark("vector", "sV")
            wproc[ci] = vv
            b.wait("tensor", "sV", vv)
            if need_bias:
                for dc in range(DC):
                    b.emit("tensor", lambda e, dc=dc, oc=oc, wi=wi: e.matmul(
                        PS_E[:, wi * DC + oc: wi * DC + oc + 1],
                        rawT[:, dc * 128:(dc + 1) * 128],
                        bet_col[:, dc:dc + 1],
                        start=(dc == 0 and oc == 0 and wi == 0),
                        stop=(dc == DC - 1 and oc == DC - 1 and wi == 2)))
    t_bet = b.mark("tensor", "sT")
    b.wait("vector", "sT", t_bet)
    b.Vd(lambda e: e.tensor_add(bqk_eff, bqk_eff, PS_E[:, 0:2 * DC]))
    v_be = b.Vd(lambda e: e.tensor_add(bv_col, bv_col, PS_E[:, 2 * DC:3 * DC]), inc="sV")
    b.wait("tensor", "sV", v_be)
    t_bt = b.emit("tensor", lambda e: e.transpose(PS_D[0:DC, 0:128], bv_col, ident), inc="sT")
    b.wait("vector", "sT", t_bt)
    v_bt = b.Vd(lambda e: e.tensor_copy(trsp[0:DC, 0:128], PS_D[0:DC, 0:128]), inc="sV")
    b.wait("gpsimd", "sV", v_bt)
    b.dma("gpsimd", flat_scr[0:DC, :], trsp[0:DC, 0:128])
    b.wait("gpsimd", "dG", b.cnt["dG"])
    b.dma("gpsimd", row_D[:, 0:D], flat_scr.rearrange("a b -> (a b)").unsqueeze(0)[:, 0:D])
    b.wait("tensor", "dG", b.cnt["dG"])
    b.wait("tensor", "sV", v_bt)
    v_bvr, _ = rep_slab(bv_rep, row_D, PS_A)
    b.wait("tensor", "sV", v_bvr)

    def newton_rsqrt(col_w, col_r):
        b.Vd(lambda e: e.tensor_scalar(stat[:, 11:12], stat[:, col_w:col_w + 1],
                                       0.5, 0.5, ALU.mult, ALU.add))
        for _ in range(4):
            b.Vd(lambda e: e.reciprocal(stat[:, 12:13], stat[:, 11:12]))
            b.Vd(lambda e: e.tensor_mul(stat[:, 12:13], stat[:, 12:13], stat[:, col_w:col_w + 1]))
            b.Vd(lambda e: e.tensor_add(stat[:, 11:12], stat[:, 11:12], stat[:, 12:13]))
            b.Vd(lambda e: e.tensor_scalar(stat[:, 11:12], stat[:, 11:12], 0.5, None, ALU.mult))
        b.Vd(lambda e: e.reciprocal(stat[:, col_r:col_r + 1], stat[:, 11:12]))

    def ln_tile(src, dst, eps, cm, cr):
        b.Vd(lambda e: e.reduce_sum(stat[:, 8:9], src, AX.X))
        b.Vd(lambda e: e.tensor_scalar(stat[:, 9:10], stat[:, 8:9], -1.0 / D, None, ALU.mult))
        v0 = b.mark("vector", "sV")
        b.wait("scalar", "sV", v0)
        s0 = b.emit("scalar", lambda e, src=src: e.activation(
            y_t, src, AF.Square, bias=stat[:, 9:10], scale=1.0), inc="sS")
        b.wait("vector", "sS", s0)
        b.Vd(lambda e: e.reduce_sum(stat[:, 10:11], y_t, AX.X))
        b.Vd(lambda e: e.tensor_scalar(stat[:, 10:11], stat[:, 10:11], 1.0 / D, eps,
                                       ALU.mult, ALU.add))
        newton_rsqrt(10, cr)
        b.Vd(lambda e: e.tensor_mul(stat[:, cm:cm + 1], stat[:, 9:10], stat[:, cr:cr + 1]))
        b.Vd(lambda e, src=src, dst=dst: e.tensor_scalar(
            dst, src, stat[:, cr:cr + 1], stat[:, cm:cm + 1], ALU.mult, ALU.add))

    def quant_tile(src, dst_bf16, d1_col):
        b.Vd(lambda e: e.tensor_reduce(stat[:, 13:14], src, AX.X, ALU.max,
                                       apply_absolute_value=True))
        b.Vd(lambda e: e.tensor_scalar_max(stat[:, 13:14], stat[:, 13:14], 1e-5))
        b.Vd(lambda e: e.reciprocal(stat[:, 14:15], stat[:, 13:14]))
        b.Vd(lambda e: e.tensor_scalar(stat[:, 14:15], stat[:, 14:15], 127.0, None, ALU.mult))
        if d1_col is not None:
            b.Vd(lambda e: e.tensor_scalar(d1_all[:, d1_col:d1_col + 1], stat[:, 13:14],
                                           1.0 / 127.0, None, ALU.mult))
        b.Vd(lambda e, src=src: e.tensor_scalar(y_t, src, stat[:, 14:15], ROFF,
                                                ALU.mult, ALU.add))
        b.Vd(lambda e: e.tensor_copy(i32_t, y_t))
        b.Vd(lambda e: e.tensor_sub(f_t, y_t, i32_t))
        b.Vd(lambda e: e.tensor_single_scalar(f_t, f_t, 0.5, ALU.is_ge))
        b.Vd(lambda e, dst=dst_bf16: e.scalar_tensor_tensor(dst, i32_t, -ROFF, f_t,
                                                            ALU.add, ALU.add))

    b.V(lambda e: e.memset(st_v.rearrange("p (pr g w) -> p pr g w", g=2, w=65)[:, :, :, 64:65], 1.0))
    b.emit("vector", lambda e: e.drain())

    xP = [x_t, x_tB]
    xhP = [xh_t, xh_tB]
    xqP = [xq_t, xq_tB]
    fmhP = [xh_fmh, xh_fmhB]
    fmlP = [xh_fml, xh_fmlB]
    rdy = {}
    backg = {}
    xsem = lambda i: "dXa" if i % 2 == 0 else "dXb"
    b.wait("sync", "sV", b.cnt["sV"])
    dxs = {0: b.dma("sync", xP[0], x_h[0:128, :], sem=xsem(0))}

    def front(ti):
        par = ti % 2
        if ti + 1 < T:
            if ti >= 1:
                b.wait("sync", "sV", rdy[ti - 1])
            dxs[ti + 1] = b.dma("sync", xP[1 - par], x_h[(ti + 1) * 128:(ti + 2) * 128, :],
                                sem=xsem(ti + 1))
        b.wait("vector", xsem(ti), dxs[ti])
        ln_tile(xP[par], xn_t, EPS1, 2, 1)
        ln_tile(xn_t, xhP[par], EPS2, 5, 4)
        quant_tile(xn_t, xqP[par], ti)
        rdy[ti] = b.mark("vector", "sV")

    fmrec = {}
    fm_dc = {}
    backg_v = {}
    xq_dg = {}

    def back_tr(ti):
        par = ti % 2
        xh_, xq_ = xhP[par], xqP[par]
        fmh_, fml_ = fmhP[par], fmlP[par]
        b.wait("tensor", "sV", rdy[ti])
        if ti >= 1:
            b.wait("tensor", "sV", backg_v[ti - 1])
        for dc in range(DC):
            b.emit("tensor", lambda e, dc=dc, xh_=xh_: e.transpose(
                PS_C[:, dc * 128:(dc + 1) * 128], xh_[:, dc * 128:(dc + 1) * 128], ident))
        t_xh = b.mark("tensor", "sT")
        if ti >= 1:
            b.wait("tensor", "sV", backg[ti - 1])
        for dc in range(DC):
            b.emit("tensor", lambda e, dc=dc, xq_=xq_: e.transpose(
                PS_D.bitcast(BF16)[:, dc * 128:(dc + 1) * 128],
                xq_[:, dc * 128:(dc + 1) * 128], ident16))
        t_xq = b.mark("tensor", "sT")
        b.wait("vector", "sT", t_xh)
        fm_marks = []
        for dc in range(DC):
            b.V(lambda e, dc=dc, fmh_=fmh_: e.tensor_copy(
                fmh_[:, dc * 128:(dc + 1) * 128], PS_C[:, dc * 128:(dc + 1) * 128]))
            b.emit("vector", lambda e: e.drain())
            b.V(lambda e, dc=dc, fmh_=fmh_, fml_=fml_: e.tensor_sub(
                fml_[:, dc * 128:(dc + 1) * 128], PS_C[:, dc * 128:(dc + 1) * 128],
                fmh_[:, dc * 128:(dc + 1) * 128]))
            fm_marks.append(b.mark("vector", "sV"))
        fm_dc[ti] = fm_marks
        b.wait("vector", "sT", t_xq)
        if ti >= 1:
            b.wait("vector", "dG", backg_dG[ti - 1])
        for dc in range(DC):
            b.V(lambda e, dc=dc: e.tensor_copy(
                xq_fm[:, dc * 128:(dc + 1) * 128],
                PS_D.bitcast(BF16)[:, dc * 128:(dc + 1) * 128]))
        v_fm2 = b.mark("vector", "sV")
        b.wait("gpsimd", "sV", v_fm2)
        b.dma("gpsimd", xq_scr[:, :, ti * 128:(ti + 1) * 128].rearrange("dc p t -> p dc t"),
              xq_fm.rearrange("p (dc t) -> p dc t", dc=DC))
        xq_dg[ti] = b.cnt["dG"]
        fmrec[ti] = v_fm2

    def back_mm(ti):
        par = ti % 2
        fmh_, fml_ = fmhP[par], fmlP[par]
        qkv_ps = {"q": PS_A, "k": PS_B, "v": PS_C}
        marks = {}
        for wi, (wname, WTp) in enumerate((("q", (WqTh, WqTl)), ("k", (WkTh, WkTl)),
                                           ("v", (WvTh, WvTl)))):
            ps = qkv_ps[wname]
            WTh, WTl = WTp
            for og in range(2):
                combos = [(fmh_, WTh), (fmh_, WTl), (fml_, WTh)]
                n_tot = len(combos) * DC
                i = 0
                for dc in range(DC):
                    if wi == 0 and og == 0:
                        b.wait("tensor", "sV", fm_dc[ti][dc])
                    for (xa, wb) in combos:
                        b.emit("tensor", lambda e, og=og, dc=dc, ps=ps, xa=xa, wb=wb,
                               first=(i == 0), last=(i == n_tot - 1): e.matmul(
                            ps[:, og * 512:(og + 1) * 512],
                            xa[:, dc * 128:(dc + 1) * 128],
                            wb[:, dc * D + og * 512: dc * D + og * 512 + 512],
                            start=first, stop=last))
                        i += 1
            marks[wname] = b.mark("tensor", "sT")
        if ti >= 1:
            b.wait("vector", "dG", backg_dG[ti - 1])
        b.wait("vector", "sT", marks["q"])
        b.V(lambda e: e.tensor_copy(q_tm, PS_A[:, 0:1024]))
        b.wait("vector", "sT", marks["k"])
        b.V(lambda e: e.tensor_copy(k_tm, PS_B[:, 0:1024]))
        v_tm = b.mark("vector", "sV")
        b.wait("vector", "sT", marks["v"])
        b.V(lambda e: e.tensor_add(
            st_v.rearrange("p (pr g w) -> p pr g w", g=2, w=65)[:, :, :, 0:64],
            PS_C[:, 0:1024].rearrange("p (pr g i) -> p pr g i", g=2, i=64),
            bv_rep.rearrange("p (pr g i) -> p pr g i", g=2, i=64)))
        v_ev_v = b.mark("vector", "sV")
        backg_v[ti] = v_ev_v
        b.wait("gpsimd", "sV", v_ev_v)
        b.dma("gpsimd", v_scr[:, ti, :, :].rearrange("pr p w -> p pr w"),
              st_v.rearrange("p (pr w) -> p pr w", w=VW0))
        b.wait("tensor", "sV", v_tm)
        for dc in range(DC):
            pst = PS_A if dc < 4 else PS_B
            b.emit("tensor", lambda e, dc=dc, pst=pst: e.transpose(
                pst[:, (dc % 4) * 128:(dc % 4 + 1) * 128], q_tm[:, dc * 128:(dc + 1) * 128], ident))
        t_trq = b.mark("tensor", "sT")
        for dc in range(DC):
            pst = PS_D if dc < 4 else PS_E
            b.emit("tensor", lambda e, dc=dc, pst=pst: e.transpose(
                pst[:, (dc % 4) * 128:(dc % 4 + 1) * 128], k_tm[:, dc * 128:(dc + 1) * 128], ident))
        t_trk = b.mark("tensor", "sT")
        b.wait("vector", "sT", t_trq)
        for oc in range(DC):
            pst = PS_A if oc < 4 else PS_B
            b.V(lambda e, oc=oc, pst=pst: e.tensor_scalar(
                st_q[:, oc * 128:(oc + 1) * 128], pst[:, (oc % 4) * 128:(oc % 4 + 1) * 128],
                bqk_eff[:, oc:oc + 1], None, ALU.add))
        v_evq = b.mark("vector", "sV")
        b.wait("gpsimd", "sV", v_evq)
        b.dma("gpsimd", q_scr[:, :, ti * 128:(ti + 1) * 128].rearrange("oc p t -> p oc t"),
              st_q.rearrange("p (oc t) -> p oc t", oc=DC))
        b.wait("vector", "sT", t_trk)
        for oc in range(DC):
            pst = PS_D if oc < 4 else PS_E
            b.V(lambda e, oc=oc, pst=pst: e.tensor_scalar(
                st_k[:, oc * 128:(oc + 1) * 128], pst[:, (oc % 4) * 128:(oc % 4 + 1) * 128],
                bqk_eff[:, DC + oc:DC + oc + 1], None, ALU.add))
        vvk = b.mark("vector", "sV")
        b.wait("gpsimd", "sV", vvk)
        b.dma("gpsimd", k_scr[:, :, ti * 128:(ti + 1) * 128].rearrange("oc p t -> p oc t"),
              st_k.rearrange("p (oc t) -> p oc t", oc=DC))
        backg[ti] = vvk
        backg_dG[ti] = b.cnt["dG"]

    backg_dG = {}
    for ti in range(T):
        if ti >= 1:
            back_tr(ti - 1)
        front(ti)
        if ti >= 1:
            back_mm(ti - 1)
    back_tr(T - 1)
    back_mm(T - 1)

    vB = b.cnt["sV"]; tB = b.cnt["sT"]; sB = b.cnt["sS"]; gB = b.cnt["dG"]; dB = b.cnt["dS"]
    for eng in ("sync", "tensor", "vector", "scalar", "gpsimd"):
        b.wait(eng, "sV", vB); b.wait(eng, "sT", tB)
        if sB: b.wait(eng, "sS", sB)
        b.wait(eng, "dG", gB); b.wait(eng, "dS", dB)
        b.wait(eng, "dXa", b.cnt["dXa"]); b.wait(eng, "dXb", b.cnt["dXb"])


    if phases <= 1:
        b.wait("gpsimd", "dG", b.cnt["dG"])
        b.wait("gpsimd", "dS", b.cnt["dS"])
        b.wait("gpsimd", "dGa", b.cnt["dGa"])
        b.wait("gpsimd", "dGb", b.cnt["dGb"])
        return finish(nc, es, block, b, ops_only=True)

    NP = NH // 2
    psS = [PS_A, PS_B]
    pv_pairs = [[PS_D, PS_E], [PS_C[:, 0:512], PS_C[:, 512:1024]]]
    rec = {}

    def load_pair(p, buf):
        b.dma("sync", k_pair[buf], k_scr[p, :, :])
        b.dma("sync", q_pair[buf], q_scr[p, :, :])
        nq = max(1, KC // 8)
        for qt in range(0, KC, nq):
            b.dma("sync", v_pairb[buf].rearrange("p (kc w) -> p kc w", w=VW)[:, qt:qt + nq, :],
                  v_scr[p, qt:qt + nq, :, :].rearrange("t p w -> p t w"))
        return b.cnt["dS"]

    def emit_pv(p, qb, kc, buf):
        b.wait("tensor", "sV", rec[("p", p, qb, kc)])
        gqb = p * QB + qb
        pair = pv_pairs[gqb % 2]
        if kc == 0:
            prev = rec.get(("evtr", gqb - 2))
            if prev is not None:
                b.wait("tensor", "sV", prev)
        pb_ = p_buf[(qb * KC + kc) % PLAG]
        last = None
        for hh in range(2):
            last = b.emit("tensor", lambda e, hh=hh, pb_=pb_, kc=kc, buf=buf, pair=pair: e.matmul(
                pair[hh][0:65, 0:512],
                v_pairb[buf][:, kc * 130 + hh * 65: kc * 130 + hh * 65 + 65],
                pb_[:, hh * 512:(hh + 1) * 512],
                start=(kc == 0), stop=(kc == KC - 1)),
                inc=("sT" if hh == 1 else None))
        rec[("pv", p, qb, kc)] = last

    pair_loaded = {0: load_pair(0, 0)}

    ovst_dg = [0]

    def emit_tail(p, qb, buf):
        gqb = p * QB + qb
        pair = pv_pairs[gqb % 2]
        b.wait("vector", "sT", rec[("pv", p, qb, KC - 1)])
        for hh in range(2):
            b.V(lambda e, hh=hh, pair=pair: e.tensor_copy(
                ovt_sb[0:65, hh * 512:(hh + 1) * 512], pair[hh][0:65, 0:512]))
        v_ovt = b.mark("vector", "sV")
        rec[("evac", p, qb)] = v_ovt
        b.wait("tensor", "sV", v_ovt)
        for hh in range(2):
            for j in range(4):
                b.emit("tensor", lambda e, hh=hh, j=j, pair=pair: e.transpose(
                    pair[hh][:, j * 65: j * 65 + 65],
                    ovt_sb[0:65, hh * 512 + j * 128: hh * 512 + j * 128 + 128],
                    ident[0:65, 0:65]))
        t_tr = b.mark("tensor", "sT")
        b.wait("vector", "sT", t_tr)
        for hh in range(2):
            for j in range(4):
                b.V(lambda e, hh=hh, j=j, pair=pair: e.reciprocal(
                    rc_st[:, hh * 4 + j: hh * 4 + j + 1],
                    pair[hh][:, j * 65 + 64: j * 65 + 65]))
        b.emit("vector", lambda e: e.drain())
        b.wait("vector", "dG", ovst_dg[0])
        for j in range(4):
            for hh in range(2):
                b.V(lambda e, hh=hh, j=j, pair=pair: e.tensor_scalar(
                    ov_st[:, j * 128 + hh * 64: j * 128 + (hh + 1) * 64],
                    pair[hh][:, j * 65: j * 65 + 64],
                    rc_st[:, hh * 4 + j: hh * 4 + j + 1], None, ALU.mult))
        vv = b.mark("vector", "sV")
        b.wait("gpsimd", "sV", vv)
        ti0 = qb * (QBS // 128)
        b.dma("gpsimd", ov_scr[ti0:ti0 + 4, :, p * 128:(p + 1) * 128].rearrange("j p d -> p j d"),
              ov_st.rearrange("p (j d) -> p j d", j=4))
        ovst_dg[0] = b.cnt["dG"]
        rec[("evtr", gqb)] = b.cnt["sV"]

    lin = lambda p, qb, kc: (p * QB + qb) * KC + kc
    NKC = QB * KC
    for p in range(NP):
        buf = p % 2
        if p + 1 < NP:
            b.wait("sync", "dS", pair_loaded[p])
            pair_loaded[p + 1] = load_pair(p + 1, 1 - buf)
        b.wait("tensor", "dS", pair_loaded[p])
        b.wait("scalar", "dS", pair_loaded[p])
        for gk in range(NKC):
            qb, kc = divmod(gk, KC)
            st = gk % 2
            prev = lin(p, qb, kc) - 2
            if prev >= 0:
                pp, rem = divmod(prev, QB * KC)
                qq, kk = divmod(rem, KC)
                b.wait("tensor", "sS", rec[("acts", pp, qq, kk)])
            kp, qp = k_pair[buf], q_pair[buf]
            b.emit("tensor", lambda e, kp=kp, qp=qp, st=st, kc=kc, qb=qb: e.matmul(
                psS[st][:, 0:512], kp[0:64, kc * 128:(kc + 1) * 128],
                qp[0:64, qb * QBS: qb * QBS + 512],
                start=True, stop=True, tile_position=(0, 0)))
            rec[("scores", p, qb, kc)] = b.emit(
                "tensor", lambda e, kp=kp, qp=qp, st=st, kc=kc, qb=qb: e.matmul(
                    psS[st][:, 512:1024], kp[64:128, kc * 128:(kc + 1) * 128],
                    qp[64:128, qb * QBS: qb * QBS + 512],
                    start=True, stop=True, tile_position=(64, 0)), inc="sT")
            if gk >= PLAG:
                qb3, kc3 = divmod(gk - PLAG, KC)
                emit_pv(p, qb3, kc3, buf)
                if kc3 == KC - 1:
                    emit_tail(p, qb3, buf)
            b.wait("scalar", "sT", rec[("scores", p, qb, kc)])
            if gk >= PLAG:
                qb3, kc3 = divmod(gk - PLAG, KC)
                b.wait("scalar", "sV", rec[("p", p, qb3, kc3)])
            tb_, eb_ = t_buf[gk % PLAG], e_buf[gk % PLAG]
            b.emit("scalar", lambda e, st=st, tb_=tb_: e.activation(
                tb_, psS[st][:, 0:1024], AF.Tanh, bias=mubias, scale=1.0 / 16))
            rec[("acts", p, qb, kc)] = b.emit(
                "scalar", lambda e, st=st, eb_=eb_: e.activation(
                    eb_, psS[st][:, 0:1024], AF.Exp, bias=0.0, scale=0.125), inc="sS")
            b.wait("vector", "sS", rec[("acts", p, qb, kc)])
            if gk >= PLAG:
                qb3, kc3 = divmod(gk - PLAG, KC)
                b.wait("vector", "sT", rec[("pv", p, qb3, kc3)])
            pb_ = p_buf[gk % PLAG]
            rec[("p", p, qb, kc)] = b.emit(
                "vector", lambda e, tb_=tb_, eb_=eb_, pb_=pb_: e.scalar_tensor_tensor(
                    pb_, tb_, 1.0, eb_, ALU.add, ALU.mult), inc="sV")
        for gk2 in range(max(0, NKC - PLAG), NKC):
            qb3, kc3 = divmod(gk2, KC)
            emit_pv(p, qb3, kc3, buf)
            if kc3 == KC - 1:
                emit_tail(p, qb3, buf)
        b.wait("sync", "sT", rec[("scores", p, QB - 1, KC - 1)])
        b.wait("sync", "sV", rec[("evtr", p * QB + QB - 1)])

    vB = b.cnt["sV"]; tB = b.cnt["sT"]; sB = b.cnt["sS"]; gB = b.cnt["dG"]; dB = b.cnt["dS"]
    for eng in ("sync", "tensor", "vector", "scalar", "gpsimd"):
        b.wait(eng, "sV", vB); b.wait(eng, "sT", tB); b.wait(eng, "sS", sB)
        b.wait(eng, "dG", gB); b.wait(eng, "dS", dB)


    if phases <= 2:
        b.wait("gpsimd", "dG", b.cnt["dG"])
        b.wait("gpsimd", "dS", b.cnt["dS"])
        b.wait("gpsimd", "dGa", b.cnt["dGa"])
        b.wait("gpsimd", "dGb", b.cnt["dGb"])
        return finish(nc, es, block, b, ops_only=True)

    wstP = [w_st, w_stB]
    wsemQ = ["dXa", "dXb"]
    b.wait("sync", "sV", b.cnt["sV"])
    mld = {0: b.dma("sync", wstP[0], w1_h[0:128, :], sem=wsemQ[0])}
    mproc = {}
    for hc in range(HC):
        if hc + 1 < HC:
            if hc >= 1:
                b.wait("sync", "sV", mproc[hc - 1])
            mld[hc + 1] = b.dma("sync", wstP[(hc + 1) % 2],
                                w1_h[(hc + 1) * 128:(hc + 2) * 128, :], sem=wsemQ[(hc + 1) % 2])
        b.wait("vector", wsemQ[hc % 2], mld[hc])
        mproc[hc] = b.V(lambda e, hc=hc: e.tensor_reduce(
            wsum[:, hc:hc + 1], wstP[hc % 2], AX.X, ALU.add, apply_absolute_value=True), inc="sV")
    b.emit("vector", lambda e: e.drain())
    b.Vd(lambda e: e.reduce_sum(stat[:, 8:9], wsum[:, 0:HC], AX.X))
    v_ws = b.mark("vector", "sV")
    b.wait("tensor", "sV", v_ws)
    t_ws = b.emit("tensor", lambda e: e.matmul(PS_D[0:1, 0:1], stat[:, 8:9], stat[:, 15:16],
                                               start=True, stop=True), inc="sT")
    b.wait("vector", "sT", t_ws)
    b.Vd(lambda e: e.tensor_scalar(nstat[:, 0:1], PS_D[0:1, 0:1], 1.0 / (H * D), 1e-5,
                                   ALU.mult, ALU.max))
    b.Vd(lambda e: e.reciprocal(nstat[:, 1:2], nstat[:, 0:1]))
    v_w1s = b.mark("vector", "sV")
    b.wait("tensor", "sV", v_w1s)
    t_sc = b.emit("tensor", lambda e: e.matmul(PS_D[:, 4:6], ones_row, nstat[:, 0:2],
                                               start=True, stop=True), inc="sT")
    b.wait("vector", "sT", t_sc)
    b.Vd(lambda e: e.tensor_copy(stat[:, 6:8], PS_D[:, 4:6]))
    v_w1c = b.mark("vector", "sV")
    b.wait("tensor", "sV", v_w1c)

    def quant_rows_to(dst_bf16, src, scale_col, clip1):
        assert clip1
        b.Vd(lambda e: e.tensor_scalar(wy, src, scale_col, None, ALU.mult))
        b.Vd(lambda e: e.tensor_single_scalar(wf, wy, 0.5, ALU.is_ge))
        b.Vd(lambda e: e.tensor_single_scalar(wy, wy, -1.0, ALU.mult))
        b.Vd(lambda e: e.tensor_single_scalar(wy, wy, 0.5, ALU.is_ge))
        b.Vd(lambda e: e.tensor_sub(dst_bf16, wf, wy))

    wqP = [wq_st, wq_stB]
    b.wait("sync", "sV", b.cnt["sV"])
    qld = {0: b.dma("sync", wstP[0], w1_h[0:128, :], sem=wsemQ[0])}
    vq_m = {}
    vv_m = {}
    trn_m = {}

    def w1_copies(hc):
        b.wait("vector", "sT", trn_m[hc])
        for dc in range(DC):
            b.V(lambda e, dc=dc, hc=hc: e.tensor_copy(
                w1qT[:, dc * H + hc * 128: dc * H + hc * 128 + 128],
                PS_B16[:, dc * 128:(dc + 1) * 128]))
        vv_m[hc] = b.mark("vector", "sV")

    for hc in range(HC):
        if hc + 1 < HC:
            if hc >= 1:
                b.wait("sync", "sV", vq_m[hc - 1])
            qld[hc + 1] = b.dma("sync", wstP[(hc + 1) % 2],
                                w1_h[(hc + 1) * 128:(hc + 2) * 128, :], sem=wsemQ[(hc + 1) % 2])
        b.wait("vector", wsemQ[hc % 2], qld[hc])
        if hc >= 2:
            b.wait("vector", "sT", trn_m[hc - 2])
        quant_rows_to(wqP[hc % 2], wstP[hc % 2], stat[:, 7:8], True)
        vq_m[hc] = b.mark("vector", "sV")
        if hc >= 1:
            w1_copies(hc - 1)
        b.wait("tensor", "sV", vv_m[hc - 1] if hc >= 1 else vq_m[hc])
        if hc == 0:
            pass
        else:
            b.wait("tensor", "sV", vq_m[hc])
        for dc in range(DC):
            b.emit("tensor", lambda e, dc=dc, hc=hc: e.transpose(
                PS_B16[:, dc * 128:(dc + 1) * 128], wqP[hc % 2][:, dc * 128:(dc + 1) * 128],
                ident16))
        trn_m[hc] = b.mark("tensor", "sT")
    w1_copies(HC - 1)
    b.wait("tensor", "sV", vv_m[HC - 1])
    b.Vd(lambda e: e.tensor_scalar(d1_all[:, 0:T], d1_all[:, 0:T], stat[:, 6:7],
                                   None, ALU.mult))
    v_d1 = b.mark("vector", "sV")
    b.wait("tensor", "sV", v_d1)
    t_d1 = b.emit("tensor", lambda e: e.transpose(PS_D[0:T, 0:128], d1_all[:, 0:T], ident),
                  inc="sT")
    b.wait("vector", "sT", t_d1)
    v_d1b = b.Vd(lambda e: e.tensor_copy(trsp[0:T, 0:128], PS_D[0:T, 0:128]), inc="sV")
    b.wait("gpsimd", "sV", v_d1b)
    b.dma("gpsimd", flat_scr[0:T, :], trsp[0:T, 0:128])
    b.wait("gpsimd", "dG", b.cnt["dG"])
    b.dma("gpsimd", row_Sa[:, 0:S], flat_scr.rearrange("a b -> (a b)").unsqueeze(0)[:, 0:S])
    g_row = b.cnt["dG"]
    b.wait("tensor", "dG", g_row)
    b.wait("tensor", "sV", v_d1b)
    last = None
    for hh in range(S // 512):
        last = b.emit("tensor", lambda e, hh=hh: e.matmul(
            PS_A[:, (hh % 2) * 512:(hh % 2) * 512 + 512], ones_row,
            row_Sa[:, hh * 512: hh * 512 + 512], start=True, stop=True),
            inc="sT")
        b.wait("vector", "sT", last)
        vv = b.V(lambda e, hh=hh: e.tensor_copy(
            d1_rep[:, hh * 512:(hh + 1) * 512], PS_A[:, (hh % 2) * 512:(hh % 2) * 512 + 512]),
            inc="sV")
        b.wait("tensor", "sV", vv)
    v_p3prep = b.mark("vector", "sV")
    b.wait("tensor", "sV", v_p3prep)
    b.wait("scalar", "sV", v_p3prep)
    b.wait("sync", "sV", v_p3prep)

    b.emit("scalar", lambda e: e.preload_act_table(AF.Silu))
    if dbg:
        dbg_w1qT = nc.dram_tensor("dbg_w1qT", [128, DC * H], BF16, kind="ExternalOutput").ap()
        dbg_d1rep = nc.dram_tensor("dbg_d1rep", [128, S], F32, kind="ExternalOutput").ap()
        dbg_hblk = nc.dram_tensor("dbg_hblk", [128, HC * (TAS + 2)], F32, kind="ExternalOutput").ap()
        b.wait("gpsimd", "sV", b.cnt["sV"])
        b.dma("gpsimd", dbg_w1qT, w1qT)
        b.dma("gpsimd", dbg_d1rep, d1_rep)
        b.wait("sync", "dG", b.cnt["dG"])

    spill_rec = {}
    evac_rec = {}
    silu_rec = {}
    red_rec = {}
    trn_rec = {}
    cv1_rec = {}
    d_xq = {}
    blk_tf = {}
    xq_blkP = [xq_blk, xq_blkB]
    cbufs3 = [cbufs[0], cbufs[1], cbuf2]
    PSH3 = [PS_A, PS_B, PS_C]

    def xq_load(tb):
        lo = tb * TAS
        has_l = tb > 0
        has_r = tb < TA - 1
        src_lo = lo - 1 if has_l else lo
        src_hi = lo + TAS + 1 if has_r else lo + TAS
        dst_lo = 0 if has_l else 1
        xqb = xq_blkP[tb % 2]
        if tb >= 2:
            b.wait("sync", "sT", blk_tf[tb - 2])
        xsm = "dXa" if tb % 2 == 0 else "dXb"
        b.dma("sync", xqb.rearrange("p (dc t) -> p dc t", dc=DC)[:, :, dst_lo:dst_lo + (src_hi - src_lo)],
              xq_scr[:, :, src_lo:src_hi].rearrange("dc p t -> p dc t"), sem=xsm)
        d_xq[tb] = (xsm, b.cnt[xsm])

    def emit_transposes(key):
        pst = PS_E if key % 2 == 0 else PS_D
        b.wait("tensor", "sS", silu_rec[key])
        if key >= 2:
            b.wait("tensor", "sV", red_rec[key - 2])
        rbuf = r_sb[key % 2]
        for j in range(NSA):
            b.emit("tensor", lambda e, j=j, rbuf=rbuf, pst=pst: e.transpose(
                pst[:, j * 128:(j + 1) * 128], rbuf[:, j * 128:(j + 1) * 128], ident))
        trn_rec[key] = b.mark("tensor", "sT")

    def emit_reduces(key):
        pst = PS_E if key % 2 == 0 else PS_D
        hc = key % HC
        b.wait("vector", "sT", trn_rec[key])
        for j in range(NSA):
            b.V(lambda e, j=j, hc=hc, pst=pst: e.tensor_reduce(
                redb[:, hc * NSA + j: hc * NSA + j + 1], pst[:, j * 128:(j + 1) * 128],
                AX.X, ALU.max, apply_absolute_value=True))
        red_rec[key] = b.mark("vector", "sV")
        if hc == HC - 1:
            tbp = key // HC
            for j in range(NSA):
                b.V(lambda e, j=j, tbp=tbp: e.tensor_reduce(
                    s2sc[:, tbp * NSA + j: tbp * NSA + j + 1],
                    redb.rearrange("p (hc j) -> p hc j", j=NSA)[:, :, j], AX.X, ALU.max))
            b.emit("vector", lambda e: e.drain())

    xq_load(0)
    for tb in range(TA):
        lo = tb * TAS
        has_l = tb > 0
        has_r = tb < TA - 1
        xqb = xq_blkP[tb % 2]
        if tb + 1 < TA:
            xq_load(tb + 1)
        if not has_l or not has_r:
            b.wait("vector", d_xq[tb][0], d_xq[tb][1])
            if not has_l:
                b.V(lambda e, xqb=xqb: e.memset(
                    xqb.rearrange("p (dc t) -> p dc t", dc=DC)[:, :, 0:1], 0.0))
            if not has_r:
                b.V(lambda e, xqb=xqb: e.memset(
                    xqb.rearrange("p (dc t) -> p dc t", dc=DC)[:, :, HB - 1:HB], 0.0))
            vz = b.mark("vector", "sV")
            b.wait("tensor", "sV", vz)
        b.wait("tensor", d_xq[tb][0], d_xq[tb][1])

        for hc in range(HC):
            key = tb * HC + hc
            psh = PSH3[key % 3]
            if key >= 3:
                b.wait("tensor", "sV", evac_rec[key - 3])
            for dc in range(DC):
                b.emit("tensor", lambda e, hc=hc, dc=dc, psh=psh, xqb=xqb: e.matmul(
                    psh[:, 0:HB],
                    w1qT[:, dc * H + hc * 128: dc * H + hc * 128 + 128],
                    xqb[:, dc * HB: dc * HB + HB],
                    start=(dc == 0), stop=(dc == DC - 1)))
            t_h = b.mark("tensor", "sT")
            if key >= 1:
                emit_transposes(key - 1)
            b.wait("vector", "sT", t_h)
            off = hc * HB
            c0 = 1 if not has_l else 0
            c1 = HB - 1 if not has_r else HB
            d0 = lo - 1 + c0
            b.V(lambda e, off=off, c0=c0, c1=c1, d0=d0, psh=psh: e.tensor_mul(
                h_blk[:, off + c0: off + c1], psh[:, c0:c1],
                d1_rep[:, d0: d0 + (c1 - c0)]))
            if not has_l:
                b.V(lambda e, off=off: e.memset(h_blk[:, off: off + 1], 0.0))
            if not has_r:
                b.V(lambda e, off=off: e.memset(h_blk[:, off + HB - 1: off + HB], 0.0))
            v_h = b.mark("vector", "sV")
            evac_rec[key] = v_h
            cbuf = cbufs3[key % 3]
            b.wait("scalar", "sV", v_h)
            cv1_rec[key] = b.emit("scalar", lambda e, off=off, hc=hc, cbuf=cbuf: e.activation(
                cbuf, h_blk[:, off: off + TAS], AF.Copy,
                bias=0.0, scale=convw[:, hc * 3: hc * 3 + 1]), inc="sS")
            b.wait("vector", "sS", cv1_rec[key])
            b.Vd(lambda e, off=off, hc=hc, cbuf=cbuf: e.scalar_tensor_tensor(
                cbuf, h_blk[:, off + 1: off + 1 + TAS],
                convw[:, hc * 3 + 1: hc * 3 + 2], cbuf, ALU.mult, ALU.add))
            b.Vd(lambda e, off=off, hc=hc, cbuf=cbuf: e.scalar_tensor_tensor(
                cbuf, h_blk[:, off + 2: off + 2 + TAS],
                convw[:, hc * 3 + 2: hc * 3 + 3], cbuf, ALU.mult, ALU.add))
            v_c = b.mark("vector", "sV")
            if key >= 1:
                emit_reduces(key - 1)
            b.wait("scalar", "sV", v_c)
            if key - 2 >= 0:
                b.wait("scalar", "dGa" if key % 2 == 0 else "dGb", spill_rec[key - 2])
            rbuf = r_sb[key % 2]
            silu_rec[key] = b.emit("scalar", lambda e, rbuf=rbuf, cbuf=cbuf, hc=hc: e.activation(
                rbuf, cbuf, AF.Silu, bias=convb[:, hc:hc + 1], scale=1.0), inc="sS")
            b.wait("gpsimd", "sS", silu_rec[key])
            spill_rec[key] = b.dma("gpsimd", r_scr[hc, :, lo:lo + TAS], rbuf,
                                   sem=("dGa" if key % 2 == 0 else "dGb"))
        blk_tf[tb] = b.cnt["sT"]
        if tb == TA - 1:
            kend = tb * HC + HC - 1
            emit_transposes(kend)
            emit_reduces(kend)
    b.wait("gpsimd", "dG", b.cnt["dG"])
    b.wait("gpsimd", "dGa", b.cnt["dGa"])
    b.wait("gpsimd", "dGb", b.cnt["dGb"])

    vB = b.cnt["sV"]; tB = b.cnt["sT"]; sB = b.cnt["sS"]; gB = b.cnt["dG"]; dB = b.cnt["dS"]
    gA = b.cnt["dGa"]; gBb = b.cnt["dGb"]
    xA = b.cnt["dXa"]; xBb = b.cnt["dXb"]
    for eng in ("sync", "tensor", "vector", "scalar", "gpsimd"):
        b.wait(eng, "sV", vB); b.wait(eng, "sT", tB); b.wait(eng, "sS", sB)
        b.wait(eng, "dG", gB); b.wait(eng, "dS", dB)
        b.wait(eng, "dGa", gA); b.wait(eng, "dGb", gBb)
        b.wait(eng, "dXa", xA); b.wait(eng, "dXb", xBb)


    if phases <= 3:
        b.wait("gpsimd", "dG", b.cnt["dG"])
        b.wait("gpsimd", "dS", b.cnt["dS"])
        b.wait("gpsimd", "dGa", b.cnt["dGa"])
        b.wait("gpsimd", "dGb", b.cnt["dGb"])
        return finish(nc, es, block, b, ops_only=True)

    w2P = [w2st, w2stB]
    b.wait("sync", "sV", b.cnt["sV"])
    m2ld = {0: b.dma("sync", w2P[0], w2_h[0:128, :], sem="dXa")}
    m2proc = {}
    for dc in range(DC):
        if dc + 1 < DC:
            if dc >= 1:
                b.wait("sync", "sV", m2proc[dc - 1])
            m2ld[dc + 1] = b.dma("sync", w2P[(dc + 1) % 2],
                                 w2_h[(dc + 1) * 128:(dc + 2) * 128, :],
                                 sem=("dXa" if (dc + 1) % 2 == 0 else "dXb"))
        b.wait("vector", "dXa" if dc % 2 == 0 else "dXb", m2ld[dc])
        m2proc[dc] = b.V(lambda e, dc=dc: e.tensor_reduce(
            wsum[:, dc:dc + 1], w2P[dc % 2], AX.X, ALU.add, apply_absolute_value=True), inc="sV")
    b.emit("vector", lambda e: e.drain())
    b.Vd(lambda e: e.reduce_sum(stat[:, 8:9], wsum[:, 0:DC], AX.X))
    v_ws2 = b.mark("vector", "sV")
    b.wait("tensor", "sV", v_ws2)
    t_ws2 = b.emit("tensor", lambda e: e.matmul(PS_D[0:1, 2:3], stat[:, 8:9], stat[:, 15:16],
                                                start=True, stop=True), inc="sT")
    b.wait("vector", "sT", t_ws2)
    b.Vd(lambda e: e.tensor_scalar(nstat[:, 2:3], PS_D[0:1, 2:3], 1.0 / (H * D), 1e-5,
                                   ALU.mult, ALU.max))
    b.Vd(lambda e: e.reciprocal(nstat[:, 3:4], nstat[:, 2:3]))
    v_w2s = b.mark("vector", "sV")
    b.wait("tensor", "sV", v_w2s)
    t_sc2 = b.emit("tensor", lambda e: e.matmul(PS_D[:, 6:8], ones_row, nstat[:, 2:4],
                                                start=True, stop=True), inc="sT")
    b.wait("vector", "sT", t_sc2)
    b.Vd(lambda e: e.tensor_copy(stat[:, 6:8], PS_D[:, 6:8]))
    b.Vd(lambda e: e.tensor_scalar_max(s2sc[:, 0:T], s2sc[:, 0:T], 1e-5))
    b.Vd(lambda e: e.tensor_scalar(d2_all[:, 0:T], s2sc[:, 0:T], 1.0 / 127.0,
                                   None, ALU.mult))
    b.Vd(lambda e: e.tensor_scalar(d2_all[:, 0:T], d2_all[:, 0:T], stat[:, 6:7],
                                   None, ALU.mult))
    b.Vd(lambda e: e.reciprocal(s2sc[:, 0:T], s2sc[:, 0:T]))
    b.Vd(lambda e: e.tensor_scalar(s2sc[:, 0:T], s2sc[:, 0:T], 127.0, None, ALU.mult))
    v_d2 = b.mark("vector", "sV")
    b.wait("tensor", "sV", v_d2)

    wqP = [wq_st, wq_stB]
    b.wait("sync", "sV", b.cnt["sV"])
    q2ld = {0: b.dma("sync", w2P[0], w2_h[0:128, :], sem="dXa")}
    vq2 = {}
    vv2 = {}
    trn2 = {}
    N2P = DC * 4

    def w2_copies(i):
        dc, part = divmod(i, 4)
        b.wait("vector", "sT", trn2[i])
        for t8 in range(8):
            hcg = part * 8 + t8
            b.V(lambda e, t8=t8, hcg=hcg, dc=dc: e.tensor_copy(
                w2qT[:, hcg * D + dc * 128: hcg * D + dc * 128 + 128],
                PS_B16[:, t8 * 128:(t8 + 1) * 128]))
        vv2[i] = b.mark("vector", "sV")

    for i in range(N2P):
        dc, part = divmod(i, 4)
        if part == 0:
            if dc + 1 < DC:
                if dc >= 1:
                    b.wait("sync", "sV", vq2[dc * 4 - 1])
                q2ld[dc + 1] = b.dma("sync", w2P[(dc + 1) % 2],
                                     w2_h[(dc + 1) * 128:(dc + 2) * 128, :],
                                     sem=("dXa" if (dc + 1) % 2 == 0 else "dXb"))
            b.wait("vector", "dXa" if dc % 2 == 0 else "dXb", q2ld[dc])
        sl = slice(part * D, (part + 1) * D)
        if i >= 2:
            b.wait("vector", "sT", trn2[i - 2])
        quant_rows_to(wqP[i % 2], w2P[dc % 2][:, sl], stat[:, 7:8], True)
        vq2[i] = b.mark("vector", "sV")
        if i >= 1:
            w2_copies(i - 1)
        b.wait("tensor", "sV", vv2[i - 1] if i >= 1 else vq2[i])
        if i >= 1:
            b.wait("tensor", "sV", vq2[i])
        for t8 in range(8):
            b.emit("tensor", lambda e, t8=t8, i=i: e.transpose(
                PS_B16[:, t8 * 128:(t8 + 1) * 128], wqP[i % 2][:, t8 * 128:(t8 + 1) * 128],
                ident16))
        trn2[i] = b.mark("tensor", "sT")
    w2_copies(N2P - 1)
    b.wait("tensor", "sV", vv2[N2P - 1])
    b.wait("sync", "sV", vv2[N2P - 1])
    wo_ld = {0: b.dma("sync", w_st, Wo_h[0:128, :], sem="dXa")}
    wo_vv = {}
    for oc in range(DC):
        wop = [w_st, w_stB][oc % 2]
        if oc + 1 < DC:
            if oc >= 1:
                b.wait("sync", "sV", wo_vv[oc - 1])
            wo_ld[oc + 1] = b.dma("sync", [w_st, w_stB][(oc + 1) % 2],
                                  Wo_h[(oc + 1) * 128:(oc + 2) * 128, :],
                                  sem=("dXa" if (oc + 1) % 2 == 0 else "dXb"))
        b.wait("tensor", "dXa" if oc % 2 == 0 else "dXb", wo_ld[oc])
        if oc >= 1:
            b.wait("tensor", "sV", wo_vv[oc - 1])
        for dc in range(DC):
            b.emit("tensor", lambda e, dc=dc, wop=wop: e.transpose(
                PS_C[:, dc * 128:(dc + 1) * 128], wop[:, dc * 128:(dc + 1) * 128], ident))
        t_tr = b.mark("tensor", "sT")
        b.wait("vector", "dXa" if oc % 2 == 0 else "dXb", wo_ld[oc])
        b.wait("vector", "sT", t_tr)
        for dc in range(DC):
            b.V(lambda e, dc=dc, oc=oc: e.tensor_copy(
                WoT[:, dc * D + oc * 128: dc * D + oc * 128 + 128],
                PS_C[:, dc * 128:(dc + 1) * 128]))
        wo_vv[oc] = b.mark("vector", "sV")
    b.wait("tensor", "sV", wo_vv[DC - 1])
    v_pb = b.cnt["sV"]
    for eng in ("sync", "tensor", "vector", "scalar", "gpsimd"):
        b.wait(eng, "sV", v_pb)
    vov_prev = None
    prev_qrec = None
    ov_tP = [ov_t, ov_tB]
    x2_tP = [x2_t, x2_tB]
    acc_tP = [acc_t, acc_tB]
    dld = {}
    asm_rec = {}
    qdone = {}
    tf_last = {}
    out_dg = {}
    rsem = lambda i: ("dRa", "dRb", "dRc", "dRd")[i % 4]

    def build_slab(g):
        dst = s2_repP[g % 2]
        b.Vd(lambda e, g=g: e.tensor_copy(stat[:, 8:12], s2sc[:, g * 4:g * 4 + 4]))
        v_s2c = b.mark("vector", "sV")
        b.wait("tensor", "sV", v_s2c)
        t_s2 = b.emit("tensor", lambda e: e.transpose(PS_D[0:4, 0:128], stat[:, 8:12], ident),
                      inc="sT")
        b.wait("vector", "sT", t_s2)
        v_s2r = b.Vd(lambda e: e.tensor_copy(trsp[0:4, 0:128], PS_D[0:4, 0:128]), inc="sV")
        b.wait("gpsimd", "sV", v_s2r)
        b.dma("gpsimd", flat_scr[0:4, :], trsp[0:4, 0:128])
        b.wait("gpsimd", "dG", b.cnt["dG"])
        b.dma("gpsimd", row_Sb[:, 0:2 * QBS2],
              flat_scr.rearrange("a b -> (a b)").unsqueeze(0)[:, 0:2 * QBS2])
        gr = b.cnt["dG"]
        b.wait("tensor", "dG", gr)
        b.wait("tensor", "sV", v_s2r)
        t_s2b = b.emit("tensor", lambda e, dst=dst: e.matmul(
            PS_C[:, 0:2 * QBS2], ones_row, row_Sb[:, 0:2 * QBS2],
            start=True, stop=True), inc="sT")
        b.wait("vector", "sT", t_s2b)
        v_s2 = b.Vd(lambda e, dst=dst: e.tensor_copy(dst, PS_C[:, 0:2 * QBS2]), inc="sV")
        slab_rdy[g] = v_s2
        b.wait("tensor", "sV", v_s2)

    def quant_half(qb, lohalf):
        nonlocal prev_qrec
        lo = qb * QBS2
        slab = s2_repP[(qb // 2) % 2][:, (qb % 2) * QBS2:(qb % 2 + 1) * QBS2]
        xq2b = xq2P_[qb % 2]
        h0, h1 = lohalf * (HC // 2), (lohalf + 1) * (HC // 2)
        if lohalf == 0:
            if qb >= 2:
                b.wait("vector", "sT", tf_last[qb - 2])
            for k in (0, 1, 2):
                if prev_qrec is not None:
                    b.wait("sync", "sV", prev_qrec[k])
                qh_rec[k] = b.dma("sync", r_ld[k], r_scr[k, :, lo:lo + QBS2], sem=rsem(k))
        for hcq in range(h0, h1):
            rb = r_ld[hcq % 4]
            nxt = hcq + 3
            if nxt < HC:
                if hcq - 1 >= 0:
                    b.wait("sync", "sV", qq_rec[hcq - 1])
                elif prev_qrec is not None:
                    b.wait("sync", "sV", prev_qrec[3])
                qh_rec[nxt] = b.dma("sync", r_ld[nxt % 4],
                                    r_scr[nxt, :, lo:lo + QBS2], sem=rsem(nxt))
            b.wait("vector", rsem(hcq), qh_rec[hcq])

            def chain(e, rb=rb, slab=slab, hcq=hcq, xq2b=xq2b):
                e.tensor_mul(hy, rb, slab)
                e.drain()
                e.tensor_single_scalar(hi32, hy, ROFF, ALU.add)
                e.drain()
                e.scalar_tensor_tensor(hf, hy, ROFF, hi32, ALU.add, ALU.subtract)
                e.drain()
                e.tensor_single_scalar(hf, hf, 0.5, ALU.is_ge)
                e.drain()
                e.scalar_tensor_tensor(
                    xq2b[:, hcq * QBS2:(hcq + 1) * QBS2], hi32, -ROFF, hf, ALU.add, ALU.add)
            b.emit("vector", chain)
            qq_rec[hcq] = b.mark("vector", "sV")
        if lohalf == 1:
            prev_qrec = tuple(qq_rec[HC - 4 + k] for k in range(4))
            qdone[qb] = qq_rec[HC - 1]

    qh_rec = {}
    qq_rec = {}
    slab_rdy = {}
    g_acc_prev = [None]
    build_slab(0)
    quant_half(0, 0)
    quant_half(0, 1)
    b.dma("scalar", ov_t, ov_scr[0, :, :], sem="dA")
    b.dma("scalar", x2_t, x_h[0:128, :], sem="dA")
    dld[0] = ("dA", b.cnt["dA"])

    for tb in range(TB2):
        if tb + 1 < TB2 and (tb + 1) % 2 == 0:
            build_slab((tb + 1) // 2)
        xq2b = xq2P_[tb % 2]
        for j in range(NS2):
            ti = tb * NS2 + j
            if tb + 1 < TB2:
                quant_half(tb + 1, j)
            nti = ti + 1
            if nti < T:
                prev_v = asm_rec.get(nti - 2)
                if prev_v is not None:
                    b.wait("scalar", "sV", prev_v)
                sA = "dA" if nti % 2 == 0 else "dAb"
                b.dma("scalar", ov_tP[nti % 2], ov_scr[nti, :, :], sem=sA)
                b.dma("scalar", x2_tP[nti % 2], x_h[nti * 128:(nti + 1) * 128, :], sem=sA)
                dld[nti] = (sA, b.cnt[sA])
            ovb, xb = ov_tP[ti % 2], x2_tP[ti % 2]
            acc_ = acc_tP[ti % 2]
            b.wait("tensor", dld[ti][0], dld[ti][1])
            if vov_prev is not None:
                b.wait("tensor", "sS", vov_prev)
            for dc in range(DC):
                b.emit("tensor", lambda e, dc=dc, ovb=ovb: e.transpose(
                    PS_C[:, dc * 128:(dc + 1) * 128], ovb[:, dc * 128:(dc + 1) * 128], ident))
            t_ov = b.mark("tensor", "sT")
            psf = PS_A
            if j == 0:
                b.wait("tensor", "sV", qdone[tb])
            if ti >= 1:
                b.wait("tensor", "sV", asm_rec[ti - 1])
            for half in range(2):
                for hcq in range(HC):
                    b.emit("tensor", lambda e, half=half, hcq=hcq, j=j, psf=psf,
                           xq2b=xq2b: e.matmul(
                        psf[:, half * 512:(half + 1) * 512],
                        xq2b[:, hcq * QBS2 + j * 128: hcq * QBS2 + j * 128 + 128],
                        w2qT[:, hcq * D + half * 512: hcq * D + half * 512 + 512],
                        start=(hcq == 0), stop=(hcq == HC - 1)))
            t_f = b.mark("tensor", "sT")
            if j == NS2 - 1:
                tf_last[tb] = t_f
            b.wait("scalar", "sT", t_ov)
            b.emit("scalar", lambda e: e.copy(ovfm[:, 0:1024], PS_C[:, 0:1024]))
            v_ov = b.mark("scalar", "sS")
            vov_prev = v_ov
            b.wait("tensor", "sS", v_ov)
            for half in range(2):
                for dc in range(DC):
                    b.emit("tensor", lambda e, half=half, dc=dc: e.matmul(
                        PS_B[:, half * 512:(half + 1) * 512],
                        ovfm[:, dc * 128:(dc + 1) * 128],
                        WoT[:, dc * D + half * 512: dc * D + half * 512 + 512],
                        start=(dc == 0), stop=(dc == DC - 1)))
            t_o = b.mark("tensor", "sT")
            b.wait("scalar", "sT", t_f)
            if g_acc_prev[0] is not None:
                b.wait("scalar", "sG", g_acc_prev[0])
            b.emit("scalar", lambda e, ti=ti, psf=psf: e.mul(
                ffn_t[:, 0:1024], psf[:, 0:1024], d2_all[:, ti:ti + 1]))
            v_ffn = b.mark("scalar", "sS")
            b.wait("gpsimd", dld[ti][0], dld[ti][1])
            if ti >= 2:
                b.wait("gpsimd", "dGa" if ti % 2 == 0 else "dGb", out_dg[ti - 2])
            b.emit("gpsimd", lambda e, xb=xb, acc_=acc_: e.tensor_add(acc_, xb, bo_rep))
            b.emit("gpsimd", lambda e: e.drain())
            b.wait("gpsimd", "sS", v_ffn)
            b.emit("gpsimd", lambda e, acc_=acc_: e.tensor_add(acc_, acc_, ffn_t))
            g_acc = b.mark("gpsimd", "sG")
            g_acc_prev[0] = g_acc
            b.wait("vector", "sG", g_acc)
            b.wait("vector", "sT", t_o)
            for half in range(2):
                b.V(lambda e, half=half, acc_=acc_: e.scalar_tensor_tensor(
                    acc_[:, half * 512:(half + 1) * 512], PS_B[:, half * 512:(half + 1) * 512],
                    1.0, acc_[:, half * 512:(half + 1) * 512], ALU.mult, ALU.add))
            v_out = b.mark("vector", "sV")
            asm_rec[ti] = v_out
            b.wait("gpsimd", "sV", v_out)
            b.dma("gpsimd", out_h[ti * 128:(ti + 1) * 128, :], acc_,
                  sem=("dGa" if ti % 2 == 0 else "dGb"))
            out_dg[ti] = b.cnt["dGa" if ti % 2 == 0 else "dGb"]
    b.wait("vector", "dG", b.cnt["dG"])

    b.wait("gpsimd", "dG", b.cnt["dG"])
    b.wait("gpsimd", "dGa", b.cnt["dGa"])
    b.wait("gpsimd", "dGb", b.cnt["dGb"])
    b.wait("gpsimd", "dS", b.cnt["dS"])
    if b.cnt["dA"]:
        b.wait("gpsimd", "dA", b.cnt["dA"])
    if b.cnt["dAb"]:
        b.wait("gpsimd", "dAb", b.cnt["dAb"])
    for _rs in ("dRa", "dRb", "dRc", "dRd"):
        if b.cnt[_rs]:
            b.wait("gpsimd", _rs, b.cnt[_rs])
    return finish(nc, es, block, b)


def finish(nc, es, block, b, ops_only=False):
    ops = b.ops

    @block.sync
    def _(sync):
        for f in ops["sync"]:
            f(sync)

    @block.tensor
    def _(tensor):
        for f in ops["tensor"]:
            f(tensor)

    @block.vector
    def _(vector):
        for f in ops["vector"]:
            f(vector)

    @block.scalar
    def _(scalar):
        for f in ops["scalar"]:
            f(scalar)

    @block.gpsimd
    def _(gpsimd):
        for f in ops["gpsimd"]:
            f(gpsimd)

    es.close()
    return nc


def make_in_map(x_b, ins):
    f = np.float32
    return {
        "x": np.ascontiguousarray(x_b, dtype=f),
        "Wq": np.asarray(ins["L0_attn_Wq"], f), "Wk": np.asarray(ins["L0_attn_Wk"], f),
        "Wv": np.asarray(ins["L0_attn_Wv"], f), "Wo": np.asarray(ins["L0_attn_Wo"], f),
        "bq": np.asarray(ins["L0_attn_bq"], f).reshape(1, D),
        "bk": np.asarray(ins["L0_attn_bk"], f).reshape(1, D),
        "bv": np.asarray(ins["L0_attn_bv"], f).reshape(1, D),
        "bo": np.asarray(ins["L0_attn_bo"], f).reshape(1, D),
        "gam": np.asarray(ins["L0_attn_ln_gamma"], f).reshape(1, D),
        "bet": np.asarray(ins["L0_attn_ln_beta"], f).reshape(1, D),
        "mu": np.asarray(ins["L0_attn_mu"], f).reshape(1, 1),
        "w1": np.asarray(ins["L0_ffn_w1"], f),
        "w2": np.asarray(ins["L0_ffn_w2"], f),
        "cw": np.asarray(ins["L0_ffn_conv_w"], f).reshape(H, 3),
        "cb": np.asarray(ins["L0_ffn_conv_b"], f).reshape(1, H),
        "ident_in": np.eye(128, dtype=f),
    }


from concourse.bass_utils import run_bass_kernel_spmd

S_FULL = 4096
N_CORES = 8


def kernel(**inputs):
    x = np.asarray(inputs["L0_attn_sequence"], np.float32)
    nc = build_core(S_FULL)
    in_maps = [make_in_map(x[b], inputs) for b in range(N_CORES)]
    res = run_bass_kernel_spmd(nc, in_maps, core_ids=list(range(N_CORES)))
    return np.stack([np.asarray(r["out"], np.float32) for r in res.results], axis=0)


# version 37
# speedup vs baseline: 1.1138x; 1.0420x over previous
import contextlib
import numpy as np
import concourse.bass as bass
import concourse.mybir as mybir

F32 = mybir.dt.float32
BF16 = mybir.dt.bfloat16
I32 = mybir.dt.int32
AF = mybir.ActivationFunctionType
ALU = mybir.AluOpType
AX = mybir.AxisListType

D = 1024
H = 4096
NH = 16
DC = D // 128
HC = H // 128
EPS1 = 1e-5
EPS2 = 1e-6
ROFF = 256.0


class Builder:
    def __init__(self, nc, es):
        self.nc = nc
        self.ops = {"sync": [], "tensor": [], "vector": [], "scalar": [], "gpsimd": []}
        self.sems = {}
        self.cnt = {}
        for name in ("dS", "dG", "dGa", "dGb", "dRa", "dRb", "dRc", "dRd", "dXa", "dXb", "dA", "dAb", "sT", "sV", "sS", "sG"):
            self.sems[name] = es.enter_context(nc.semaphore(name))
            self.cnt[name] = 0

    def emit(self, eng, make, inc=None, inc_by=1):
        if inc is not None:
            self.cnt[inc] += inc_by
        sem = self.sems[inc] if inc is not None else None

        def run(e, make=make, sem=sem, inc_by=inc_by):
            ins = make(e)
            if sem is not None:
                ins.then_inc(sem, inc_by)
        self.ops[eng].append(run)
        return self.cnt[inc] if inc is not None else None

    def mark(self, eng, inc):
        return self.emit(eng, lambda e: e.drain(), inc=inc)

    def wait(self, eng, sem, val):
        assert val is not None and val >= 0
        s = self.sems[sem]
        self.ops[eng].append(lambda e, s=s, val=val: e.wait_ge(s, val))

    def dma(self, eng, out, in_, slow=False, sem=None):
        if sem is None:
            sem = {"sync": "dS", "gpsimd": "dG", "scalar": "dA"}[eng]
        return self.emit(eng, lambda e, o=out, i=in_, s=slow: e.dma_start(
            out=o, in_=i, allow_slow_non_contiguous=s), inc=sem, inc_by=16)

    def V(self, make, inc=None):
        return self.emit("vector", make, inc=inc)

    def Vd(self, make, inc=None):
        r = self.emit("vector", make, inc=inc)
        self.emit("vector", lambda e: e.drain())
        return r


VW0 = 130

def build_core(S=4096, phases=4, dbg=False):
    T = S // 128
    KC = S // 128
    QBS = min(512, S)
    QB = S // QBS
    TAS = min(256, S)
    TA = S // TAS
    NSA = TAS // 128
    TBS = min(512, S)
    TB = S // TBS
    NSUB = TBS // 128
    QBS2 = min(256, S)
    TB2 = S // QBS2
    NS2 = QBS2 // 128

    es = contextlib.ExitStack()
    nc = bass.Bass("TRN2", target_bir_lowering=False)
    b = Builder(nc, es)

    def din(name, shape, dt=F32):
        return nc.dram_tensor(name, shape, dt, kind="ExternalInput").ap()
    x_h = din("x", [S, D])
    Wq_h = din("Wq", [D, D]); Wk_h = din("Wk", [D, D]); Wv_h = din("Wv", [D, D]); Wo_h = din("Wo", [D, D])
    bq_h = din("bq", [1, D]); bk_h = din("bk", [1, D]); bv_h = din("bv", [1, D]); bo_h = din("bo", [1, D])
    gam_h = din("gam", [1, D]); bet_h = din("bet", [1, D])
    mu_h = din("mu", [1, 1])
    w1_h = din("w1", [H, D]); w2_h = din("w2", [D, H])
    cw_h = din("cw", [H, 3]); cb_h = din("cb", [1, H])
    id_h = din("ident_in", [128, 128])
    out_h = nc.dram_tensor("out", [S, D], F32, kind="ExternalOutput").ap()

    KND = "ExternalOutput" if dbg else "Internal"
    q_scr = nc.dram_tensor("q_scr", [DC, 128, S], F32, kind=KND).ap()
    k_scr = nc.dram_tensor("k_scr", [DC, 128, S], F32, kind=KND).ap()
    v_scr = nc.dram_tensor("v_scr", [DC, T, 128, VW0], F32, kind=KND).ap()
    ov_scr = nc.dram_tensor("ov_scr", [T, 128, D], F32, kind=KND).ap()
    xq_scr = nc.dram_tensor("xq_scr", [DC, 128, S], BF16, kind=KND).ap()
    r_scr = nc.dram_tensor("r_scr", [HC, 128, S], F32, kind=KND).ap()
    flat_scr = nc.dram_tensor("flat_scr", [128, 128], F32, kind="Internal").ap()

    A = nc.alloc_sbuf_tensor
    ident = A("ident", [128, 128], F32).ap()
    ident16 = A("ident16", [128, 128], BF16).ap()
    ones_row = A("ones_row", [1, 128], F32).ap()
    mu_sb = A("mu_sb", [1, 1], F32).ap()
    mubias = A("mubias", [128, 1], F32).ap()
    gam_row = A("gam_row", [1, D], F32).ap()
    bet_col = A("bet_col", [128, DC], F32).ap()
    bv_rep = A("bv_rep", [128, D], F32).ap()
    bo_rep = A("bo_rep", [128, D], F32).ap()
    bqk_eff = A("bqk_eff", [128, 2 * DC], F32).ap()
    bv_col = A("bv_col", [128, DC], F32).ap()
    row_D = A("row_D", [1, D], F32).ap()
    row_D2 = A("row_D2", [1, D], F32).ap()
    d1_all = A("d1_all", [128, T], F32).ap()
    s2sc = A("s2sc", [128, T], F32).ap()
    d2_all = A("d2_all", [128, T], F32).ap()
    convw = A("convw", [128, 3 * HC], F32).ap()
    convb = A("convb", [128, HC], F32).ap()
    stat = A("stat", [128, 16], F32).ap()
    nstat = A("nstat", [1, 16], F32).ap()
    trsp = A("trsp", [128, 128], F32).ap()

    POOL_B = 179 * 1024
    pool = A("pool", [128, POOL_B], mybir.dt.uint8)
    pool_addr = nc.lookup_mloc(pool).addr

    def carve(off, name, shape, dt):
        handle = nc.alloc_sbuf_tensor_at(name, shape, dt, offset=pool_addr + int(off))
        return handle.ap()

    WqTh = carve(0, "WqTh", [128, DC * D], BF16)
    WqTl = carve(16384, "WqTl", [128, DC * D], BF16)
    WkTh = carve(32768, "WkTh", [128, DC * D], BF16)
    WkTl = carve(49152, "WkTl", [128, DC * D], BF16)
    WvTh = carve(65536, "WvTh", [128, DC * D], BF16)
    WvTl = carve(81920, "WvTl", [128, DC * D], BF16)
    wraw = carve(98304, "wraw", [128, D], F32)
    fold = carve(102400, "fold", [128, D], F32)
    x_t = carve(106496, "x_t", [128, D], F32)
    xn_t = carve(110592, "xn_t", [128, D], F32)
    xh_t = carve(114688, "xh_t", [128, D], F32)
    y_t = carve(118784, "y_t", [128, D], F32)
    i32_t = carve(122880, "i32_t", [128, D], I32)
    f_t = carve(126976, "f_t", [128, D], F32)
    xq_t = carve(131072, "xq_t", [128, D], BF16)
    xh_fmh = carve(133120, "xh_fmh", [128, D], BF16)
    xh_fml = carve(135168, "xh_fml", [128, D], BF16)
    xq_fm = carve(137216, "xq_fm", [128, D], BF16)
    st_q = carve(139264, "st_q", [128, D], F32)
    st_k = carve(143360, "st_k", [128, D], F32)
    st_v = carve(147456, "st_v", [128, DC * VW0], F32)
    rawT = carve(151680, "rawT", [128, D], F32)
    gam_rep = carve(155776, "gam_rep", [128, D], F32)
    x_tB = carve(159872, "x_tB", [128, D], F32)
    xh_tB = carve(163968, "xh_tB", [128, D], F32)
    xq_tB = carve(168064, "xq_tB", [128, D], BF16)
    xh_fmhB = carve(170112, "xh_fmhB", [128, D], BF16)
    xh_fmlB = carve(172160, "xh_fmlB", [128, D], BF16)
    q_tm = carve(174208, "q_tm", [128, D], F32)
    k_tm = carve(178304, "k_tm", [128, D], F32)
    assert 178304 + 4096 <= POOL_B

    k_pair = [carve(0, "k_pair0", [128, S], F32),
              carve(4 * S, "k_pair1", [128, S], F32)]
    q_pair = [carve(8 * S, "q_pair0", [128, S], F32),
              carve(12 * S, "q_pair1", [128, S], F32)]
    VW = 130
    v_pairb = [carve(16 * S, "v_pair0", [128, KC * VW], F32),
               carve(16 * S + 4 * KC * VW, "v_pair1", [128, KC * VW], F32)]
    p2o = 16 * S + 8 * KC * VW
    PLAG = 6
    t_buf = [carve(p2o + 4096 * i, "t_buf%d" % i, [128, 1024], F32) for i in range(PLAG)]
    e_buf = [carve(p2o + 4096 * PLAG + 4096 * i, "e_buf%d" % i, [128, 1024], F32) for i in range(PLAG)]
    p_buf = [carve(p2o + 8192 * PLAG + 4096 * i, "p_buf%d" % i, [128, 1024], F32) for i in range(PLAG)]
    ov_st = carve(p2o + 12288 * PLAG, "ov_st", [128, 512], F32)
    rc_st = carve(p2o + 12288 * PLAG + 2048, "rc_st", [128, 8], F32)
    ovt_sb = carve(p2o + 12288 * PLAG + 2176, "ovt_sb", [128, 1024], F32)
    assert p2o + 12288 * PLAG + 6272 <= POOL_B

    w1qT = carve(0, "w1qT", [128, DC * H], BF16)
    HB = TAS + 2
    h_blk = carve(65536, "h_blk", [128, HC * HB], F32)
    p3a0 = 65536 + 4 * HC * HB
    d1_rep = carve(p3a0, "d1_rep", [128, S], F32)
    p3a1 = p3a0 + 4 * S
    w_st = carve(p3a1, "w_st", [128, D], F32)
    wy = carve(p3a1 + 4096, "wy", [128, D], F32)
    wi32 = carve(p3a1 + 8192, "wi32", [128, D], I32)
    wf = carve(p3a1 + 12288, "wf", [128, D], F32)
    wq_st = carve(p3a1 + 16384, "wq_st", [128, D], BF16)
    wsum = carve(p3a1 + 18432, "wsum", [128, HC], F32)
    xq_blk = carve(p3a1 + 18560, "xq_blk", [128, DC * HB], BF16)
    p3a2 = p3a1 + 18560 + 2 * DC * HB
    cbufs = [carve(p3a2, "cbuf0", [128, TAS], F32),
             carve(p3a2 + 4 * TAS, "cbuf1", [128, TAS], F32)]
    r_sb = [carve(p3a2 + 8 * TAS, "r_sb0", [128, TAS], F32),
            carve(p3a2 + 12 * TAS, "r_sb1", [128, TAS], F32)]
    redb = carve(p3a2 + 16 * TAS, "redb", [128, NSA * HC], F32)
    p3a3 = p3a2 + 16 * TAS + 4 * NSA * HC
    row_Sa = carve(p3a3, "row_Sa", [1, S], F32)
    w_stB = carve(p3a3 + 4 * S, "w_stB", [128, D], F32)
    p3a4 = p3a3 + 4 * S + 4096
    xq_blkB = carve(p3a4, "xq_blkB", [128, DC * HB], BF16)
    cbuf2 = carve(p3a4 + 2 * DC * HB, "cbuf2", [128, TAS], F32)
    wq_stB = carve(p3a4, "wq_stB", [128, D], BF16)
    assert p3a4 + 2 * DC * HB + 4 * TAS <= POOL_B

    w2qT = carve(0, "w2qT", [128, HC * D], BF16)
    w2st = carve(65536, "w2st", [128, H], F32)
    WoT = carve(65536, "WoT2", [128, DC * D], F32)
    p3b0 = 65536 + 4 * DC * D
    xq2P_ = [carve(p3b0, "xq2_blk0", [128, HC * QBS2], BF16),
             carve(p3b0 + 2 * HC * QBS2, "xq2_blk1", [128, HC * QBS2], BF16)]
    p3b1 = p3b0 + 4 * HC * QBS2
    r_ld = [carve(p3b1, "r_ld0", [128, QBS2], F32),
            carve(p3b1 + 1024, "r_ld1", [128, QBS2], F32),
            carve(p3b1 + 34816 + 12288, "r_ld2", [128, QBS2], F32),
            carve(p3b1 + 34816 + 13312, "r_ld3", [128, QBS2], F32)]
    roffb = carve(p3b1 + 34816 + 14336, "roffb", [128, QBS2], F32)
    halfb = carve(p3b1 + 34816 + 15360, "halfb", [128, QBS2], F32)
    s2_repP = [carve(p3b1 + 2048, "s2_rep0", [128, 2 * QBS2], F32),
               carve(p3b1 + 4096, "s2_rep1", [128, 2 * QBS2], F32)]
    ov_t = carve(p3b1 + 6144, "ov_t", [128, D], F32)
    ovfm = carve(p3b1 + 10240, "ovfm", [128, D], F32)
    x2_t = carve(p3b1 + 14336, "x2_t", [128, D], F32)
    ov_tB = carve(p3b1 + 34816, "ov_tB", [128, D], F32)
    x2_tB = carve(p3b1 + 34816 + 4096, "x2_tB", [128, D], F32)
    ffn_t = carve(p3b1 + 18432, "ffn_t", [128, D], F32)
    acc_t = carve(p3b1 + 22528, "acc_t", [128, D], F32)
    w2stB = carve(p3b1 + 34816, "w2stB", [128, H], F32)
    assert p3b1 + 34816 + 4 * H <= POOL_B
    hy = carve(p3b1 + 26624, "hy", [128, QBS2], F32)
    hi32 = carve(p3b1 + 27648, "hi32", [128, QBS2], I32)
    hf = carve(p3b1 + 28672, "hf", [128, QBS2], F32)
    hyG = carve(p3b1 + 29696, "hyG", [128, QBS2], F32)
    hi32G = carve(p3b1 + 30720, "hi32G", [128, QBS2], I32)
    hfG = carve(p3b1 + 31744, "hfG", [128, QBS2], F32)
    row_Sb = carve(p3b1 + 32768, "row_Sb", [1, 2 * QBS2], F32)
    acc_tB = carve(p3b1 + 34816 + 8192, "acc_tB", [128, D], F32)
    assert p3b1 + 34816 + 4 * H <= POOL_B

    PS_A = nc.alloc_psum_tensor("PS_A", [128, 1024], F32).ap()
    PS_B = nc.alloc_psum_tensor("PS_B", [128, 1024], F32).ap()
    PS_C = nc.alloc_psum_tensor("PS_C", [128, 1024], F32).ap()
    PS_D = nc.alloc_psum_tensor("PS_D", [128, 512], F32).ap()
    PS_E = nc.alloc_psum_tensor("PS_E", [128, 512], F32).ap()
    PS_B16 = PS_B.bitcast(BF16)

    block = es.enter_context(nc.Block())

    def transpose_tile(src, dst, ps, ident_, n=DC, inc_last="sV"):
        for dc in range(n):
            b.emit("tensor", lambda e, dc=dc, src=src, ps=ps, ident_=ident_: e.transpose(
                ps[:, dc * 128:(dc + 1) * 128], src[:, dc * 128:(dc + 1) * 128], ident_))
        t = b.mark("tensor", "sT")
        b.wait("vector", "sT", t)
        vv = None
        for dc in range(n):
            vv = b.emit("vector", lambda e, dc=dc, dst=dst, ps=ps: e.tensor_copy(
                dst[:, dc * 128:(dc + 1) * 128], ps[:, dc * 128:(dc + 1) * 128]),
                inc=(inc_last if dc == n - 1 else None))
        return vv

    b.dma("sync", ident, id_h)
    b.dma("sync", mu_sb, mu_h)
    b.dma("sync", gam_row, gam_h)
    b.dma("sync", bet_col, bet_h.squeeze(0).rearrange("(dc p) -> p dc", p=128), slow=True)
    b.dma("sync", convw.rearrange("p (hc k) -> p hc k", k=3), cw_h.rearrange("(hc p) k -> p hc k", p=128))
    b.dma("sync", convb, cb_h.squeeze(0).rearrange("(hc p) -> p hc", p=128), slow=True)
    b.dma("sync", row_D, bv_h)
    b.dma("sync", row_D2, bo_h)
    b.dma("sync", bqk_eff[:, 0:DC], bq_h.squeeze(0).rearrange("(oc p) -> p oc", p=128), slow=True)
    b.dma("sync", bqk_eff[:, DC:2 * DC], bk_h.squeeze(0).rearrange("(oc p) -> p oc", p=128), slow=True)
    b.dma("sync", bv_col, bv_h.squeeze(0).rearrange("(oc p) -> p oc", p=128), slow=True)
    d0 = b.cnt["dS"]
    b.wait("sync", "dS", d0)
    b.wait("vector", "dS", d0)
    b.V(lambda e: e.tensor_copy(ident16, ident))
    b.V(lambda e: e.memset(ones_row, 1.0))
    b.V(lambda e: e.memset(stat[:, 15:16], 1.0))
    v0 = b.mark("vector", "sV")
    b.wait("tensor", "sV", v0)
    b.wait("tensor", "dS", d0)
    b.emit("tensor", lambda e: e.matmul(PS_D[:, 0:1], ones_row, mu_sb, start=True, stop=True))

    def rep_slab(dst, src_row, ps):
        last = None
        for hh in range(D // 512):
            last = b.emit("tensor", lambda e, hh=hh, ps=ps, src_row=src_row: e.matmul(
                ps[:, hh * 512:(hh + 1) * 512], ones_row, src_row[:, hh * 512:(hh + 1) * 512],
                start=True, stop=True),
                inc=("sT" if hh == D // 512 - 1 else None))
        b.wait("vector", "sT", last)
        return b.Vd(lambda e, dst=dst, ps=ps: e.tensor_copy(dst, ps[:, 0:D]), inc="sV"), last

    v_gam, _ = rep_slab(gam_rep, gam_row, PS_A)
    b.wait("tensor", "sV", v_gam)
    v_bo, _ = rep_slab(bo_rep, row_D2, PS_A)
    t_mu = b.mark("tensor", "sT")
    b.wait("vector", "sT", t_mu)
    v_mu = b.Vd(lambda e: e.tensor_scalar(mubias, PS_D[:, 0:1], -0.5, None, ALU.mult), inc="sV")

    b.emit("scalar", lambda e: e.preload_act_table(AF.Tanh))

    b.wait("vector", "sV", v_bo)
    WTs = (("q", Wq_h, (WqTh, WqTl), True), ("k", Wk_h, (WkTh, WkTl), True),
           ("v", Wv_h, (WvTh, WvTl), True))
    wrawP = [wraw, carve(159872, "wrawB", [128, D], F32)]
    wsemP = ["dXa", "dXb"]
    wld = {}
    wproc = {}
    for wi, (wname, W_h, WT_sb, need_bias) in enumerate(WTs):
        for oc in range(DC):
            ci = wi * DC + oc
            if ci == 0:
                wld[0] = b.dma("sync", wrawP[0], W_h[0:128, :], sem=wsemP[0])
            if ci + 1 < 3 * DC:
                wi2, oc2 = divmod(ci + 1, DC)
                W2 = WTs[wi2][1]
                if ci >= 1:
                    b.wait("sync", "sV", wproc[ci - 1])
                wld[ci + 1] = b.dma("sync", wrawP[(ci + 1) % 2],
                                    W2[oc2 * 128:(oc2 + 1) * 128, :], sem=wsemP[(ci + 1) % 2])
            wraw_c = wrawP[ci % 2]
            b.wait("vector", wsemP[ci % 2], wld[ci])
            v1 = b.Vd(lambda e, wraw_c=wraw_c: e.tensor_mul(fold, wraw_c, gam_rep), inc="sV")
            b.wait("tensor", "sV", v1)
            for dc in range(DC):
                b.emit("tensor", lambda e, dc=dc: e.transpose(
                    PS_C[:, dc * 128:(dc + 1) * 128], fold[:, dc * 128:(dc + 1) * 128], ident))
            if need_bias:
                for dc in range(DC):
                    b.emit("tensor", lambda e, dc=dc, wraw_c=wraw_c: e.transpose(
                        PS_A[:, dc * 128:(dc + 1) * 128], wraw_c[:, dc * 128:(dc + 1) * 128], ident))
            t1 = b.mark("tensor", "sT")
            WTh, WTl = WT_sb
            b.wait("scalar", "sT", t1)
            for dc in range(DC):
                sl = slice(dc * D + oc * 128, dc * D + oc * 128 + 128)
                b.emit("scalar", lambda e, dc=dc, sl=sl, WTh=WTh: e.copy(
                    WTh[:, sl], PS_C[:, dc * 128:(dc + 1) * 128]))
            if need_bias:
                for dc in range(DC):
                    b.emit("scalar", lambda e, dc=dc: e.copy(
                        rawT[:, dc * 128:(dc + 1) * 128], PS_A[:, dc * 128:(dc + 1) * 128]))
            s_h = b.mark("scalar", "sS")
            b.wait("vector", "sS", s_h)
            for dc in range(DC):
                sl = slice(dc * D + oc * 128, dc * D + oc * 128 + 128)
                b.V(lambda e, dc=dc, sl=sl, WTh=WTh, WTl=WTl: e.tensor_sub(
                    WTl[:, sl], PS_C[:, dc * 128:(dc + 1) * 128], WTh[:, sl]))
            vv = b.mark("vector", "sV")
            wproc[ci] = vv
            b.wait("tensor", "sV", vv)
            if need_bias:
                for dc in range(DC):
                    b.emit("tensor", lambda e, dc=dc, oc=oc, wi=wi: e.matmul(
                        PS_E[:, wi * DC + oc: wi * DC + oc + 1],
                        rawT[:, dc * 128:(dc + 1) * 128],
                        bet_col[:, dc:dc + 1],
                        start=(dc == 0 and oc == 0 and wi == 0),
                        stop=(dc == DC - 1 and oc == DC - 1 and wi == 2)))
    t_bet = b.mark("tensor", "sT")
    b.wait("vector", "sT", t_bet)
    b.Vd(lambda e: e.tensor_add(bqk_eff, bqk_eff, PS_E[:, 0:2 * DC]))
    v_be = b.Vd(lambda e: e.tensor_add(bv_col, bv_col, PS_E[:, 2 * DC:3 * DC]), inc="sV")
    b.wait("tensor", "sV", v_be)
    t_bt = b.emit("tensor", lambda e: e.transpose(PS_D[0:DC, 0:128], bv_col, ident), inc="sT")
    b.wait("vector", "sT", t_bt)
    v_bt = b.Vd(lambda e: e.tensor_copy(trsp[0:DC, 0:128], PS_D[0:DC, 0:128]), inc="sV")
    b.wait("gpsimd", "sV", v_bt)
    b.dma("gpsimd", flat_scr[0:DC, :], trsp[0:DC, 0:128])
    b.wait("gpsimd", "dG", b.cnt["dG"])
    b.dma("gpsimd", row_D[:, 0:D], flat_scr.rearrange("a b -> (a b)").unsqueeze(0)[:, 0:D])
    b.wait("tensor", "dG", b.cnt["dG"])
    b.wait("tensor", "sV", v_bt)
    v_bvr, _ = rep_slab(bv_rep, row_D, PS_A)
    b.wait("tensor", "sV", v_bvr)

    def newton_rsqrt(col_w, col_r):
        b.Vd(lambda e: e.tensor_scalar(stat[:, 11:12], stat[:, col_w:col_w + 1],
                                       0.5, 0.5, ALU.mult, ALU.add))
        for _ in range(4):
            b.Vd(lambda e: e.reciprocal(stat[:, 12:13], stat[:, 11:12]))
            b.Vd(lambda e: e.tensor_mul(stat[:, 12:13], stat[:, 12:13], stat[:, col_w:col_w + 1]))
            b.Vd(lambda e: e.tensor_add(stat[:, 11:12], stat[:, 11:12], stat[:, 12:13]))
            b.Vd(lambda e: e.tensor_scalar(stat[:, 11:12], stat[:, 11:12], 0.5, None, ALU.mult))
        b.Vd(lambda e: e.reciprocal(stat[:, col_r:col_r + 1], stat[:, 11:12]))

    def ln_tile(src, dst, eps, cm, cr):
        b.Vd(lambda e: e.reduce_sum(stat[:, 8:9], src, AX.X))
        b.Vd(lambda e: e.tensor_scalar(stat[:, 9:10], stat[:, 8:9], -1.0 / D, None, ALU.mult))
        v0 = b.mark("vector", "sV")
        b.wait("scalar", "sV", v0)
        s0 = b.emit("scalar", lambda e, src=src: e.activation(
            y_t, src, AF.Square, bias=stat[:, 9:10], scale=1.0), inc="sS")
        b.wait("vector", "sS", s0)
        b.Vd(lambda e: e.reduce_sum(stat[:, 10:11], y_t, AX.X))
        b.Vd(lambda e: e.tensor_scalar(stat[:, 10:11], stat[:, 10:11], 1.0 / D, eps,
                                       ALU.mult, ALU.add))
        newton_rsqrt(10, cr)
        b.Vd(lambda e: e.tensor_mul(stat[:, cm:cm + 1], stat[:, 9:10], stat[:, cr:cr + 1]))
        b.Vd(lambda e, src=src, dst=dst: e.tensor_scalar(
            dst, src, stat[:, cr:cr + 1], stat[:, cm:cm + 1], ALU.mult, ALU.add))

    def quant_tile(src, dst_bf16, d1_col):
        b.Vd(lambda e: e.tensor_reduce(stat[:, 13:14], src, AX.X, ALU.max,
                                       apply_absolute_value=True))
        b.Vd(lambda e: e.tensor_scalar_max(stat[:, 13:14], stat[:, 13:14], 1e-5))
        b.Vd(lambda e: e.reciprocal(stat[:, 14:15], stat[:, 13:14]))
        b.Vd(lambda e: e.tensor_scalar(stat[:, 14:15], stat[:, 14:15], 127.0, None, ALU.mult))
        if d1_col is not None:
            b.Vd(lambda e: e.tensor_scalar(d1_all[:, d1_col:d1_col + 1], stat[:, 13:14],
                                           1.0 / 127.0, None, ALU.mult))
        b.Vd(lambda e, src=src: e.tensor_scalar(y_t, src, stat[:, 14:15], ROFF,
                                                ALU.mult, ALU.add))
        b.Vd(lambda e: e.tensor_copy(i32_t, y_t))
        b.Vd(lambda e: e.tensor_sub(f_t, y_t, i32_t))
        b.Vd(lambda e: e.tensor_single_scalar(f_t, f_t, 0.5, ALU.is_ge))
        b.Vd(lambda e, dst=dst_bf16: e.scalar_tensor_tensor(dst, i32_t, -ROFF, f_t,
                                                            ALU.add, ALU.add))

    b.V(lambda e: e.memset(st_v.rearrange("p (pr g w) -> p pr g w", g=2, w=65)[:, :, :, 64:65], 1.0))
    b.emit("vector", lambda e: e.drain())

    xP = [x_t, x_tB]
    xhP = [xh_t, xh_tB]
    xqP = [xq_t, xq_tB]
    fmhP = [xh_fmh, xh_fmhB]
    fmlP = [xh_fml, xh_fmlB]
    rdy = {}
    backg = {}
    xsem = lambda i: "dXa" if i % 2 == 0 else "dXb"
    b.wait("sync", "sV", b.cnt["sV"])
    dxs = {0: b.dma("sync", xP[0], x_h[0:128, :], sem=xsem(0))}

    def front(ti):
        par = ti % 2
        if ti + 1 < T:
            if ti >= 1:
                b.wait("sync", "sV", rdy[ti - 1])
            dxs[ti + 1] = b.dma("sync", xP[1 - par], x_h[(ti + 1) * 128:(ti + 2) * 128, :],
                                sem=xsem(ti + 1))
        b.wait("vector", xsem(ti), dxs[ti])
        ln_tile(xP[par], xn_t, EPS1, 2, 1)
        ln_tile(xn_t, xhP[par], EPS2, 5, 4)
        quant_tile(xn_t, xqP[par], ti)
        rdy[ti] = b.mark("vector", "sV")

    fmrec = {}
    fm_dc = {}
    backg_v = {}
    xq_dg = {}

    def back_tr(ti):
        par = ti % 2
        xh_, xq_ = xhP[par], xqP[par]
        fmh_, fml_ = fmhP[par], fmlP[par]
        b.wait("tensor", "sV", rdy[ti])
        if ti >= 1:
            b.wait("tensor", "sV", backg_v[ti - 1])
        for dc in range(DC):
            b.emit("tensor", lambda e, dc=dc, xh_=xh_: e.transpose(
                PS_C[:, dc * 128:(dc + 1) * 128], xh_[:, dc * 128:(dc + 1) * 128], ident))
        t_xh = b.mark("tensor", "sT")
        if ti >= 1:
            b.wait("tensor", "sV", backg[ti - 1])
        for dc in range(DC):
            b.emit("tensor", lambda e, dc=dc, xq_=xq_: e.transpose(
                PS_D.bitcast(BF16)[:, dc * 128:(dc + 1) * 128],
                xq_[:, dc * 128:(dc + 1) * 128], ident16))
        t_xq = b.mark("tensor", "sT")
        b.wait("vector", "sT", t_xh)
        fm_marks = []
        for dc in range(DC):
            b.V(lambda e, dc=dc, fmh_=fmh_: e.tensor_copy(
                fmh_[:, dc * 128:(dc + 1) * 128], PS_C[:, dc * 128:(dc + 1) * 128]))
            b.emit("vector", lambda e: e.drain())
            b.V(lambda e, dc=dc, fmh_=fmh_, fml_=fml_: e.tensor_sub(
                fml_[:, dc * 128:(dc + 1) * 128], PS_C[:, dc * 128:(dc + 1) * 128],
                fmh_[:, dc * 128:(dc + 1) * 128]))
            fm_marks.append(b.mark("vector", "sV"))
        fm_dc[ti] = fm_marks
        b.wait("vector", "sT", t_xq)
        if ti >= 1:
            b.wait("vector", "dG", backg_dG[ti - 1])
        for dc in range(DC):
            b.V(lambda e, dc=dc: e.tensor_copy(
                xq_fm[:, dc * 128:(dc + 1) * 128],
                PS_D.bitcast(BF16)[:, dc * 128:(dc + 1) * 128]))
        v_fm2 = b.mark("vector", "sV")
        b.wait("gpsimd", "sV", v_fm2)
        b.dma("gpsimd", xq_scr[:, :, ti * 128:(ti + 1) * 128].rearrange("dc p t -> p dc t"),
              xq_fm.rearrange("p (dc t) -> p dc t", dc=DC))
        xq_dg[ti] = b.cnt["dG"]
        fmrec[ti] = v_fm2

    def back_mm(ti):
        par = ti % 2
        fmh_, fml_ = fmhP[par], fmlP[par]
        qkv_ps = {"q": PS_A, "k": PS_B, "v": PS_C}
        marks = {}
        for wi, (wname, WTp) in enumerate((("q", (WqTh, WqTl)), ("k", (WkTh, WkTl)),
                                           ("v", (WvTh, WvTl)))):
            ps = qkv_ps[wname]
            WTh, WTl = WTp
            for og in range(2):
                combos = [(fmh_, WTh), (fmh_, WTl), (fml_, WTh)]
                n_tot = len(combos) * DC
                i = 0
                for dc in range(DC):
                    if wi == 0 and og == 0:
                        b.wait("tensor", "sV", fm_dc[ti][dc])
                    for (xa, wb) in combos:
                        b.emit("tensor", lambda e, og=og, dc=dc, ps=ps, xa=xa, wb=wb,
                               first=(i == 0), last=(i == n_tot - 1): e.matmul(
                            ps[:, og * 512:(og + 1) * 512],
                            xa[:, dc * 128:(dc + 1) * 128],
                            wb[:, dc * D + og * 512: dc * D + og * 512 + 512],
                            start=first, stop=last))
                        i += 1
            marks[wname] = b.mark("tensor", "sT")
        b.wait("scalar", "sT", marks["q"])
        b.emit("scalar", lambda e: e.copy(q_tm, PS_A[:, 0:1024]))
        b.wait("scalar", "sT", marks["k"])
        b.emit("scalar", lambda e: e.copy(k_tm, PS_B[:, 0:1024]))
        v_tm = b.mark("scalar", "sS")
        if ti >= 1:
            b.wait("vector", "dG", backg_dG[ti - 1])
        b.wait("vector", "sT", marks["v"])
        b.V(lambda e: e.tensor_add(
            st_v.rearrange("p (pr g w) -> p pr g w", g=2, w=65)[:, :, :, 0:64],
            PS_C[:, 0:1024].rearrange("p (pr g i) -> p pr g i", g=2, i=64),
            bv_rep.rearrange("p (pr g i) -> p pr g i", g=2, i=64)))
        v_ev_v = b.mark("vector", "sV")
        backg_v[ti] = v_ev_v
        b.wait("gpsimd", "sV", v_ev_v)
        b.dma("gpsimd", v_scr[:, ti, :, :].rearrange("pr p w -> p pr w"),
              st_v.rearrange("p (pr w) -> p pr w", w=VW0))
        b.wait("tensor", "sS", v_tm)
        b.wait("tensor", "sV", fmrec[ti])
        for dc in range(DC):
            pst = PS_A if dc < 4 else PS_B
            b.emit("tensor", lambda e, dc=dc, pst=pst: e.transpose(
                pst[:, (dc % 4) * 128:(dc % 4 + 1) * 128], q_tm[:, dc * 128:(dc + 1) * 128], ident))
        t_trq = b.mark("tensor", "sT")
        for dc in range(DC):
            pst = PS_D if dc < 4 else PS_E
            b.emit("tensor", lambda e, dc=dc, pst=pst: e.transpose(
                pst[:, (dc % 4) * 128:(dc % 4 + 1) * 128], k_tm[:, dc * 128:(dc + 1) * 128], ident))
        t_trk = b.mark("tensor", "sT")
        b.wait("vector", "sT", t_trq)
        for oc in range(DC):
            pst = PS_A if oc < 4 else PS_B
            b.V(lambda e, oc=oc, pst=pst: e.tensor_scalar(
                st_q[:, oc * 128:(oc + 1) * 128], pst[:, (oc % 4) * 128:(oc % 4 + 1) * 128],
                bqk_eff[:, oc:oc + 1], None, ALU.add))
        v_evq = b.mark("vector", "sV")
        b.wait("gpsimd", "sV", v_evq)
        b.dma("gpsimd", q_scr[:, :, ti * 128:(ti + 1) * 128].rearrange("oc p t -> p oc t"),
              st_q.rearrange("p (oc t) -> p oc t", oc=DC))
        b.wait("vector", "sT", t_trk)
        for oc in range(DC):
            pst = PS_D if oc < 4 else PS_E
            b.V(lambda e, oc=oc, pst=pst: e.tensor_scalar(
                st_k[:, oc * 128:(oc + 1) * 128], pst[:, (oc % 4) * 128:(oc % 4 + 1) * 128],
                bqk_eff[:, DC + oc:DC + oc + 1], None, ALU.add))
        vvk = b.mark("vector", "sV")
        b.wait("gpsimd", "sV", vvk)
        b.dma("gpsimd", k_scr[:, :, ti * 128:(ti + 1) * 128].rearrange("oc p t -> p oc t"),
              st_k.rearrange("p (oc t) -> p oc t", oc=DC))
        backg[ti] = vvk
        backg_dG[ti] = b.cnt["dG"]

    backg_dG = {}
    for ti in range(T):
        if ti >= 1:
            back_tr(ti - 1)
        front(ti)
        if ti >= 1:
            back_mm(ti - 1)
    back_tr(T - 1)
    back_mm(T - 1)

    vB = b.cnt["sV"]; tB = b.cnt["sT"]; sB = b.cnt["sS"]; gB = b.cnt["dG"]; dB = b.cnt["dS"]
    for eng in ("sync", "tensor", "vector", "scalar", "gpsimd"):
        b.wait(eng, "sV", vB); b.wait(eng, "sT", tB)
        if sB: b.wait(eng, "sS", sB)
        b.wait(eng, "dG", gB); b.wait(eng, "dS", dB)
        b.wait(eng, "dXa", b.cnt["dXa"]); b.wait(eng, "dXb", b.cnt["dXb"])


    if phases <= 1:
        b.wait("gpsimd", "dG", b.cnt["dG"])
        b.wait("gpsimd", "dS", b.cnt["dS"])
        b.wait("gpsimd", "dGa", b.cnt["dGa"])
        b.wait("gpsimd", "dGb", b.cnt["dGb"])
        return finish(nc, es, block, b, ops_only=True)

    NP = NH // 2
    psS = [PS_A, PS_B]
    pv_pairs = [[PS_D, PS_E], [PS_C[:, 0:512], PS_C[:, 512:1024]]]
    rec = {}

    def load_pair(p, buf):
        b.dma("sync", k_pair[buf], k_scr[p, :, :])
        b.dma("sync", q_pair[buf], q_scr[p, :, :])
        nq = max(1, KC // 8)
        for qt in range(0, KC, nq):
            b.dma("sync", v_pairb[buf].rearrange("p (kc w) -> p kc w", w=VW)[:, qt:qt + nq, :],
                  v_scr[p, qt:qt + nq, :, :].rearrange("t p w -> p t w"))
        return b.cnt["dS"]

    def emit_pv(p, qb, kc, buf):
        b.wait("tensor", "sV", rec[("p", p, qb, kc)])
        gqb = p * QB + qb
        pair = pv_pairs[gqb % 2]
        if kc == 0:
            prev = rec.get(("evtr", gqb - 2))
            if prev is not None:
                b.wait("tensor", "sV", prev)
        pb_ = p_buf[(qb * KC + kc) % PLAG]
        last = None
        for hh in range(2):
            last = b.emit("tensor", lambda e, hh=hh, pb_=pb_, kc=kc, buf=buf, pair=pair: e.matmul(
                pair[hh][0:65, 0:512],
                v_pairb[buf][:, kc * 130 + hh * 65: kc * 130 + hh * 65 + 65],
                pb_[:, hh * 512:(hh + 1) * 512],
                start=(kc == 0), stop=(kc == KC - 1)),
                inc=("sT" if hh == 1 else None))
        rec[("pv", p, qb, kc)] = last

    pair_loaded = {0: load_pair(0, 0)}

    ovst_dg = [0]

    def emit_tail(p, qb, buf):
        gqb = p * QB + qb
        pair = pv_pairs[gqb % 2]
        b.wait("vector", "sT", rec[("pv", p, qb, KC - 1)])
        for hh in range(2):
            b.V(lambda e, hh=hh, pair=pair: e.tensor_copy(
                ovt_sb[0:65, hh * 512:(hh + 1) * 512], pair[hh][0:65, 0:512]))
        v_ovt = b.mark("vector", "sV")
        rec[("evac", p, qb)] = v_ovt
        b.wait("tensor", "sV", v_ovt)
        for hh in range(2):
            for j in range(4):
                b.emit("tensor", lambda e, hh=hh, j=j, pair=pair: e.transpose(
                    pair[hh][:, j * 65: j * 65 + 65],
                    ovt_sb[0:65, hh * 512 + j * 128: hh * 512 + j * 128 + 128],
                    ident[0:65, 0:65]))
        t_tr = b.mark("tensor", "sT")
        b.wait("vector", "sT", t_tr)
        for hh in range(2):
            for j in range(4):
                b.V(lambda e, hh=hh, j=j, pair=pair: e.reciprocal(
                    rc_st[:, hh * 4 + j: hh * 4 + j + 1],
                    pair[hh][:, j * 65 + 64: j * 65 + 65]))
        b.emit("vector", lambda e: e.drain())
        b.wait("vector", "dG", ovst_dg[0])
        for j in range(4):
            for hh in range(2):
                b.V(lambda e, hh=hh, j=j, pair=pair: e.tensor_scalar(
                    ov_st[:, j * 128 + hh * 64: j * 128 + (hh + 1) * 64],
                    pair[hh][:, j * 65: j * 65 + 64],
                    rc_st[:, hh * 4 + j: hh * 4 + j + 1], None, ALU.mult))
        vv = b.mark("vector", "sV")
        b.wait("gpsimd", "sV", vv)
        ti0 = qb * (QBS // 128)
        b.dma("gpsimd", ov_scr[ti0:ti0 + 4, :, p * 128:(p + 1) * 128].rearrange("j p d -> p j d"),
              ov_st.rearrange("p (j d) -> p j d", j=4))
        ovst_dg[0] = b.cnt["dG"]
        rec[("evtr", gqb)] = b.cnt["sV"]

    lin = lambda p, qb, kc: (p * QB + qb) * KC + kc
    NKC = QB * KC
    for p in range(NP):
        buf = p % 2
        if p + 1 < NP:
            b.wait("sync", "dS", pair_loaded[p])
            pair_loaded[p + 1] = load_pair(p + 1, 1 - buf)
        b.wait("tensor", "dS", pair_loaded[p])
        b.wait("scalar", "dS", pair_loaded[p])
        for gk in range(NKC):
            qb, kc = divmod(gk, KC)
            st = gk % 2
            prev = lin(p, qb, kc) - 2
            if prev >= 0:
                pp, rem = divmod(prev, QB * KC)
                qq, kk = divmod(rem, KC)
                b.wait("tensor", "sS", rec[("acts", pp, qq, kk)])
            kp, qp = k_pair[buf], q_pair[buf]
            b.emit("tensor", lambda e, kp=kp, qp=qp, st=st, kc=kc, qb=qb: e.matmul(
                psS[st][:, 0:512], kp[0:64, kc * 128:(kc + 1) * 128],
                qp[0:64, qb * QBS: qb * QBS + 512],
                start=True, stop=True, tile_position=(0, 0)))
            rec[("scores", p, qb, kc)] = b.emit(
                "tensor", lambda e, kp=kp, qp=qp, st=st, kc=kc, qb=qb: e.matmul(
                    psS[st][:, 512:1024], kp[64:128, kc * 128:(kc + 1) * 128],
                    qp[64:128, qb * QBS: qb * QBS + 512],
                    start=True, stop=True, tile_position=(64, 0)), inc="sT")
            if gk >= PLAG:
                qb3, kc3 = divmod(gk - PLAG, KC)
                emit_pv(p, qb3, kc3, buf)
                if kc3 == KC - 1:
                    emit_tail(p, qb3, buf)
            b.wait("scalar", "sT", rec[("scores", p, qb, kc)])
            if gk >= PLAG:
                qb3, kc3 = divmod(gk - PLAG, KC)
                b.wait("scalar", "sV", rec[("p", p, qb3, kc3)])
            tb_, eb_ = t_buf[gk % PLAG], e_buf[gk % PLAG]
            b.emit("scalar", lambda e, st=st, tb_=tb_: e.activation(
                tb_, psS[st][:, 0:1024], AF.Tanh, bias=mubias, scale=1.0 / 16))
            rec[("acts", p, qb, kc)] = b.emit(
                "scalar", lambda e, st=st, eb_=eb_: e.activation(
                    eb_, psS[st][:, 0:1024], AF.Exp, bias=0.0, scale=0.125), inc="sS")
            b.wait("vector", "sS", rec[("acts", p, qb, kc)])
            if gk >= PLAG:
                qb3, kc3 = divmod(gk - PLAG, KC)
                b.wait("vector", "sT", rec[("pv", p, qb3, kc3)])
            pb_ = p_buf[gk % PLAG]
            rec[("p", p, qb, kc)] = b.emit(
                "vector", lambda e, tb_=tb_, eb_=eb_, pb_=pb_: e.scalar_tensor_tensor(
                    pb_, tb_, 1.0, eb_, ALU.add, ALU.mult), inc="sV")
        for gk2 in range(max(0, NKC - PLAG), NKC):
            qb3, kc3 = divmod(gk2, KC)
            emit_pv(p, qb3, kc3, buf)
            if kc3 == KC - 1:
                emit_tail(p, qb3, buf)
        b.wait("sync", "sT", rec[("scores", p, QB - 1, KC - 1)])
        b.wait("sync", "sV", rec[("evtr", p * QB + QB - 1)])

    vB = b.cnt["sV"]; tB = b.cnt["sT"]; sB = b.cnt["sS"]; gB = b.cnt["dG"]; dB = b.cnt["dS"]
    for eng in ("sync", "tensor", "vector", "scalar", "gpsimd"):
        b.wait(eng, "sV", vB); b.wait(eng, "sT", tB); b.wait(eng, "sS", sB)
        b.wait(eng, "dG", gB); b.wait(eng, "dS", dB)


    if phases <= 2:
        b.wait("gpsimd", "dG", b.cnt["dG"])
        b.wait("gpsimd", "dS", b.cnt["dS"])
        b.wait("gpsimd", "dGa", b.cnt["dGa"])
        b.wait("gpsimd", "dGb", b.cnt["dGb"])
        return finish(nc, es, block, b, ops_only=True)

    wstP = [w_st, w_stB]
    wsemQ = ["dXa", "dXb"]
    b.wait("sync", "sV", b.cnt["sV"])
    mld = {0: b.dma("sync", wstP[0], w1_h[0:128, :], sem=wsemQ[0])}
    mproc = {}
    for hc in range(HC):
        if hc + 1 < HC:
            if hc >= 1:
                b.wait("sync", "sV", mproc[hc - 1])
            mld[hc + 1] = b.dma("sync", wstP[(hc + 1) % 2],
                                w1_h[(hc + 1) * 128:(hc + 2) * 128, :], sem=wsemQ[(hc + 1) % 2])
        b.wait("vector", wsemQ[hc % 2], mld[hc])
        mproc[hc] = b.V(lambda e, hc=hc: e.tensor_reduce(
            wsum[:, hc:hc + 1], wstP[hc % 2], AX.X, ALU.add, apply_absolute_value=True), inc="sV")
    b.emit("vector", lambda e: e.drain())
    b.Vd(lambda e: e.reduce_sum(stat[:, 8:9], wsum[:, 0:HC], AX.X))
    v_ws = b.mark("vector", "sV")
    b.wait("tensor", "sV", v_ws)
    t_ws = b.emit("tensor", lambda e: e.matmul(PS_D[0:1, 0:1], stat[:, 8:9], stat[:, 15:16],
                                               start=True, stop=True), inc="sT")
    b.wait("vector", "sT", t_ws)
    b.Vd(lambda e: e.tensor_scalar(nstat[:, 0:1], PS_D[0:1, 0:1], 1.0 / (H * D), 1e-5,
                                   ALU.mult, ALU.max))
    b.Vd(lambda e: e.reciprocal(nstat[:, 1:2], nstat[:, 0:1]))
    v_w1s = b.mark("vector", "sV")
    b.wait("tensor", "sV", v_w1s)
    t_sc = b.emit("tensor", lambda e: e.matmul(PS_D[:, 4:6], ones_row, nstat[:, 0:2],
                                               start=True, stop=True), inc="sT")
    b.wait("vector", "sT", t_sc)
    b.Vd(lambda e: e.tensor_copy(stat[:, 6:8], PS_D[:, 4:6]))
    v_w1c = b.mark("vector", "sV")
    b.wait("tensor", "sV", v_w1c)

    def quant_rows_to(dst_bf16, src, scale_col, clip1):
        assert clip1
        b.Vd(lambda e: e.tensor_scalar(wy, src, scale_col, None, ALU.mult))
        b.Vd(lambda e: e.tensor_single_scalar(wf, wy, 0.5, ALU.is_ge))
        b.Vd(lambda e: e.tensor_single_scalar(wy, wy, -1.0, ALU.mult))
        b.Vd(lambda e: e.tensor_single_scalar(wy, wy, 0.5, ALU.is_ge))
        b.Vd(lambda e: e.tensor_sub(dst_bf16, wf, wy))

    wqP = [wq_st, wq_stB]
    b.wait("sync", "sV", b.cnt["sV"])
    qld = {0: b.dma("sync", wstP[0], w1_h[0:128, :], sem=wsemQ[0])}
    vq_m = {}
    vv_m = {}
    trn_m = {}

    def w1_copies(hc):
        b.wait("vector", "sT", trn_m[hc])
        for dc in range(DC):
            b.V(lambda e, dc=dc, hc=hc: e.tensor_copy(
                w1qT[:, dc * H + hc * 128: dc * H + hc * 128 + 128],
                PS_B16[:, dc * 128:(dc + 1) * 128]))
        vv_m[hc] = b.mark("vector", "sV")

    for hc in range(HC):
        if hc + 1 < HC:
            if hc >= 1:
                b.wait("sync", "sV", vq_m[hc - 1])
            qld[hc + 1] = b.dma("sync", wstP[(hc + 1) % 2],
                                w1_h[(hc + 1) * 128:(hc + 2) * 128, :], sem=wsemQ[(hc + 1) % 2])
        b.wait("vector", wsemQ[hc % 2], qld[hc])
        if hc >= 2:
            b.wait("vector", "sT", trn_m[hc - 2])
        quant_rows_to(wqP[hc % 2], wstP[hc % 2], stat[:, 7:8], True)
        vq_m[hc] = b.mark("vector", "sV")
        if hc >= 1:
            w1_copies(hc - 1)
        b.wait("tensor", "sV", vv_m[hc - 1] if hc >= 1 else vq_m[hc])
        if hc == 0:
            pass
        else:
            b.wait("tensor", "sV", vq_m[hc])
        for dc in range(DC):
            b.emit("tensor", lambda e, dc=dc, hc=hc: e.transpose(
                PS_B16[:, dc * 128:(dc + 1) * 128], wqP[hc % 2][:, dc * 128:(dc + 1) * 128],
                ident16))
        trn_m[hc] = b.mark("tensor", "sT")
    w1_copies(HC - 1)
    b.wait("tensor", "sV", vv_m[HC - 1])
    b.Vd(lambda e: e.tensor_scalar(d1_all[:, 0:T], d1_all[:, 0:T], stat[:, 6:7],
                                   None, ALU.mult))
    v_d1 = b.mark("vector", "sV")
    b.wait("tensor", "sV", v_d1)
    t_d1 = b.emit("tensor", lambda e: e.transpose(PS_D[0:T, 0:128], d1_all[:, 0:T], ident),
                  inc="sT")
    b.wait("vector", "sT", t_d1)
    v_d1b = b.Vd(lambda e: e.tensor_copy(trsp[0:T, 0:128], PS_D[0:T, 0:128]), inc="sV")
    b.wait("gpsimd", "sV", v_d1b)
    b.dma("gpsimd", flat_scr[0:T, :], trsp[0:T, 0:128])
    b.wait("gpsimd", "dG", b.cnt["dG"])
    b.dma("gpsimd", row_Sa[:, 0:S], flat_scr.rearrange("a b -> (a b)").unsqueeze(0)[:, 0:S])
    g_row = b.cnt["dG"]
    b.wait("tensor", "dG", g_row)
    b.wait("tensor", "sV", v_d1b)
    last = None
    for hh in range(S // 512):
        last = b.emit("tensor", lambda e, hh=hh: e.matmul(
            PS_A[:, (hh % 2) * 512:(hh % 2) * 512 + 512], ones_row,
            row_Sa[:, hh * 512: hh * 512 + 512], start=True, stop=True),
            inc="sT")
        b.wait("vector", "sT", last)
        vv = b.V(lambda e, hh=hh: e.tensor_copy(
            d1_rep[:, hh * 512:(hh + 1) * 512], PS_A[:, (hh % 2) * 512:(hh % 2) * 512 + 512]),
            inc="sV")
        b.wait("tensor", "sV", vv)
    v_p3prep = b.mark("vector", "sV")
    b.wait("tensor", "sV", v_p3prep)
    b.wait("scalar", "sV", v_p3prep)
    b.wait("sync", "sV", v_p3prep)

    b.emit("scalar", lambda e: e.preload_act_table(AF.Silu))
    if dbg:
        dbg_w1qT = nc.dram_tensor("dbg_w1qT", [128, DC * H], BF16, kind="ExternalOutput").ap()
        dbg_d1rep = nc.dram_tensor("dbg_d1rep", [128, S], F32, kind="ExternalOutput").ap()
        dbg_hblk = nc.dram_tensor("dbg_hblk", [128, HC * (TAS + 2)], F32, kind="ExternalOutput").ap()
        b.wait("gpsimd", "sV", b.cnt["sV"])
        b.dma("gpsimd", dbg_w1qT, w1qT)
        b.dma("gpsimd", dbg_d1rep, d1_rep)
        b.wait("sync", "dG", b.cnt["dG"])

    spill_rec = {}
    evac_rec = {}
    silu_rec = {}
    red_rec = {}
    trn_rec = {}
    cv1_rec = {}
    d_xq = {}
    blk_tf = {}
    xq_blkP = [xq_blk, xq_blkB]
    cbufs3 = [cbufs[0], cbufs[1], cbuf2]
    PSH3 = [PS_A, PS_B, PS_C]

    def xq_load(tb):
        lo = tb * TAS
        has_l = tb > 0
        has_r = tb < TA - 1
        src_lo = lo - 1 if has_l else lo
        src_hi = lo + TAS + 1 if has_r else lo + TAS
        dst_lo = 0 if has_l else 1
        xqb = xq_blkP[tb % 2]
        if tb >= 2:
            b.wait("sync", "sT", blk_tf[tb - 2])
        xsm = "dXa" if tb % 2 == 0 else "dXb"
        b.dma("sync", xqb.rearrange("p (dc t) -> p dc t", dc=DC)[:, :, dst_lo:dst_lo + (src_hi - src_lo)],
              xq_scr[:, :, src_lo:src_hi].rearrange("dc p t -> p dc t"), sem=xsm)
        d_xq[tb] = (xsm, b.cnt[xsm])

    def emit_transposes(key):
        pst = PS_E if key % 2 == 0 else PS_D
        b.wait("tensor", "sS", silu_rec[key])
        if key >= 2:
            b.wait("tensor", "sV", red_rec[key - 2])
        rbuf = r_sb[key % 2]
        for j in range(NSA):
            b.emit("tensor", lambda e, j=j, rbuf=rbuf, pst=pst: e.transpose(
                pst[:, j * 128:(j + 1) * 128], rbuf[:, j * 128:(j + 1) * 128], ident))
        trn_rec[key] = b.mark("tensor", "sT")

    def emit_reduces(key):
        pst = PS_E if key % 2 == 0 else PS_D
        hc = key % HC
        b.wait("vector", "sT", trn_rec[key])
        for j in range(NSA):
            b.V(lambda e, j=j, hc=hc, pst=pst: e.tensor_reduce(
                redb[:, hc * NSA + j: hc * NSA + j + 1], pst[:, j * 128:(j + 1) * 128],
                AX.X, ALU.max, apply_absolute_value=True))
        red_rec[key] = b.mark("vector", "sV")
        if hc == HC - 1:
            tbp = key // HC
            for j in range(NSA):
                b.V(lambda e, j=j, tbp=tbp: e.tensor_reduce(
                    s2sc[:, tbp * NSA + j: tbp * NSA + j + 1],
                    redb.rearrange("p (hc j) -> p hc j", j=NSA)[:, :, j], AX.X, ALU.max))
            b.emit("vector", lambda e: e.drain())

    xq_load(0)
    for tb in range(TA):
        lo = tb * TAS
        has_l = tb > 0
        has_r = tb < TA - 1
        xqb = xq_blkP[tb % 2]
        if tb + 1 < TA:
            xq_load(tb + 1)
        if not has_l or not has_r:
            b.wait("vector", d_xq[tb][0], d_xq[tb][1])
            if not has_l:
                b.V(lambda e, xqb=xqb: e.memset(
                    xqb.rearrange("p (dc t) -> p dc t", dc=DC)[:, :, 0:1], 0.0))
            if not has_r:
                b.V(lambda e, xqb=xqb: e.memset(
                    xqb.rearrange("p (dc t) -> p dc t", dc=DC)[:, :, HB - 1:HB], 0.0))
            vz = b.mark("vector", "sV")
            b.wait("tensor", "sV", vz)
        b.wait("tensor", d_xq[tb][0], d_xq[tb][1])

        for hc in range(HC):
            key = tb * HC + hc
            psh = PSH3[key % 3]
            if key >= 3:
                b.wait("tensor", "sV", evac_rec[key - 3])
            for dc in range(DC):
                b.emit("tensor", lambda e, hc=hc, dc=dc, psh=psh, xqb=xqb: e.matmul(
                    psh[:, 0:HB],
                    w1qT[:, dc * H + hc * 128: dc * H + hc * 128 + 128],
                    xqb[:, dc * HB: dc * HB + HB],
                    start=(dc == 0), stop=(dc == DC - 1)))
            t_h = b.mark("tensor", "sT")
            if key >= 1:
                emit_transposes(key - 1)
            b.wait("vector", "sT", t_h)
            off = hc * HB
            c0 = 1 if not has_l else 0
            c1 = HB - 1 if not has_r else HB
            d0 = lo - 1 + c0
            b.V(lambda e, off=off, c0=c0, c1=c1, d0=d0, psh=psh: e.tensor_mul(
                h_blk[:, off + c0: off + c1], psh[:, c0:c1],
                d1_rep[:, d0: d0 + (c1 - c0)]))
            if not has_l:
                b.V(lambda e, off=off: e.memset(h_blk[:, off: off + 1], 0.0))
            if not has_r:
                b.V(lambda e, off=off: e.memset(h_blk[:, off + HB - 1: off + HB], 0.0))
            v_h = b.mark("vector", "sV")
            evac_rec[key] = v_h
            cbuf = cbufs3[key % 3]
            b.wait("scalar", "sV", v_h)
            cv1_rec[key] = b.emit("scalar", lambda e, off=off, hc=hc, cbuf=cbuf: e.activation(
                cbuf, h_blk[:, off: off + TAS], AF.Copy,
                bias=0.0, scale=convw[:, hc * 3: hc * 3 + 1]), inc="sS")
            b.wait("vector", "sS", cv1_rec[key])
            b.Vd(lambda e, off=off, hc=hc, cbuf=cbuf: e.scalar_tensor_tensor(
                cbuf, h_blk[:, off + 1: off + 1 + TAS],
                convw[:, hc * 3 + 1: hc * 3 + 2], cbuf, ALU.mult, ALU.add))
            b.Vd(lambda e, off=off, hc=hc, cbuf=cbuf: e.scalar_tensor_tensor(
                cbuf, h_blk[:, off + 2: off + 2 + TAS],
                convw[:, hc * 3 + 2: hc * 3 + 3], cbuf, ALU.mult, ALU.add))
            v_c = b.mark("vector", "sV")
            if key >= 1:
                emit_reduces(key - 1)
            b.wait("scalar", "sV", v_c)
            if key - 2 >= 0:
                b.wait("scalar", "dGa" if key % 2 == 0 else "dGb", spill_rec[key - 2])
            rbuf = r_sb[key % 2]
            silu_rec[key] = b.emit("scalar", lambda e, rbuf=rbuf, cbuf=cbuf, hc=hc: e.activation(
                rbuf, cbuf, AF.Silu, bias=convb[:, hc:hc + 1], scale=1.0), inc="sS")
            b.wait("gpsimd", "sS", silu_rec[key])
            spill_rec[key] = b.dma("gpsimd", r_scr[hc, :, lo:lo + TAS], rbuf,
                                   sem=("dGa" if key % 2 == 0 else "dGb"))
        blk_tf[tb] = b.cnt["sT"]
        if tb == TA - 1:
            kend = tb * HC + HC - 1
            emit_transposes(kend)
            emit_reduces(kend)
    b.wait("gpsimd", "dG", b.cnt["dG"])
    b.wait("gpsimd", "dGa", b.cnt["dGa"])
    b.wait("gpsimd", "dGb", b.cnt["dGb"])

    vB = b.cnt["sV"]; tB = b.cnt["sT"]; sB = b.cnt["sS"]; gB = b.cnt["dG"]; dB = b.cnt["dS"]
    gA = b.cnt["dGa"]; gBb = b.cnt["dGb"]
    xA = b.cnt["dXa"]; xBb = b.cnt["dXb"]
    for eng in ("sync", "tensor", "vector", "scalar", "gpsimd"):
        b.wait(eng, "sV", vB); b.wait(eng, "sT", tB); b.wait(eng, "sS", sB)
        b.wait(eng, "dG", gB); b.wait(eng, "dS", dB)
        b.wait(eng, "dGa", gA); b.wait(eng, "dGb", gBb)
        b.wait(eng, "dXa", xA); b.wait(eng, "dXb", xBb)


    if phases <= 3:
        b.wait("gpsimd", "dG", b.cnt["dG"])
        b.wait("gpsimd", "dS", b.cnt["dS"])
        b.wait("gpsimd", "dGa", b.cnt["dGa"])
        b.wait("gpsimd", "dGb", b.cnt["dGb"])
        return finish(nc, es, block, b, ops_only=True)

    w2P = [w2st, w2stB]
    b.wait("sync", "sV", b.cnt["sV"])
    m2ld = {0: b.dma("sync", w2P[0], w2_h[0:128, :], sem="dXa")}
    m2proc = {}
    for dc in range(DC):
        if dc + 1 < DC:
            if dc >= 1:
                b.wait("sync", "sV", m2proc[dc - 1])
            m2ld[dc + 1] = b.dma("sync", w2P[(dc + 1) % 2],
                                 w2_h[(dc + 1) * 128:(dc + 2) * 128, :],
                                 sem=("dXa" if (dc + 1) % 2 == 0 else "dXb"))
        b.wait("vector", "dXa" if dc % 2 == 0 else "dXb", m2ld[dc])
        m2proc[dc] = b.V(lambda e, dc=dc: e.tensor_reduce(
            wsum[:, dc:dc + 1], w2P[dc % 2], AX.X, ALU.add, apply_absolute_value=True), inc="sV")
    b.emit("vector", lambda e: e.drain())
    b.Vd(lambda e: e.reduce_sum(stat[:, 8:9], wsum[:, 0:DC], AX.X))
    v_ws2 = b.mark("vector", "sV")
    b.wait("tensor", "sV", v_ws2)
    t_ws2 = b.emit("tensor", lambda e: e.matmul(PS_D[0:1, 2:3], stat[:, 8:9], stat[:, 15:16],
                                                start=True, stop=True), inc="sT")
    b.wait("vector", "sT", t_ws2)
    b.Vd(lambda e: e.tensor_scalar(nstat[:, 2:3], PS_D[0:1, 2:3], 1.0 / (H * D), 1e-5,
                                   ALU.mult, ALU.max))
    b.Vd(lambda e: e.reciprocal(nstat[:, 3:4], nstat[:, 2:3]))
    v_w2s = b.mark("vector", "sV")
    b.wait("tensor", "sV", v_w2s)
    t_sc2 = b.emit("tensor", lambda e: e.matmul(PS_D[:, 6:8], ones_row, nstat[:, 2:4],
                                                start=True, stop=True), inc="sT")
    b.wait("vector", "sT", t_sc2)
    b.Vd(lambda e: e.tensor_copy(stat[:, 6:8], PS_D[:, 6:8]))
    b.Vd(lambda e: e.tensor_scalar_max(s2sc[:, 0:T], s2sc[:, 0:T], 1e-5))
    b.Vd(lambda e: e.tensor_scalar(d2_all[:, 0:T], s2sc[:, 0:T], 1.0 / 127.0,
                                   None, ALU.mult))
    b.Vd(lambda e: e.tensor_scalar(d2_all[:, 0:T], d2_all[:, 0:T], stat[:, 6:7],
                                   None, ALU.mult))
    b.Vd(lambda e: e.reciprocal(s2sc[:, 0:T], s2sc[:, 0:T]))
    b.Vd(lambda e: e.tensor_scalar(s2sc[:, 0:T], s2sc[:, 0:T], 127.0, None, ALU.mult))
    v_d2 = b.mark("vector", "sV")
    b.wait("tensor", "sV", v_d2)

    wqP = [wq_st, wq_stB]
    b.wait("sync", "sV", b.cnt["sV"])
    q2ld = {0: b.dma("sync", w2P[0], w2_h[0:128, :], sem="dXa")}
    vq2 = {}
    vv2 = {}
    trn2 = {}
    N2P = DC * 4

    def w2_copies(i):
        dc, part = divmod(i, 4)
        b.wait("vector", "sT", trn2[i])
        for t8 in range(8):
            hcg = part * 8 + t8
            b.V(lambda e, t8=t8, hcg=hcg, dc=dc: e.tensor_copy(
                w2qT[:, hcg * D + dc * 128: hcg * D + dc * 128 + 128],
                PS_B16[:, t8 * 128:(t8 + 1) * 128]))
        vv2[i] = b.mark("vector", "sV")

    for i in range(N2P):
        dc, part = divmod(i, 4)
        if part == 0:
            if dc + 1 < DC:
                if dc >= 1:
                    b.wait("sync", "sV", vq2[dc * 4 - 1])
                q2ld[dc + 1] = b.dma("sync", w2P[(dc + 1) % 2],
                                     w2_h[(dc + 1) * 128:(dc + 2) * 128, :],
                                     sem=("dXa" if (dc + 1) % 2 == 0 else "dXb"))
            b.wait("vector", "dXa" if dc % 2 == 0 else "dXb", q2ld[dc])
        sl = slice(part * D, (part + 1) * D)
        if i >= 2:
            b.wait("vector", "sT", trn2[i - 2])
        quant_rows_to(wqP[i % 2], w2P[dc % 2][:, sl], stat[:, 7:8], True)
        vq2[i] = b.mark("vector", "sV")
        if i >= 1:
            w2_copies(i - 1)
        b.wait("tensor", "sV", vv2[i - 1] if i >= 1 else vq2[i])
        if i >= 1:
            b.wait("tensor", "sV", vq2[i])
        for t8 in range(8):
            b.emit("tensor", lambda e, t8=t8, i=i: e.transpose(
                PS_B16[:, t8 * 128:(t8 + 1) * 128], wqP[i % 2][:, t8 * 128:(t8 + 1) * 128],
                ident16))
        trn2[i] = b.mark("tensor", "sT")
    w2_copies(N2P - 1)
    b.wait("tensor", "sV", vv2[N2P - 1])
    b.wait("sync", "sV", vv2[N2P - 1])
    wo_ld = {0: b.dma("sync", w_st, Wo_h[0:128, :], sem="dXa")}
    wo_vv = {}
    for oc in range(DC):
        wop = [w_st, w_stB][oc % 2]
        if oc + 1 < DC:
            if oc >= 1:
                b.wait("sync", "sV", wo_vv[oc - 1])
            wo_ld[oc + 1] = b.dma("sync", [w_st, w_stB][(oc + 1) % 2],
                                  Wo_h[(oc + 1) * 128:(oc + 2) * 128, :],
                                  sem=("dXa" if (oc + 1) % 2 == 0 else "dXb"))
        b.wait("tensor", "dXa" if oc % 2 == 0 else "dXb", wo_ld[oc])
        if oc >= 1:
            b.wait("tensor", "sV", wo_vv[oc - 1])
        for dc in range(DC):
            b.emit("tensor", lambda e, dc=dc, wop=wop: e.transpose(
                PS_C[:, dc * 128:(dc + 1) * 128], wop[:, dc * 128:(dc + 1) * 128], ident))
        t_tr = b.mark("tensor", "sT")
        b.wait("vector", "dXa" if oc % 2 == 0 else "dXb", wo_ld[oc])
        b.wait("vector", "sT", t_tr)
        for dc in range(DC):
            b.V(lambda e, dc=dc, oc=oc: e.tensor_copy(
                WoT[:, dc * D + oc * 128: dc * D + oc * 128 + 128],
                PS_C[:, dc * 128:(dc + 1) * 128]))
        wo_vv[oc] = b.mark("vector", "sV")
    b.wait("tensor", "sV", wo_vv[DC - 1])
    v_pb = b.cnt["sV"]
    for eng in ("sync", "tensor", "vector", "scalar", "gpsimd"):
        b.wait(eng, "sV", v_pb)
    vov_prev = None
    prev_qrec = None
    ov_tP = [ov_t, ov_tB]
    x2_tP = [x2_t, x2_tB]
    acc_tP = [acc_t, acc_tB]
    dld = {}
    asm_rec = {}
    qdone = {}
    tf_last = {}
    out_dg = {}
    rsem = lambda i: ("dRa", "dRb", "dRc", "dRd")[i % 4]

    def build_slab(g):
        dst = s2_repP[g % 2]
        b.Vd(lambda e, g=g: e.tensor_copy(stat[:, 8:12], s2sc[:, g * 4:g * 4 + 4]))
        v_s2c = b.mark("vector", "sV")
        b.wait("tensor", "sV", v_s2c)
        t_s2 = b.emit("tensor", lambda e: e.transpose(PS_D[0:4, 0:128], stat[:, 8:12], ident),
                      inc="sT")
        b.wait("vector", "sT", t_s2)
        v_s2r = b.Vd(lambda e: e.tensor_copy(trsp[0:4, 0:128], PS_D[0:4, 0:128]), inc="sV")
        b.wait("gpsimd", "sV", v_s2r)
        b.dma("gpsimd", flat_scr[0:4, :], trsp[0:4, 0:128])
        b.wait("gpsimd", "dG", b.cnt["dG"])
        b.dma("gpsimd", row_Sb[:, 0:2 * QBS2],
              flat_scr.rearrange("a b -> (a b)").unsqueeze(0)[:, 0:2 * QBS2])
        gr = b.cnt["dG"]
        b.wait("tensor", "dG", gr)
        b.wait("tensor", "sV", v_s2r)
        t_s2b = b.emit("tensor", lambda e, dst=dst: e.matmul(
            PS_C[:, 0:2 * QBS2], ones_row, row_Sb[:, 0:2 * QBS2],
            start=True, stop=True), inc="sT")
        b.wait("vector", "sT", t_s2b)
        v_s2 = b.Vd(lambda e, dst=dst: e.tensor_copy(dst, PS_C[:, 0:2 * QBS2]), inc="sV")
        slab_rdy[g] = v_s2
        b.wait("tensor", "sV", v_s2)

    def quant_half(qb, lohalf):
        nonlocal prev_qrec
        lo = qb * QBS2
        slab = s2_repP[(qb // 2) % 2][:, (qb % 2) * QBS2:(qb % 2 + 1) * QBS2]
        xq2b = xq2P_[qb % 2]
        h0, h1 = lohalf * (HC // 2), (lohalf + 1) * (HC // 2)
        if lohalf == 0:
            if qb >= 2:
                b.wait("vector", "sT", tf_last[qb - 2])
            for k in (0, 1, 2):
                if prev_qrec is not None:
                    b.wait("sync", "sV", prev_qrec[k])
                qh_rec[k] = b.dma("sync", r_ld[k], r_scr[k, :, lo:lo + QBS2], sem=rsem(k))
        for hcq in range(h0, h1):
            rb = r_ld[hcq % 4]
            nxt = hcq + 3
            if nxt < HC:
                if hcq - 1 >= 0:
                    b.wait("sync", "sV", qq_rec[hcq - 1])
                elif prev_qrec is not None:
                    b.wait("sync", "sV", prev_qrec[3])
                qh_rec[nxt] = b.dma("sync", r_ld[nxt % 4],
                                    r_scr[nxt, :, lo:lo + QBS2], sem=rsem(nxt))
            b.wait("vector", rsem(hcq), qh_rec[hcq])

            def chain(e, rb=rb, slab=slab, hcq=hcq, xq2b=xq2b):
                e.tensor_mul(hy, rb, slab)
                e.drain()
                e.tensor_single_scalar(hi32, hy, ROFF, ALU.add)
                e.drain()
                e.scalar_tensor_tensor(hf, hy, ROFF, hi32, ALU.add, ALU.subtract)
                e.drain()
                e.tensor_single_scalar(hf, hf, 0.5, ALU.is_ge)
                e.drain()
                e.scalar_tensor_tensor(
                    xq2b[:, hcq * QBS2:(hcq + 1) * QBS2], hi32, -ROFF, hf, ALU.add, ALU.add)
            b.emit("vector", chain)
            qq_rec[hcq] = b.mark("vector", "sV")
        if lohalf == 1:
            prev_qrec = tuple(qq_rec[HC - 4 + k] for k in range(4))
            qdone[qb] = qq_rec[HC - 1]

    qh_rec = {}
    qq_rec = {}
    slab_rdy = {}
    g_acc_prev = [None]
    build_slab(0)
    quant_half(0, 0)
    quant_half(0, 1)
    b.dma("scalar", ov_t, ov_scr[0, :, :], sem="dA")
    b.dma("scalar", x2_t, x_h[0:128, :], sem="dA")
    dld[0] = ("dA", b.cnt["dA"])

    for tb in range(TB2):
        if tb + 1 < TB2 and (tb + 1) % 2 == 0:
            build_slab((tb + 1) // 2)
        xq2b = xq2P_[tb % 2]
        for j in range(NS2):
            ti = tb * NS2 + j
            if tb + 1 < TB2:
                quant_half(tb + 1, j)
            nti = ti + 1
            if nti < T:
                prev_v = asm_rec.get(nti - 2)
                if prev_v is not None:
                    b.wait("scalar", "sV", prev_v)
                sA = "dA" if nti % 2 == 0 else "dAb"
                b.dma("scalar", ov_tP[nti % 2], ov_scr[nti, :, :], sem=sA)
                b.dma("scalar", x2_tP[nti % 2], x_h[nti * 128:(nti + 1) * 128, :], sem=sA)
                dld[nti] = (sA, b.cnt[sA])
            ovb, xb = ov_tP[ti % 2], x2_tP[ti % 2]
            acc_ = acc_tP[ti % 2]
            b.wait("tensor", dld[ti][0], dld[ti][1])
            if vov_prev is not None:
                b.wait("tensor", "sS", vov_prev)
            for dc in range(DC):
                b.emit("tensor", lambda e, dc=dc, ovb=ovb: e.transpose(
                    PS_C[:, dc * 128:(dc + 1) * 128], ovb[:, dc * 128:(dc + 1) * 128], ident))
            t_ov = b.mark("tensor", "sT")
            psf = PS_A
            if j == 0:
                b.wait("tensor", "sV", qdone[tb])
            if ti >= 1:
                b.wait("tensor", "sV", asm_rec[ti - 1])
            for half in range(2):
                for hcq in range(HC):
                    b.emit("tensor", lambda e, half=half, hcq=hcq, j=j, psf=psf,
                           xq2b=xq2b: e.matmul(
                        psf[:, half * 512:(half + 1) * 512],
                        xq2b[:, hcq * QBS2 + j * 128: hcq * QBS2 + j * 128 + 128],
                        w2qT[:, hcq * D + half * 512: hcq * D + half * 512 + 512],
                        start=(hcq == 0), stop=(hcq == HC - 1)))
            t_f = b.mark("tensor", "sT")
            if j == NS2 - 1:
                tf_last[tb] = t_f
            b.wait("scalar", "sT", t_ov)
            b.emit("scalar", lambda e: e.copy(ovfm[:, 0:1024], PS_C[:, 0:1024]))
            v_ov = b.mark("scalar", "sS")
            vov_prev = v_ov
            b.wait("tensor", "sS", v_ov)
            for half in range(2):
                for dc in range(DC):
                    b.emit("tensor", lambda e, half=half, dc=dc: e.matmul(
                        PS_B[:, half * 512:(half + 1) * 512],
                        ovfm[:, dc * 128:(dc + 1) * 128],
                        WoT[:, dc * D + half * 512: dc * D + half * 512 + 512],
                        start=(dc == 0), stop=(dc == DC - 1)))
            t_o = b.mark("tensor", "sT")
            b.wait("scalar", "sT", t_f)
            if g_acc_prev[0] is not None:
                b.wait("scalar", "sG", g_acc_prev[0])
            b.emit("scalar", lambda e, ti=ti, psf=psf: e.mul(
                ffn_t[:, 0:1024], psf[:, 0:1024], d2_all[:, ti:ti + 1]))
            v_ffn = b.mark("scalar", "sS")
            b.wait("gpsimd", dld[ti][0], dld[ti][1])
            if ti >= 2:
                b.wait("gpsimd", "dGa" if ti % 2 == 0 else "dGb", out_dg[ti - 2])
            b.emit("gpsimd", lambda e, xb=xb, acc_=acc_: e.tensor_add(acc_, xb, bo_rep))
            b.emit("gpsimd", lambda e: e.drain())
            b.wait("gpsimd", "sS", v_ffn)
            b.emit("gpsimd", lambda e, acc_=acc_: e.tensor_add(acc_, acc_, ffn_t))
            g_acc = b.mark("gpsimd", "sG")
            g_acc_prev[0] = g_acc
            b.wait("vector", "sG", g_acc)
            b.wait("vector", "sT", t_o)
            for half in range(2):
                b.V(lambda e, half=half, acc_=acc_: e.scalar_tensor_tensor(
                    acc_[:, half * 512:(half + 1) * 512], PS_B[:, half * 512:(half + 1) * 512],
                    1.0, acc_[:, half * 512:(half + 1) * 512], ALU.mult, ALU.add))
            v_out = b.mark("vector", "sV")
            asm_rec[ti] = v_out
            b.wait("gpsimd", "sV", v_out)
            b.dma("gpsimd", out_h[ti * 128:(ti + 1) * 128, :], acc_,
                  sem=("dGa" if ti % 2 == 0 else "dGb"))
            out_dg[ti] = b.cnt["dGa" if ti % 2 == 0 else "dGb"]
    b.wait("vector", "dG", b.cnt["dG"])

    b.wait("gpsimd", "dG", b.cnt["dG"])
    b.wait("gpsimd", "dGa", b.cnt["dGa"])
    b.wait("gpsimd", "dGb", b.cnt["dGb"])
    b.wait("gpsimd", "dS", b.cnt["dS"])
    if b.cnt["dA"]:
        b.wait("gpsimd", "dA", b.cnt["dA"])
    if b.cnt["dAb"]:
        b.wait("gpsimd", "dAb", b.cnt["dAb"])
    for _rs in ("dRa", "dRb", "dRc", "dRd"):
        if b.cnt[_rs]:
            b.wait("gpsimd", _rs, b.cnt[_rs])
    return finish(nc, es, block, b)


def finish(nc, es, block, b, ops_only=False):
    ops = b.ops

    @block.sync
    def _(sync):
        for f in ops["sync"]:
            f(sync)

    @block.tensor
    def _(tensor):
        for f in ops["tensor"]:
            f(tensor)

    @block.vector
    def _(vector):
        for f in ops["vector"]:
            f(vector)

    @block.scalar
    def _(scalar):
        for f in ops["scalar"]:
            f(scalar)

    @block.gpsimd
    def _(gpsimd):
        for f in ops["gpsimd"]:
            f(gpsimd)

    es.close()
    return nc


def make_in_map(x_b, ins):
    f = np.float32
    return {
        "x": np.ascontiguousarray(x_b, dtype=f),
        "Wq": np.asarray(ins["L0_attn_Wq"], f), "Wk": np.asarray(ins["L0_attn_Wk"], f),
        "Wv": np.asarray(ins["L0_attn_Wv"], f), "Wo": np.asarray(ins["L0_attn_Wo"], f),
        "bq": np.asarray(ins["L0_attn_bq"], f).reshape(1, D),
        "bk": np.asarray(ins["L0_attn_bk"], f).reshape(1, D),
        "bv": np.asarray(ins["L0_attn_bv"], f).reshape(1, D),
        "bo": np.asarray(ins["L0_attn_bo"], f).reshape(1, D),
        "gam": np.asarray(ins["L0_attn_ln_gamma"], f).reshape(1, D),
        "bet": np.asarray(ins["L0_attn_ln_beta"], f).reshape(1, D),
        "mu": np.asarray(ins["L0_attn_mu"], f).reshape(1, 1),
        "w1": np.asarray(ins["L0_ffn_w1"], f),
        "w2": np.asarray(ins["L0_ffn_w2"], f),
        "cw": np.asarray(ins["L0_ffn_conv_w"], f).reshape(H, 3),
        "cb": np.asarray(ins["L0_ffn_conv_b"], f).reshape(1, H),
        "ident_in": np.eye(128, dtype=f),
    }


from concourse.bass_utils import run_bass_kernel_spmd

S_FULL = 4096
N_CORES = 8


def kernel(**inputs):
    x = np.asarray(inputs["L0_attn_sequence"], np.float32)
    nc = build_core(S_FULL)
    in_maps = [make_in_map(x[b], inputs) for b in range(N_CORES)]
    res = run_bass_kernel_spmd(nc, in_maps, core_ids=list(range(N_CORES)))
    return np.stack([np.asarray(r["out"], np.float32) for r in res.results], axis=0)
